# version 25
# baseline (speedup 1.0000x reference)
import math
import numpy as np
import ml_dtypes
from contextlib import ExitStack

import concourse.bass as bass
import concourse.mybir as mybir
from concourse.bass_utils import run_bass_kernel_spmd

F32 = mybir.dt.float32
BF16 = mybir.dt.bfloat16
AF = mybir.ActivationFunctionType
AL = mybir.AluOpType
AXX = mybir.AxisListType.X

GR = 512


def tile_cols(a, cw):
    R, W = a.shape
    return np.ascontiguousarray(
        a.reshape(R // 128, 128, W // cw, cw).transpose(2, 1, 0, 3)
    ).reshape(W // cw, 128, (R // 128) * cw)


class Buf:
    __slots__ = ("w", "r")

    def __init__(self):
        self.w = None
        self.r = []


class Region:
    def __init__(self, off, size):
        self.off = off
        self.size = size
        n = (size + GR - 1) // GR
        self.bufs = [Buf() for _ in range(n)]

    def b(self, off, n):
        lo = off // GR
        hi = (off + n + GR - 1) // GR
        return self.bufs[lo:hi]


class MK:
    ENGS = ("T", "V", "S", "Q", "G", "P", "Q2", "G2")
    STEP = {"T": 1, "V": 1, "S": 1, "Q": 16, "G": 16, "P": 1, "Q2": 16, "G2": 16}
    QCHAN = {"Q": ("Q", "Q2"), "G": ("G", "G2")}

    def __init__(self, cfg):
        self.cfg = cfg
        c = cfg
        self.D = c["D"]; self.TOK = c["TOK"]; self.H = c["H"]
        self.DH = c["DH"]; self.INTER = c["INTER"]; self.HG = c["HG"]
        self.DC = self.D // 128
        self.KT = self.TOK // 128
        assert self.KT % 2 == 0
        self.OWN = self.TOK // 2
        self.OC = self.OWN // 128
        self.QH = min(512, self.OWN)
        self.NQH = self.OWN // self.QH
        self.IC = self.INTER // 128
        self.NG = self.H // self.HG
        self.PW = min(512, self.OWN)
        self.NP = self.TOK // self.PW
        self.OPW = min(512, self.OWN)
        self.NOP = self.OWN // self.OPW
        self.dbg = set(c.get("dbg", ()))

        self.nc = bass.Bass("TRN2", target_bir_lowering=False)
        self.ctx = ExitStack()
        self.prog = {e: [] for e in ("T", "V", "S", "Q", "G")}
        self._flip = {"Q": 0, "G": 0}
        self.cnt = {e: 0 for e in self.ENGS}
        self.maxw = {e: {} for e in self.ENGS}
        self._wpref = {}
        self._mk_dram()
        self._mk_sbuf()
        self._mk_sems()
        self._build()
        self._finish()


    def _mk_sems(self):
        self.sem = {}
        for e in self.ENGS:
            self.sem[e] = self.ctx.enter_context(self.nc.semaphore("sem" + e))
        self.sem["C"] = self.ctx.enter_context(self.nc.semaphore("semC"))
        self.k_snd_buf = Buf()
        self.v_snd_buf = Buf()
        self.kv_rcv_buf = Buf()

    def op(self, eng, fn, reads=(), writes=(), inc=True, serialize=False,
           record=None):
        toks = []
        if serialize and self.cnt[eng] > 0:
            toks.append((eng, self.cnt[eng]))
        for b in reads:
            if b.w is not None:
                toks.append(b.w)
        for b in writes:
            if b.w is not None:
                toks.append(b.w)
            toks.extend(b.r)
        need = {}
        for (e2, v) in toks:
            if eng == "T" and e2 == "T":
                continue
            if v > need.get(e2, -1):
                need[e2] = v
        if eng in ("Q", "G", "Q2", "G2") and self.cnt[eng] > need.get(eng, -1):
            need[eng] = self.cnt[eng]
        waits = []
        mw = self.maxw[eng]
        for e2, v in need.items():
            if v > mw.get(e2, 0):
                waits.append((e2, v))
                mw[e2] = v
        tok = None
        if inc:
            self.cnt[eng] += self.STEP[eng]
            tok = (eng, self.cnt[eng])
        sems = self.sem
        step = self.STEP[eng]

        def thunk(e, fn=fn, waits=waits, tok=tok, eng=eng):
            for (e2, v) in waits:
                e.wait_ge(sems[e2], v)
            ins = fn(e)
            if tok is not None:
                ins.then_inc(sems[eng], step)

        stream = {"P": "G", "Q2": "Q", "G2": "G"}.get(eng, eng)
        self.prog[stream].append(thunk)
        if tok is not None:
            for b in reads:
                b.r.append(tok)
            for b in writes:
                b.w = tok
                b.r = []
        elif record is not None:
            record.extend(reads)
            record.extend(writes)
        return tok

    def attach(self, tok, bufs):
        for b in bufs:
            b.r.append(tok)

    def dma(self, eng, out_ap, in_ap, reads=(), writes=(), slow=False):
        if eng in self._flip:
            ch = self.QCHAN[eng][self._flip[eng]]
            self._flip[eng] ^= 1
            eng = ch
        if slow:
            return self.op(eng, lambda e: e.dma_start(
                out=out_ap, in_=in_ap, allow_slow_non_contiguous=True),
                reads=reads, writes=writes)
        return self.op(eng, lambda e: e.dma_start(out=out_ap, in_=in_ap),
                       reads=reads, writes=writes)

    def mm(self, out_ap, lhsT_ap, rhs_ap, reads, writes, start, stop,
           inc=True, record=None):
        return self.op("T", lambda e: e.matmul(out_ap, lhsT_ap, rhs_ap,
                                               start=start, stop=stop),
                       reads=reads, writes=writes, inc=inc, record=record)


    def _mk_dram(self):
        nc = self.nc
        D, TOK, OWN, INTER = self.D, self.TOK, self.OWN, self.INTER
        inner = self.H * 2 * self.DH
        dt = nc.dram_tensor
        self.d_xT = dt("xT", [D, TOK], F32, kind="ExternalInput").ap()
        DC, IC = self.DC, self.IC
        NIC = self.H
        VW = min(512, inner)
        self.VW = VW
        self.d_wq = dt("wq", [NIC, 128, DC * 128], F32, kind="ExternalInput").ap()
        self.d_wk = dt("wk", [NIC, 128, DC * 128], F32, kind="ExternalInput").ap()
        self.d_wv = dt("wv", [inner // VW, 128, DC * VW], F32,
                       kind="ExternalInput").ap()
        self.d_wo = dt("wo", [inner, D], F32, kind="ExternalInput").ap()
        self.d_gwT = dt("gwT", [IC, 128, DC * 128], F32, kind="ExternalInput").ap()
        self.d_uwT = dt("uwT", [IC, 128, DC * 128], F32, kind="ExternalInput").ap()
        self.d_dwT = dt("dwT", [DC, 128, IC * 128], F32, kind="ExternalInput").ap()
        self.d_gs = dt("gs", [INTER], F32, kind="ExternalInput").ap()
        self.d_us = dt("us", [INTER], F32, kind="ExternalInput").ap()
        self.d_ds = dt("ds", [D], F32, kind="ExternalInput").ap()
        self.d_gnw = dt("gnw", [inner], F32, kind="ExternalInput").ap()
        self.d_gnb = dt("gnb", [inner], F32, kind="ExternalInput").ap()
        self.d_lq1 = dt("lq1", [self.DH], F32, kind="ExternalInput").ap()
        self.d_lk1 = dt("lk1", [self.DH], F32, kind="ExternalInput").ap()
        self.d_lq2 = dt("lq2", [self.DH], F32, kind="ExternalInput").ap()
        self.d_lk2 = dt("lk2", [self.DH], F32, kind="ExternalInput").ap()
        self.d_out = dt("outT", [D, OWN], F32, kind="ExternalOutput").ap()
        W0 = self.H * 2 * self.DH
        self.W0 = W0
        self.d_k_snd = dt("k_snd", [W0, OWN], BF16)
        self.d_k_rcv = dt("k_rcv", [2 * W0, OWN], BF16)
        self.d_v_snd = dt("v_snd", [OWN, W0], BF16)
        self.d_v_rcv = dt("v_rcv", [2 * OWN, W0], BF16)
        self.d_dbg = {}
        for name, shp in (("xn", [D, OWN]), ("aT", [inner, OWN]),
                          ("sT", [D, OWN]), ("hdn", [INTER, OWN]),
                          ("sgT", [D, OWN])):
            if name in self.dbg:
                self.d_dbg[name] = dt("dbg_" + name, shp, BF16,
                                      kind="ExternalOutput").ap()

    def _mk_sbuf(self):
        nc = self.nc
        D, TOK, OWN = self.D, self.TOK, self.OWN
        DC, KT, QH, IC, HG = self.DC, self.KT, self.QH, self.IC, self.HG
        sizes = {}
        sizes["A"] = DC * OWN
        sizes["B"] = max(16 * OWN, 8 * D)
        sizes["C"] = max(KT * QH, DC * HG * 128)
        sizes["E"] = max(HG * TOK, 4 * D)
        sizes["F"] = KT * HG * 128
        sizes["G"] = HG * OWN + 2 * QH
        sizes["H"] = max(2 * DC * 128, 8 * OWN // 2)
        sizes["AT"] = max(self.H * OWN, 2 * IC * 128 + OWN, 2 * OWN)
        sizes["ST"] = max(DC * OWN, 28 * self.PW)
        sizes["SC"] = 8 * QH
        total = 0
        offs = {}
        for k in ("A", "B", "C", "E", "F", "G", "H", "AT", "ST", "SC"):
            offs[k] = total
            total += sizes[k]
        self.pool_el = total
        self.pool = self.ctx.enter_context(
            nc.sbuf_tensor("pool", [128, total], BF16))
        self.R = {k: Region(offs[k], sizes[k]) for k in offs}

        self.cb = self.ctx.enter_context(
            nc.sbuf_tensor("cb", [128, 3 * 128], BF16))
        self.cb_bufs = [Buf() for _ in range(3)]
        self.cf = self.ctx.enter_context(
            nc.sbuf_tensor("cf", [128, 8 + IC * 2 + DC + self.H * 2], F32))
        self.cf_bufs = Buf()
        self.row = self.ctx.enter_context(
            nc.sbuf_tensor("rowscr", [1, 8 * max(64, self.DH)], F32))
        self.row_buf = Buf()
        self.ones_row = self.ctx.enter_context(
            nc.sbuf_tensor("ones_row", [1, 128], F32))
        self.ones_row_buf = Buf()
        self.lam128 = self.ctx.enter_context(
            nc.sbuf_tensor("lam128", [128, 1], F32))
        self.lam128_buf = Buf()
        self.cm8 = self.ctx.enter_context(
            nc.sbuf_tensor("cm8", [128, 1], F32))
        self.cm8_buf = Buf()
        self.ceps = self.ctx.enter_context(
            nc.sbuf_tensor("ceps", [128, 1], F32))
        self.ceps_buf = Buf()
        self.lnlam = self.ctx.enter_context(
            nc.sbuf_tensor("lnlam", [128, 1], F32))
        self.lnlam_buf = Buf()
        self.psum = self.ctx.enter_context(
            nc.psum_tensor("ps", [128, 4096], F32))
        self.pb = [Buf() for _ in range(8)]

    def ap(self, reg, off, n):
        r = self.R[reg]
        assert off + n <= r.size, (reg, off, n, r.size)
        return self.pool[:, r.off + off: r.off + off + n]

    def ap32(self, reg, off, n32):
        r = self.R[reg]
        assert off + 2 * n32 <= r.size, (reg, off, n32, r.size)
        return self.pool[:, r.off + off: r.off + off + 2 * n32].bitcast(F32)

    def bufs(self, reg, off, n):
        return self.R[reg].b(off, n)

    def pbank(self, b, width=None, off=0):
        w = 512 if width is None else width
        return self.psum[:, b * 512 + off: b * 512 + off + w]


    def _build(self):
        stop = self.cfg.get("stop_after", "")
        phases = [("consts", self._consts), ("ph0", self._ph0_layernorm),
                  ("ph1", self._ph1_lam), ("ph1b", self._ph1b_kv),
                  ("ph3", self._ph3_gateup), ("ph4", self._ph4_down),
                  ("ph2", self._ph2_attention), ("ph2b", self._ph2b_wo)]
        for name, fn in phases:
            fn()
            if name == stop:
                break
        sems = self.sem
        for ch in ("Q", "Q2"):
            v = self.cnt[ch]
            if v:
                self.prog["Q"].append(lambda e, c=ch, vv=v: e.wait_ge(sems[c], vv))
        for ch in ("G", "G2"):
            v = self.cnt[ch]
            if v:
                self.prog["G"].append(lambda e, c=ch, vv=v: e.wait_ge(sems[c], vv))

    def _consts(self):
        nc = self.nc
        H, DC, IC = self.H, self.DC, self.IC
        from concourse.hw_specs import get_activation_tables
        tabs = list(get_activation_tables(self.nc.m.arch).keys())
        self.tab_idx = {name: i for i, name in enumerate(tabs)}

        def load_tab(e, idx):
            return e.add_instruction(mybir.InstLoadActFuncSet(
                name=self.nc.get_next_instruction_name(),
                act_func_set_id=idx, ins=[], outs=[]))
        idx_nle = self.tab_idx["natural_log_exp_and_others"]
        self.op("S", lambda e: load_tab(e, idx_nle), serialize=True)

        cb, cf = self.cb, self.cf
        b = self.cb_bufs
        self.op("V", lambda e: e.memset(cb[:, 0:128], 1.0), writes=[b[0]])
        self.op("V", lambda e: e.memset(cb[:, 128:256], 1.0 / (2 * self.DH)),
                writes=[b[1]])
        self.op("V", lambda e: e.memset(cb[:, 256:384], 1.0 / self.D),
                writes=[b[2]])
        self.op("V", lambda e: e.memset(self.ones_row[:], 1.0),
                writes=[self.ones_row_buf])
        self.op("V", lambda e: e.memset(self.cm8[:], -8.0),
                writes=[self.cm8_buf])
        self.op("V", lambda e: e.memset(self.ceps[:], 1e-5),
                writes=[self.ceps_buf])
        self.o_gs = 0
        self.o_us = IC
        self.o_ds = 2 * IC
        self.o_gw = 2 * IC + DC
        self.o_gb = 2 * IC + DC + H
        cfb = self.cf_bufs
        self.dma("Q", cf[:, self.o_gs:self.o_gs + IC],
                 self.d_gs.rearrange("(c p) -> p c", p=128), writes=[cfb], slow=True)
        self.dma("Q", cf[:, self.o_us:self.o_us + IC],
                 self.d_us.rearrange("(c p) -> p c", p=128), writes=[cfb], slow=True)
        self.dma("Q", cf[:, self.o_ds:self.o_ds + DC],
                 self.d_ds.rearrange("(c p) -> p c", p=128), writes=[cfb], slow=True)
        self.dma("Q", cf[:, self.o_gw:self.o_gw + H],
                 self.d_gnw.rearrange("(c p) -> p c", p=128), writes=[cfb], slow=True)
        self.dma("Q", cf[:, self.o_gb:self.o_gb + H],
                 self.d_gnb.rearrange("(c p) -> p c", p=128), writes=[cfb], slow=True)
        self.op("V", lambda e: e.tensor_scalar_mul(
            cf[:, self.o_gw:self.o_gw + 2 * H],
            cf[:, self.o_gw:self.o_gw + 2 * H], 0.2),
            reads=[cfb], writes=[cfb])

    def xn_ap(self, c, t0, n):
        OWN = self.OWN
        if t0 >= OWN:
            return self.ap("B", c * (self.TOK - OWN) + (t0 - OWN), n)
        assert t0 + n <= OWN
        return self.ap("A", c * OWN + t0, n)

    def xn_bufs(self, c, t0, n):
        OWN = self.OWN
        if t0 >= OWN:
            return self.bufs("B", c * (self.TOK - OWN) + (t0 - OWN), n)
        return self.bufs("A", c * OWN + t0, n)

    def _ph0_layernorm(self):
        nc = self.nc
        D, TOK, OWN, DC, PW, NP = self.D, self.TOK, self.OWN, self.DC, self.PW, self.NP
        NP = self.OWN // PW
        xT3 = self.d_xT.rearrange("(c p) t -> p c t", p=128)
        for p in range(NP):
            t0 = p * PW
            base = self.R["A"].off
            out3 = self.pool[:, base: base + DC * OWN].rearrange(
                "p (c f) -> p c f", c=DC)[:, :, t0:t0 + PW]
            wb = []
            for c in range(DC):
                wb += self.xn_bufs(c, t0, PW)
            self.dma("G", out3, xT3[:, :, t0:t0 + PW], writes=wb)
        onesLN = self.cb[:, 256:384]
        ST = "ST"
        def scr(p):
            base = (p % 2) * 12 * PW
            muS = self.ap32(ST, base, PW)
            var = self.ap32(ST, base + 2 * PW, PW)
            rstd = self.ap32(ST, base + 4 * PW, PW)
            mumix = self.ap32(ST, base + 6 * PW, PW)
            return (muS, var, rstd, mumix,
                    self.bufs(ST, base, 2 * PW),
                    self.bufs(ST, base + 2 * PW, 2 * PW),
                    self.bufs(ST, base + 4 * PW, 2 * PW),
                    self.bufs(ST, base + 6 * PW, 2 * PW))
        for p in range(NP):
            muS, var, rstd, mumix, bmu, bvar, brstd, bmix = scr(p)
            t0 = p * PW
            bmu_k = (p % 2) * 2
            bs2_k = bmu_k + 1
            rec0, rec1 = [], []
            for c in range(DC):
                t = self.mm(self.pbank(bmu_k, PW), onesLN, self.xn_ap(c, t0, PW),
                            reads=[self.cb_bufs[2]] + self.xn_bufs(c, t0, PW),
                            writes=[self.pb[bmu_k]], start=(c == 0),
                            stop=(c == DC - 1), inc=(c == DC - 1), record=rec0)
                if t is not None:
                    self.attach(t, rec0)
            for c in range(DC):
                sq = self.ap(ST, 24 * PW + ((p * DC + c) % 2) * PW, PW)
                sqb = self.bufs(ST, 24 * PW + ((p * DC + c) % 2) * PW, PW)
                self.op("S", lambda e, i=self.xn_ap(c, t0, PW), o=sq:
                        e.activation(o, i, AF.Square),
                        reads=self.xn_bufs(c, t0, PW), writes=sqb)
                self.mm(self.pbank(bs2_k, PW), onesLN, sq,
                        reads=[self.cb_bufs[2]] + sqb,
                        writes=[self.pb[bs2_k]], start=(c == 0),
                        stop=(c == DC - 1))
            self.op("V", lambda e, o=muS, b=bmu_k: e.tensor_copy(o, self.pbank(b, PW)),
                    reads=[self.pb[bmu_k]], writes=bmu)
            self.op("V", lambda e, o=var, a=muS: e.tensor_tensor(o, a, a, AL.mult),
                    reads=bmu, writes=bvar)
            self.op("V", lambda e, o=var, a=var, b=bs2_k: e.tensor_tensor(
                o, self.pbank(b, PW), a, AL.subtract),
                reads=[self.pb[bs2_k]] + bvar, writes=bvar)
            self.op("S", lambda e, o=rstd, a=var: e.activation(
                o, a, AF.Ln, bias=self.ceps[:, 0:1]),
                reads=bvar + [self.ceps_buf], writes=brstd)
            self.op("S", lambda e, o=rstd, a=rstd: e.activation(
                o, a, AF.Exp, scale=-0.5),
                reads=brstd, writes=brstd)
            self.op("V", lambda e, o=mumix, a=muS, b2=rstd: e.scalar_tensor_tensor(
                o, a, -1.0, b2, AL.mult, AL.mult),
                reads=bmu + brstd, writes=bmix)
            for c in range(DC):
                eng = "V"
                t = self.ap(ST, 26 * PW + (c % 2) * PW, PW)
                tb = self.bufs(ST, 26 * PW + (c % 2) * PW, PW)
                self.op(eng, lambda e, o=t, a=self.xn_ap(c, t0, PW), b2=rstd:
                        e.tensor_tensor(o, a, b2, AL.mult),
                        reads=self.xn_bufs(c, t0, PW) + brstd, writes=tb)
                self.op(eng, lambda e, o=self.xn_ap(c, t0, PW), a=t, b2=mumix:
                        e.tensor_tensor(o, a, b2, AL.add),
                        reads=tb + bmix, writes=self.xn_bufs(c, t0, PW))
        if "xn" in self.d_dbg:
            for c in range(DC):
                self.dma("Q", self.d_dbg["xn"][c * 128:(c + 1) * 128, :],
                         self.xn_ap(c, 0, OWN), reads=self.xn_bufs(c, 0, OWN))

    def _ph1_lam(self):
        row, rb = self.row, self.row_buf
        DH = self.DH
        for i, d in enumerate((self.d_lq1, self.d_lk1, self.d_lq2, self.d_lk2)):
            self.dma("Q", row[0:1, i * DH:(i + 1) * DH],
                     d.rearrange("(o f) -> o f", o=1), writes=[rb])
        p1 = row[0:1, 4 * DH:5 * DH]
        self.op("V", lambda e: e.tensor_tensor(
            p1, row[0:1, 0:DH], row[0:1, DH:2 * DH], AL.mult),
            reads=[rb], writes=[rb])
        s1 = row[0:1, 5 * DH:5 * DH + 1]
        self.op("V", lambda e: e.reduce_sum(s1, p1, axis=AXX),
                reads=[rb], writes=[rb])
        self.op("V", lambda e: e.tensor_tensor(
            p1, row[0:1, 2 * DH:3 * DH], row[0:1, 3 * DH:4 * DH], AL.mult),
            reads=[rb], writes=[rb])
        s2 = row[0:1, 5 * DH + 1:5 * DH + 2]
        self.op("V", lambda e: e.reduce_sum(s2, p1, axis=AXX),
                reads=[rb], writes=[rb])
        e1 = row[0:1, 5 * DH + 2:5 * DH + 3]
        e2 = row[0:1, 5 * DH + 3:5 * DH + 4]
        self.op("S", lambda e: e.activation(e1, s1, AF.Exp), reads=[rb], writes=[rb])
        self.op("S", lambda e: e.activation(e2, s2, AF.Exp), reads=[rb], writes=[rb])
        lam = row[0:1, 5 * DH + 4:5 * DH + 5]
        self.op("V", lambda e: e.tensor_tensor(lam, e1, e2, AL.subtract),
                reads=[rb], writes=[rb])
        self.op("V", lambda e: e.tensor_scalar_add(lam, lam, 0.8),
                reads=[rb], writes=[rb])
        self.mm(self.pbank(2, 1), self.ones_row[:], lam,
                reads=[self.ones_row_buf, rb], writes=[self.pb[2]],
                start=True, stop=True)
        self.op("V", lambda e: e.tensor_copy(self.lam128[:], self.pbank(2, 1)),
                reads=[self.pb[2]], writes=[self.lam128_buf])




    def _wslot(self, idx):
        base = (idx % 2) * self.DC * 128
        return base

    def _ph1b_kv(self):
        DC, OWN, OC = self.DC, self.OWN, self.OC
        NIC = self.H
        ST = "ST"
        kvK = self.d_k_snd.ap()
        units = []

        def k_unit(icc):
            def emit():
                at_off = 2 * OWN + (icc % 2) * DC * 128
                wap = self.ap("AT", at_off, DC * 128)
                wb = self.bufs("AT", at_off, DC * 128)
                self.dma("G", wap, self.d_wk[icc], writes=wb)
                sslot = (icc % 2) * OWN
                for pc in range(self.NOP):
                    t0 = pc * self.OPW
                    bank = 4 + (icc % 2) * 2 + pc
                    rec = []
                    for d in range(DC):
                        t = self.mm(self.pbank(bank, self.OPW),
                                    wap[:, d * 128:(d + 1) * 128],
                                    self.xn_ap(d, t0, self.OPW),
                                    reads=wb + self.xn_bufs(d, t0, self.OPW),
                                    writes=[self.pb[bank]],
                                    start=(d == 0), stop=(d == DC - 1),
                                    inc=(d == DC - 1), record=rec)
                    self.attach(t, rec)
                    sg = self.ap(ST, sslot + t0, self.OPW)
                    sgb = self.bufs(ST, sslot + t0, self.OPW)
                    self.op("V", lambda e, o=sg, b=bank: e.tensor_copy(
                        o, self.pbank(b, self.OPW)),
                        reads=[self.pb[bank]], writes=sgb)
                self.dma("Q", kvK[icc * 128:(icc + 1) * 128, :],
                         self.ap(ST, sslot, OWN),
                         reads=self.bufs(ST, sslot, OWN),
                         writes=[self.k_snd_buf])
            return emit

        VW = self.VW
        NVG = self.W0 // VW

        def v_unit(vg, t, uidx):
            def emit():
                wvb = self.bufs("C", 0, DC * VW)
                if t == 0:
                    self.dma("G", self.ap("C", 0, DC * VW),
                             self.d_wv[vg], writes=wvb)
                bank = 4 + uidx % 4
                rec = []
                for d in range(DC):
                    tk = self.mm(self.pbank(bank, VW),
                                 self.xn_ap(d, t * 128, 128),
                                 self.ap("C", d * VW, VW),
                                 reads=self.xn_bufs(d, t * 128, 128) + wvb,
                                 writes=[self.pb[bank]],
                                 start=(d == 0), stop=(d == DC - 1),
                                 inc=(d == DC - 1), record=rec)
                self.attach(tk, rec)
                sg = self.ap(ST, 2 * OWN + (uidx % 2) * VW, VW)
                sgb = self.bufs(ST, 2 * OWN + (uidx % 2) * VW, VW)
                self.op("V", lambda e, o=sg, b=bank, w=VW: e.tensor_copy(
                    o, self.pbank(b, w)),
                    reads=[self.pb[bank]], writes=sgb)
                self.dma("Q", self.d_v_snd.ap()[
                    t * 128:(t + 1) * 128, vg * VW:(vg + 1) * VW],
                    sg, reads=sgb, writes=[self.v_snd_buf])
            return emit

        for icc in range(NIC):
            units.append(k_unit(icc))
        ui = 0
        for vg in range(NVG):
            for t in range(OC):
                units.append(v_unit(vg, t, ui))
                ui += 1
        rg = self.cfg.get("replica_groups")
        if rg is None:
            rg = [[0, 1], [2, 3], [4, 5], [6, 7]]
        semC = self.sem["C"]
        ks, kr = self.d_k_snd.ap().opt(), self.d_k_rcv.ap().opt()
        vs, vr = self.d_v_snd.ap().opt(), self.d_v_rcv.ap().opt()

        def k_trig():
            self.op("G", lambda e: e.collective_compute(
                "AllGather", AL.bypass, replica_groups=rg,
                ins=[ks], outs=[kr]).then_inc(semC, 1),
                reads=[self.k_snd_buf], inc=False)

        def v_trig():
            self.op("G", lambda e: e.collective_compute(
                "AllGather", AL.bypass, replica_groups=rg,
                ins=[vs], outs=[vr]).then_inc(semC, 1),
                reads=[self.v_snd_buf], inc=False)
            self.kv_rcv_buf.w = ("C", 2)
        self._kv_units = units
        self._kv_ktrig = (NIC, k_trig)
        self._kv_vtrig = (len(units), v_trig)
        self._kv_consumed = 0

    def _kv_step(self, n=2):
        for _ in range(n):
            if self._kv_consumed < len(self._kv_units):
                self._kv_units[self._kv_consumed]()
                self._kv_consumed += 1
                if self._kv_consumed == self._kv_ktrig[0]:
                    self._kv_ktrig[1]()
                if self._kv_consumed == self._kv_vtrig[0]:
                    self._kv_vtrig[1]()

    def _load_wcol(self, dram, chunk, slot_idx):
        si = self._wslot(slot_idx)
        wap = self.ap("H", si, self.DC * 128)
        wb = self.bufs("H", si, self.DC * 128)
        self.dma("G", wap, dram[chunk], writes=wb)
        return (wap, wb)

    def _ph2_attention(self):
        DC, HG, NG, OWN, TOK, QH, KT = (self.DC, self.HG, self.NG, self.OWN,
                                        self.TOK, self.QH, self.KT)
        DH = self.DH
        slot_i = 0
        self._pend_tail = None
        idx_nle = self.tab_idx["natural_log_exp_and_others"]

        def load_tab(e, idx=idx_nle):
            return e.add_instruction(mybir.InstLoadActFuncSet(
                name=self.nc.get_next_instruction_name(),
                act_func_set_id=idx, ins=[], outs=[]))
        self.op("S", load_tab, serialize=True)
        for z in range(2):
            apz = self.ap("G", HG * OWN + z * QH, QH)
            self.op("V", lambda e, a=apz: e.memset(a, 0.0),
                    writes=self.bufs("G", HG * OWN + z * QH, QH))
        for i in range(self.H):
            reg, off = self._wo_slot(i)
            if reg == "B":
                self.dma("G", self.ap(reg, off, self.D),
                         self.d_wo[i * 128:(i + 1) * 128, :],
                         writes=self.bufs(reg, off, self.D))
        NGrun = self.cfg.get("ph2_groups", NG)
        noheads = self.cfg.get("ph2_noheads", False)
        for g in range(NGrun):
            col0 = g * HG * 128
            for cc in range(HG):
                wap, wb = self._load_wcol(self.d_wq, g * HG + cc, slot_i)
                slot_i += 1
                for pc in range(self.NOP):
                    t0 = pc * self.OPW
                    bank = pc % 2
                    rec = []
                    for d in range(DC):
                        t = self.mm(self.pbank(bank, self.OPW),
                                    wap[:, d * 128:(d + 1) * 128],
                                    self.xn_ap(d, t0, self.OPW),
                                    reads=wb + self.xn_bufs(d, t0, self.OPW),
                                    writes=[self.pb[bank]],
                                    start=(d == 0), stop=(d == DC - 1),
                                    inc=(d == DC - 1), record=rec)
                    self.attach(t, rec)
                    q1 = self.ap("G", cc * OWN + t0, self.OPW)
                    qb1 = self.bufs("G", cc * OWN + t0, self.OPW)
                    self.op("V", lambda e, o=q1, b=bank: e.tensor_copy(
                        o, self.pbank(b, self.OPW)),
                        reads=[self.pb[bank]], writes=qb1)
            gw = HG * 128
            W0 = self.W0
            kvb = [self.kv_rcv_buf]
            for half in range(2):
                kview = self.d_k_rcv.ap()[half * W0:(half + 1) * W0, :]
                eb = []
                for cc in range(HG):
                    eb += self.bufs("E", cc * TOK + half * OWN, OWN)
                eout = self.pool[:, self.R["E"].off:
                                 self.R["E"].off + HG * TOK].rearrange(
                    "p (c t) -> p c t", c=HG)[:, :, half * OWN:(half + 1) * OWN]
                self.dma("Q", eout,
                         kview[g * gw:(g + 1) * gw, :].rearrange(
                             "(c p) t -> p c t", p=128),
                         reads=kvb, writes=eb)
                vview = self.d_v_rcv.ap()[
                    half * OWN:(half + 1) * OWN, g * gw:(g + 1) * gw]
                fb = []
                for t in range(KT // 2):
                    fb += self.bufs("F", (half * KT // 2 + t) * gw, gw)
                fout = self.pool[:, self.R["F"].off:
                                 self.R["F"].off + KT * gw].rearrange(
                    "p (t f) -> p t f", t=KT)[:, half * KT // 2:(half + 1) * KT // 2, :]
                self.dma("Q", fout,
                         vview.rearrange("(t p) f -> p t f", p=128),
                         reads=kvb, writes=fb)
            for l in range(HG):
                if noheads:
                    break
                head = g * HG + l
                for qh in range(self.NQH):
                    nj = None
                    if qh + 1 < self.NQH:
                        nj = (l, (qh + 1) * self.QH)
                    elif l + 1 < HG:
                        nj = (l + 1, 0)
                    self._pend_tail = self._attend(head, l, qh,
                                                   flush=self._pend_tail,
                                                   next_job=nj)
        if self._pend_tail is not None:
            self._pend_tail()
            self._pend_tail = None
        if "aT" in self.d_dbg:
            for hh in range(self.H):
                self.dma("Q", self.d_dbg["aT"][hh * 128:(hh + 1) * 128, :],
                         self.ap("AT", hh * OWN, OWN),
                         reads=self.bufs("AT", hh * OWN, OWN))

    def _emit_qz(self, l, q0):
        QH, DH, OWN = self.QH, self.DH, self.OWN
        aps, bs = [], []
        for comp in range(2):
            apz = self.ap("G", self.HG * OWN + comp * QH, QH)
            bz = self.bufs("G", self.HG * OWN + comp * QH, QH)
            rows = slice(0, DH) if comp == 0 else slice(DH, 128)
            qp = self.ap("G", l * OWN + q0, QH)
            qpb = self.bufs("G", l * OWN + q0, QH)
            self.op("V", lambda e, o=apz, a=qp, r=rows: e.tensor_copy(
                o[r, :], a[r, :]),
                reads=qpb, writes=bz)
            aps.append(apz)
            bs.append(bz)
        self._qz_state = (l, q0, aps, bs)
        return aps, bs

    def _attend(self, head, l, qh, flush=None, next_job=None):
        QH, KT, OWN, TOK = self.QH, self.KT, self.OWN, self.TOK
        DH = self.DH
        q0 = qh * QH
        ones128 = self.cb[:, 0:128]
        onesGN = self.cb[:, 128:256]
        SA, SB, D_, O1, O2 = 0, 2, 4, 5, 6
        kap_all = lambda kt: self.ap("E", l * TOK + kt * 128, 128)
        kb_all = lambda kt: self.bufs("E", l * TOK + kt * 128, 128)
        r_scr = [self.ap32("SC", 0, QH), self.ap32("SC", 2 * QH, QH)]
        r_bufs = [self.bufs("SC", 0, 2 * QH), self.bufs("SC", 2 * QH, 2 * QH)]

        crec = [[], []]
        prec = [[], []]

        st = getattr(self, "_qz_state", None)
        if st is not None and st[0] == l and st[1] == q0:
            qz_ap, qz_b = st[2], st[3]
        else:
            qz_ap, qz_b = self._emit_qz(l, q0)

        def e_reg(comp):
            return "C" if comp == 0 else "B"

        def cspv(j, comp):
            dbank = D_ if comp == 0 else 7
            for z in range(2):
                kt = 2 * j + z
                last = (kt == KT - 1)
                et = self.ap(e_reg(comp), kt * QH, QH)
                etb = self.bufs(e_reg(comp), kt * QH, QH)
                t = self.mm(self.pbank(dbank, QH), ones128, et,
                            reads=[self.cb_bufs[0]] + etb,
                            writes=[self.pb[dbank]],
                            start=(kt == 0), stop=last,
                            inc=last, record=crec[comp])
                if t is not None:
                    self.attach(t, crec[comp])

        npair = KT // 2

        def unit_flush(u):
            comp, j = u
            cspv(j, comp)
            if j == npair - 1:
                db = D_ if comp == 0 else 7
                self.op("S", lambda e, o=r_scr[comp], b=db: e.activation(
                    o, self.pbank(b, QH), AF.Ln),
                    reads=[self.pb[db]], writes=r_bufs[comp])
                self.op("S", lambda e, o=r_scr[comp]: e.activation(
                    o, o, AF.Exp, scale=-1.0),
                    reads=r_bufs[comp], writes=r_bufs[comp])

        pend = []
        unit_no = 0
        for comp in range(2):
            qz = qz_ap[comp]
            qzb = qz_b[comp]
            for j in range(npair):
                pair = SA if (j % 2 == 0) else SB
                for z in range(2):
                    kt = 2 * j + z
                    off = pair * 512 + z * QH
                    sap = self.psum[:, off: off + QH]
                    self.mm(sap, kap_all(kt), qz,
                            reads=kb_all(kt) + qzb,
                            writes=[self.pb[off // 512]],
                            start=True, stop=True)
                eap = self.ap(e_reg(comp), 2 * j * QH, 2 * QH)
                eb = self.bufs(e_reg(comp), 2 * j * QH, 2 * QH)
                self.op("S", lambda e, o=eap, p=pair: e.activation(
                    o, self.psum[:, p * 512: p * 512 + 2 * QH],
                    AF.Exp, scale=1.0 / math.sqrt(DH), bias=self.cm8[:, 0:1]),
                    reads=[self.pb[pair], self.pb[pair + 1],
                           self.pb[(pair * 512 + QH) // 512], self.cm8_buf],
                    writes=eb)
                lag = min(2, npair - 1)
                if unit_no == min(5, npair - 1) and flush is not None:
                    flush()
                unit_no += 1
                pend.append((comp, j))
                while len(pend) > lag:
                    unit_flush(pend.pop(0))
        if next_job is not None:
            self._emit_qz(*next_job)
        for u in pend:
            unit_flush(u)
        cscr = self.ap("SC", 2 * QH, QH)
        cb16 = self.ap("SC", 4 * QH, QH)
        cb16b = self.bufs("SC", 4 * QH, QH)
        self.op("V", lambda e, o=cscr, r=r_scr[1]: e.tensor_tensor(
            o, self.pbank(D_, QH), r, AL.mult),
            reads=[self.pb[D_]] + r_bufs[1], writes=r_bufs[1])
        self.op("V", lambda e, o=cb16, a=cscr: e.tensor_scalar(
            o, a, self.lam128[:, 0:1], None, AL.mult),
            reads=r_bufs[1] + [self.lam128_buf], writes=cb16b)
        for j in range(npair):
            for z in range(2):
                kt = 2 * j + z
                eng = "V" if kt % 2 == 0 else "P"
                e1t = self.ap("C", kt * QH, QH)
                e1b = self.bufs("C", kt * QH, QH)
                e2t = self.ap("B", kt * QH, QH)
                e2b = self.bufs("B", kt * QH, QH)
                tmp = self.ap(e_reg(1), kt * QH, QH)
                self.op(eng, lambda e, o=e2t, a=e2t, c=cb16: e.tensor_tensor(
                    o, a, c, AL.mult),
                    reads=e2b + cb16b, writes=e2b)
                self.op(eng, lambda e, o=e1t, a=e1t, b=e2t: e.tensor_tensor(
                    o, a, b, AL.subtract),
                    reads=e1b + e2b, writes=e1b)
            for z in range(2):
                kt = 2 * j + z
                last = (kt == KT - 1)
                et = self.ap("C", kt * QH, QH)
                etb = self.bufs("C", kt * QH, QH)
                vt = self.ap("F", kt * self.HG * 128 + l * 128, 128)
                vtb = self.bufs("F", kt * self.HG * 128 + l * 128, 128)
                t = self.mm(self.pbank(O1, QH), vt, et,
                            reads=vtb + etb,
                            writes=[self.pb[O1]],
                            start=(kt == 0), stop=last,
                            inc=last, record=prec[0])
                if t is not None:
                    self.attach(t, prec[0])
        gnA = self.ap32("SC", 4 * QH, QH)
        gnB = self.ap32("SC", 6 * QH, QH)
        m2 = self.ap32("SC", 0, QH)
        bA = self.bufs("SC", 4 * QH, 2 * QH)
        bB = self.bufs("SC", 6 * QH, 2 * QH)
        bM = self.bufs("SC", 0, 2 * QH)
        oB = self.ap("AT", head * OWN + q0, QH)
        oBb = self.bufs("AT", head * OWN + q0, QH)
        self.op("V", lambda e, o=oB, r=r_scr[0]: e.tensor_tensor(
            o, self.pbank(5, QH), r, AL.mult),
            reads=[self.pb[5]] + r_bufs[0], writes=oBb)
        sq = self.ap("SC", 0, QH)
        sqb = self.bufs("SC", 0, QH)
        self.op("V", lambda e, o=sq, a=oB: e.tensor_tensor(o, a, a, AL.mult),
                reads=oBb, writes=sqb)

        def tail():
            self.mm(self.pbank(7, QH), onesGN, sq,
                    reads=[self.cb_bufs[1]] + sqb, writes=[self.pb[7]],
                    start=True, stop=True)
            self.op("V", lambda e, o=gnB: e.tensor_copy(o, self.pbank(7, QH)),
                    reads=[self.pb[7]], writes=bB)
            self.mm(self.pbank(7, QH), onesGN, oB,
                    reads=[self.cb_bufs[1]] + oBb, writes=[self.pb[7]],
                    start=True, stop=True)
            self.op("V", lambda e, o=gnA: e.tensor_copy(o, self.pbank(7, QH)),
                    reads=[self.pb[7]], writes=bA)
            self.op("V", lambda e, o=m2, a=gnA: e.tensor_tensor(o, a, a, AL.mult),
                    reads=bA, writes=bM)
            self.op("V", lambda e, o=gnB, a=gnB, b=m2: e.tensor_tensor(
                o, a, b, AL.subtract),
                reads=bB + bM, writes=bB)
            self.op("S", lambda e, o=m2, a=gnB: e.activation(
                o, a, AF.Ln, bias=self.ceps[:, 0:1]),
                reads=bB + [self.ceps_buf], writes=bM)
            self.op("S", lambda e, o=gnB, a=m2: e.activation(
                o, a, AF.Exp, scale=-0.5),
                reads=bM, writes=bB)
            self.op("V", lambda e, o=gnA, a=oB, b=gnA: e.tensor_tensor(
                o, a, b, AL.subtract),
                reads=oBb + bA, writes=bA)
            self.op("V", lambda e, o=gnA, a=gnA, b=gnB: e.tensor_tensor(
                o, a, b, AL.mult),
                reads=bA + bB, writes=bA)
            gw = self.cf[:, self.o_gw + head:self.o_gw + head + 1]
            gb = self.cf[:, self.o_gb + head:self.o_gb + head + 1]
            self.op("V", lambda e, o=oB, a=gnA: e.tensor_scalar(
                o, a, gw, gb, AL.mult, AL.add),
                reads=bA + [self.cf_bufs], writes=oBb)
        return tail

    def _wo_slot(self, i):
        D = self.D
        half_b = self.R["B"].size // 2
        per_b = half_b // D
        if i < per_b:
            return ("B", half_b + i * D)
        i -= per_b
        for reg in ("C", "E", "F"):
            cap = self.R[reg].size // D
            if i < cap:
                return (reg, i * D)
            i -= cap
        raise AssertionError("wo overflow")

    def _ph2b_wo(self):
        DC, OWN, D = self.DC, self.OWN, self.D
        NIC = self.H
        for i in range(NIC):
            reg, off = self._wo_slot(i)
            if reg != "B":
                self.dma("G", self.ap(reg, off, D),
                         self.d_wo[i * 128:(i + 1) * 128, :],
                         writes=self.bufs(reg, off, D))
        for dc in range(DC):
            for pc in range(self.NOP):
                t0 = pc * self.OPW
                bank = (dc % 2) * 2 + pc % 2
                rec = []
                for i in range(NIC):
                    reg, off = self._wo_slot(i)
                    t = self.mm(self.pbank(bank, self.OPW),
                                self.ap(reg, off + dc * 128, 128),
                                self.ap("AT", i * OWN + t0, self.OPW),
                                reads=self.bufs(reg, off + dc * 128, 128) +
                                      self.bufs("AT", i * OWN + t0, self.OPW),
                                writes=[self.pb[bank]],
                                start=(i == 0), stop=(i == NIC - 1),
                                inc=(i == NIC - 1), record=rec)
                self.attach(t, rec)
                fap = self.ap("ST", dc * OWN + t0, self.OPW)
                fb2 = self.bufs("ST", dc * OWN + t0, self.OPW)
                oap = self.ap("A", dc * OWN + t0, self.OPW)
                ob = self.bufs("A", dc * OWN + t0, self.OPW)
                self.op("V", lambda e, o=oap, f=fap, b=bank: e.tensor_tensor(
                    o, self.pbank(b, self.OPW), f, AL.mult),
                    reads=[self.pb[bank]] + fb2, writes=ob)
            xap = self.ap32("H", 0, OWN)
            xb = self.bufs("H", 0, 2 * OWN)
            self.dma("Q", xap, self.d_xT[dc * 128:(dc + 1) * 128, 0:OWN],
                     writes=xb)
            oap2 = self.ap32("H", 2 * OWN, OWN)
            ob2 = self.bufs("H", 2 * OWN, 2 * OWN)
            sgap = self.ap("A", dc * OWN, OWN)
            sgb = self.bufs("A", dc * OWN, OWN)
            self.op("V", lambda e, o=oap2, a=sgap, b=xap: e.tensor_tensor(
                o, a, b, AL.add),
                reads=sgb + xb, writes=ob2)
            self.dma("Q", self.d_out[dc * 128:(dc + 1) * 128, :], oap2,
                     reads=ob2)
            if "sgT" in self.d_dbg:
                self.dma("Q", self.d_dbg["sgT"][dc * 128:(dc + 1) * 128, :],
                         self.ap("A", dc * OWN, OWN),
                         reads=self.bufs("A", dc * OWN, OWN))

    def _hdn_slot(self, i):
        OWN = self.OWN
        for reg in ("B", "E", "F", "G", "C"):
            cap = self.R[reg].size // OWN
            if i < cap:
                return (reg, i * OWN)
            i -= cap
        raise AssertionError("hdn overflow")

    def _ph3_gateup(self):
        DC, IC, OWN = self.DC, self.IC, self.OWN
        idx_silu = self.tab_idx["silu_and_others"]

        def load_tab(e, idx=idx_silu):
            return e.add_instruction(mybir.InstLoadActFuncSet(
                name=self.nc.get_next_instruction_name(),
                act_func_set_id=idx, ins=[], outs=[]))
        self.op("S", load_tab, serialize=True)
        sg = self.ap("AT", 0, OWN)
        sgb = self.bufs("AT", 0, OWN)
        uu = self.ap("AT", OWN, OWN)
        uub = self.bufs("AT", OWN, OWN)
        slot_i = 0
        for i in range(IC):
            self._kv_step(4)
            si = self._wslot(slot_i); slot_i += 1
            wap = self.ap("H", si, DC * 128)
            wb = self.bufs("H", si, DC * 128)
            self.dma("G", wap, self.d_gwT[i], writes=wb)
            for pc in range(self.NOP):
                bank = pc
                rec = []
                for d in range(DC):
                    t = self.mm(self.pbank(bank, self.OPW),
                                wap[:, d * 128:(d + 1) * 128],
                                self.xn_ap(d, pc * self.OPW, self.OPW),
                                reads=wb + self.xn_bufs(d, pc * self.OPW, self.OPW),
                                writes=[self.pb[bank]],
                                start=(d == 0), stop=(d == DC - 1),
                                inc=(d == DC - 1), record=rec)
                self.attach(t, rec)
            gsl = self.cf[:, self.o_gs + i:self.o_gs + i + 1]
            self.op("S", lambda e, o=sg, s=gsl: e.activation(
                o, self.psum[:, 0:OWN], AF.Silu, scale=s),
                reads=[self.pb[0], self.pb[1], self.cf_bufs], writes=sgb)
            si = self._wslot(slot_i); slot_i += 1
            wap2 = self.ap("H", si, DC * 128)
            wb2 = self.bufs("H", si, DC * 128)
            self.dma("G", wap2, self.d_uwT[i], writes=wb2)
            for pc in range(self.NOP):
                bank = 2 + pc
                rec = []
                for d in range(DC):
                    t = self.mm(self.pbank(bank, self.OPW),
                                wap2[:, d * 128:(d + 1) * 128],
                                self.xn_ap(d, pc * self.OPW, self.OPW),
                                reads=wb2 + self.xn_bufs(d, pc * self.OPW, self.OPW),
                                writes=[self.pb[bank]],
                                start=(d == 0), stop=(d == DC - 1),
                                inc=(d == DC - 1), record=rec)
                self.attach(t, rec)
            usl = self.cf[:, self.o_us + i:self.o_us + i + 1]
            self.op("V", lambda e, o=uu, s=usl: e.tensor_scalar(
                o, self.psum[:, 1024:1024 + OWN], s, None, AL.mult),
                reads=[self.pb[2], self.pb[3], self.cf_bufs], writes=uub)
            reg, off = self._hdn_slot(i)
            hap = self.ap(reg, off, OWN)
            hb = self.bufs(reg, off, OWN)
            self.op("V", lambda e, o=hap, a=sg, b=uu: e.tensor_tensor(
                o, a, b, AL.mult),
                reads=sgb + uub, writes=hb)
        self._kv_step(len(self._kv_units))
        if "hdn" in self.d_dbg:
            for i in range(IC):
                reg, off = self._hdn_slot(i)
                self.dma("Q", self.d_dbg["hdn"][i * 128:(i + 1) * 128, :],
                         self.ap(reg, off, OWN), reads=self.bufs(reg, off, OWN))

    def _ph4_down(self):
        DC, IC, OWN = self.DC, self.IC, self.OWN
        for hd in range(DC):
            si = (hd % 2) * IC * 128
            wap = self.ap("AT", si, IC * 128)
            wb = self.bufs("AT", si, IC * 128)
            self.dma("G", wap, self.d_dwT[hd], writes=wb)
            for pc in range(self.NOP):
                bank = (hd % 2) * 2 + pc
                rec = []
                for i in range(IC):
                    reg, off = self._hdn_slot(i)
                    t = self.mm(self.pbank(bank, self.OPW),
                                wap[:, i * 128:(i + 1) * 128],
                                self.ap(reg, off + pc * self.OPW, self.OPW),
                                reads=wb + self.bufs(reg, off + pc * self.OPW, self.OPW),
                                writes=[self.pb[bank]],
                                start=(i == 0), stop=(i == IC - 1),
                                inc=(i == IC - 1), record=rec)
                self.attach(t, rec)
            dsl = self.cf[:, self.o_ds + hd:self.o_ds + hd + 1]
            pso = (hd % 2) * 2 * 512
            fap = self.ap("ST", hd * OWN, OWN)
            fb2 = self.bufs("ST", hd * OWN, OWN)
            self.op("S", lambda e, o=fap, s=dsl, po=pso: e.activation(
                o, self.psum[:, po:po + OWN], AF.Silu, scale=s),
                reads=[self.pb[(hd % 2) * 2], self.pb[(hd % 2) * 2 + 1],
                       self.cf_bufs], writes=fb2)




    def _finish(self):
        blk = self.ctx.enter_context(self.nc.Block())
        progs = self.prog

        @blk.tensor
        def _(e):
            for f in progs["T"]:
                f(e)

        @blk.vector
        def _(e):
            for f in progs["V"]:
                f(e)

        @blk.scalar
        def _(e):
            for f in progs["S"]:
                f(e)

        @blk.sync
        def _(e):
            for f in progs["Q"]:
                f(e)

        @blk.gpsimd
        def _(e):
            for f in progs["G"]:
                f(e)
        self.ctx.close()


_FULL = dict(D=2048, TOK=2048, H=16, DH=64, INTER=5632, HG=4)
_NC_CACHE = {}


def build_nc(cfg):
    key = tuple(sorted((k, tuple(sorted(v)) if isinstance(v, (list, set, tuple)) else v)
                       for k, v in cfg.items()))
    if key not in _NC_CACHE:
        _NC_CACHE[key] = MK(cfg).nc
    return _NC_CACHE[key]


def kernel(**inputs):
    x = np.ascontiguousarray(np.asarray(inputs["L0_attn_x"], dtype=np.float32))
    B, S, D = x.shape
    wq = np.ascontiguousarray(np.asarray(inputs["L0_attn_Wq"], dtype=np.float32))
    wk = np.ascontiguousarray(np.asarray(inputs["L0_attn_Wk"], dtype=np.float32))
    wv = np.ascontiguousarray(np.asarray(inputs["L0_attn_Wv"], dtype=np.float32))
    wo = np.ascontiguousarray(np.asarray(inputs["L0_attn_Wo"], dtype=np.float32))
    gwT = np.ascontiguousarray(np.asarray(inputs["L0_ffn_gate_w"], dtype=np.float32).T)
    uwT = np.ascontiguousarray(np.asarray(inputs["L0_ffn_up_w"], dtype=np.float32).T)
    dwT = np.ascontiguousarray(np.asarray(inputs["L0_ffn_down_w"], dtype=np.float32).T)
    VW = min(512, wq.shape[1])
    com = dict(
        wq=tile_cols(wq, 128), wk=tile_cols(wk, 128), wv=tile_cols(wv, VW),
        wo=wo, gwT=tile_cols(gwT, 128), uwT=tile_cols(uwT, 128),
        dwT=tile_cols(dwT, 128),
        gs=np.ascontiguousarray(np.asarray(inputs["L0_ffn_gate_s"], dtype=np.float32)),
        us=np.ascontiguousarray(np.asarray(inputs["L0_ffn_up_s"], dtype=np.float32)),
        ds=np.ascontiguousarray(np.asarray(inputs["L0_ffn_down_s"], dtype=np.float32)),
        gnw=np.ascontiguousarray(np.asarray(inputs["L0_attn_gn_weight"], dtype=np.float32)),
        gnb=np.ascontiguousarray(np.asarray(inputs["L0_attn_gn_bias"], dtype=np.float32)),
        lq1=np.ascontiguousarray(np.asarray(inputs["L0_attn_lambda_q1"], dtype=np.float32)),
        lk1=np.ascontiguousarray(np.asarray(inputs["L0_attn_lambda_k1"], dtype=np.float32)),
        lq2=np.ascontiguousarray(np.asarray(inputs["L0_attn_lambda_q2"], dtype=np.float32)),
        lk2=np.ascontiguousarray(np.asarray(inputs["L0_attn_lambda_k2"], dtype=np.float32)),
    )
    nc = build_nc(_FULL)
    OWN = S // 2
    in_maps = []
    for c in range(8):
        bb, ph = c // 2, c % 2
        if ph == 0:
            xs = x[bb]
        else:
            xs = np.concatenate([x[bb, OWN:], x[bb, :OWN]], axis=0)
        m = dict(com)
        m["xT"] = np.ascontiguousarray(xs.T)
        in_maps.append(m)
    res = run_bass_kernel_spmd(nc, in_maps, core_ids=list(range(8)))
    out = np.empty((B, S, D), dtype=np.float32)
    for c in range(8):
        bb, ph = c // 2, c % 2
        out[bb, ph * OWN:(ph + 1) * OWN] = \
            np.ascontiguousarray(res.results[c]["outT"]).T
    return out


# version 27
# speedup vs baseline: 1.2553x; 1.2553x over previous
import math
import numpy as np
import ml_dtypes
from contextlib import ExitStack

import concourse.bass as bass
import concourse.mybir as mybir
from concourse.bass_utils import run_bass_kernel_spmd

F32 = mybir.dt.float32
BF16 = mybir.dt.bfloat16
AF = mybir.ActivationFunctionType
AL = mybir.AluOpType
AXX = mybir.AxisListType.X

GR = 512


def tile_cols(a, cw):
    R, W = a.shape
    return np.ascontiguousarray(
        a.reshape(R // 128, 128, W // cw, cw).transpose(2, 1, 0, 3)
    ).reshape(W // cw, 128, (R // 128) * cw)


class Buf:
    __slots__ = ("w", "r")

    def __init__(self):
        self.w = None
        self.r = []


class Region:
    def __init__(self, off, size):
        self.off = off
        self.size = size
        n = (size + GR - 1) // GR
        self.bufs = [Buf() for _ in range(n)]

    def b(self, off, n):
        lo = off // GR
        hi = (off + n + GR - 1) // GR
        return self.bufs[lo:hi]


class MK:
    ENGS = ("T", "V", "S", "Q", "G", "P", "Q2", "G2")
    STEP = {"T": 1, "V": 1, "S": 1, "Q": 16, "G": 16, "P": 1, "Q2": 16, "G2": 16}
    QCHAN = {"Q": ("Q", "Q2"), "G": ("G", "G2")}

    def __init__(self, cfg):
        self.cfg = cfg
        c = cfg
        self.D = c["D"]; self.TOK = c["TOK"]; self.H = c["H"]
        self.DH = c["DH"]; self.INTER = c["INTER"]; self.HG = c["HG"]
        self.DC = self.D // 128
        self.KT = self.TOK // 128
        assert self.KT % 2 == 0
        self.OWN = self.TOK // 2
        self.OC = self.OWN // 128
        self.QH = min(512, self.OWN)
        self.NQH = self.OWN // self.QH
        self.IC = self.INTER // 128
        self.NG = self.H // self.HG
        self.PW = min(512, self.OWN)
        self.NP = self.TOK // self.PW
        self.OPW = min(512, self.OWN)
        self.NOP = self.OWN // self.OPW
        self.dbg = set(c.get("dbg", ()))

        self.nc = bass.Bass("TRN2", target_bir_lowering=False)
        self.ctx = ExitStack()
        self.prog = {e: [] for e in ("T", "V", "S", "Q", "G")}
        self._flip = {"Q": 0, "G": 0}
        self.cnt = {e: 0 for e in self.ENGS}
        self.maxw = {e: {} for e in self.ENGS}
        self._wpref = {}
        self._mk_dram()
        self._mk_sbuf()
        self._mk_sems()
        self._build()
        self._finish()


    def _mk_sems(self):
        self.sem = {}
        for e in self.ENGS:
            self.sem[e] = self.ctx.enter_context(self.nc.semaphore("sem" + e))
        self.sem["C"] = self.ctx.enter_context(self.nc.semaphore("semC"))
        self.k_snd_buf = Buf()
        self.v_snd_buf = Buf()
        self.kv_rcv_buf = Buf()

    def op(self, eng, fn, reads=(), writes=(), inc=True, serialize=False,
           record=None):
        toks = []
        if serialize and self.cnt[eng] > 0:
            toks.append((eng, self.cnt[eng]))
        for b in reads:
            if b.w is not None:
                toks.append(b.w)
        for b in writes:
            if b.w is not None:
                toks.append(b.w)
            toks.extend(b.r)
        need = {}
        for (e2, v) in toks:
            if eng == "T" and e2 == "T":
                continue
            if v > need.get(e2, -1):
                need[e2] = v
        if eng in ("Q", "G", "Q2", "G2") and self.cnt[eng] > need.get(eng, -1):
            need[eng] = self.cnt[eng]
        waits = []
        mw = self.maxw[eng]
        for e2, v in need.items():
            if v > mw.get(e2, 0):
                waits.append((e2, v))
                mw[e2] = v
        tok = None
        if inc:
            self.cnt[eng] += self.STEP[eng]
            tok = (eng, self.cnt[eng])
        sems = self.sem
        step = self.STEP[eng]

        def thunk(e, fn=fn, waits=waits, tok=tok, eng=eng):
            for (e2, v) in waits:
                e.wait_ge(sems[e2], v)
            ins = fn(e)
            if tok is not None:
                ins.then_inc(sems[eng], step)

        stream = {"P": "G", "Q2": "Q", "G2": "G"}.get(eng, eng)
        self.prog[stream].append(thunk)
        if tok is not None:
            for b in reads:
                b.r.append(tok)
            for b in writes:
                b.w = tok
                b.r = []
        elif record is not None:
            record.extend(reads)
            record.extend(writes)
        return tok

    def attach(self, tok, bufs):
        for b in bufs:
            b.r.append(tok)

    def dma(self, eng, out_ap, in_ap, reads=(), writes=(), slow=False):
        if eng in self._flip:
            ch = self.QCHAN[eng][self._flip[eng]]
            self._flip[eng] ^= 1
            eng = ch
        if slow:
            return self.op(eng, lambda e: e.dma_start(
                out=out_ap, in_=in_ap, allow_slow_non_contiguous=True),
                reads=reads, writes=writes)
        return self.op(eng, lambda e: e.dma_start(out=out_ap, in_=in_ap),
                       reads=reads, writes=writes)

    def mm(self, out_ap, lhsT_ap, rhs_ap, reads, writes, start, stop,
           inc=True, record=None):
        return self.op("T", lambda e: e.matmul(out_ap, lhsT_ap, rhs_ap,
                                               start=start, stop=stop),
                       reads=reads, writes=writes, inc=inc, record=record)


    def _mk_dram(self):
        nc = self.nc
        D, TOK, OWN, INTER = self.D, self.TOK, self.OWN, self.INTER
        inner = self.H * 2 * self.DH
        dt = nc.dram_tensor
        self.d_xT = dt("xT", [D, TOK], F32, kind="ExternalInput").ap()
        DC, IC = self.DC, self.IC
        NIC = self.H
        VW = min(512, inner)
        self.VW = VW
        self.d_wq = dt("wq", [NIC, 128, DC * 128], F32, kind="ExternalInput").ap()
        self.d_wk = dt("wk", [NIC, 128, DC * 128], F32, kind="ExternalInput").ap()
        self.d_wv = dt("wv", [inner // VW, 128, DC * VW], F32,
                       kind="ExternalInput").ap()
        self.d_wo = dt("wo", [inner, D], F32, kind="ExternalInput").ap()
        self.d_gwT = dt("gwT", [IC, 128, DC * 128], F32, kind="ExternalInput").ap()
        self.d_uwT = dt("uwT", [IC, 128, DC * 128], F32, kind="ExternalInput").ap()
        self.d_dwT = dt("dwT", [DC, 128, IC * 128], F32, kind="ExternalInput").ap()
        self.d_gs = dt("gs", [INTER], F32, kind="ExternalInput").ap()
        self.d_us = dt("us", [INTER], F32, kind="ExternalInput").ap()
        self.d_ds = dt("ds", [D], F32, kind="ExternalInput").ap()
        self.d_gnw = dt("gnw", [inner], F32, kind="ExternalInput").ap()
        self.d_gnb = dt("gnb", [inner], F32, kind="ExternalInput").ap()
        self.d_lq1 = dt("lq1", [self.DH], F32, kind="ExternalInput").ap()
        self.d_lk1 = dt("lk1", [self.DH], F32, kind="ExternalInput").ap()
        self.d_lq2 = dt("lq2", [self.DH], F32, kind="ExternalInput").ap()
        self.d_lk2 = dt("lk2", [self.DH], F32, kind="ExternalInput").ap()
        self.d_out = dt("outT", [D, OWN], F32, kind="ExternalOutput").ap()
        W0 = self.H * 2 * self.DH
        self.W0 = W0
        self.d_k_snd = dt("k_snd", [W0, OWN], BF16)
        self.d_k_rcv = dt("k_rcv", [2 * W0, OWN], BF16)
        self.d_v_snd = dt("v_snd", [OWN, W0], BF16)
        self.d_v_rcv = dt("v_rcv", [2 * OWN, W0], BF16)
        self.d_dbg = {}
        for name, shp in (("xn", [D, OWN]), ("aT", [inner, OWN]),
                          ("sT", [D, OWN]), ("hdn", [INTER, OWN]),
                          ("sgT", [D, OWN])):
            if name in self.dbg:
                self.d_dbg[name] = dt("dbg_" + name, shp, BF16,
                                      kind="ExternalOutput").ap()

    def _mk_sbuf(self):
        nc = self.nc
        D, TOK, OWN = self.D, self.TOK, self.OWN
        DC, KT, QH, IC, HG = self.DC, self.KT, self.QH, self.IC, self.HG
        sizes = {}
        sizes["A"] = DC * OWN
        sizes["B"] = max(16 * OWN, 8 * D)
        sizes["C"] = max(KT * QH, DC * HG * 128)
        sizes["E"] = max(HG * TOK, 4 * D)
        sizes["F"] = KT * HG * 128
        sizes["G"] = HG * OWN + 2 * QH
        sizes["H"] = max(2 * DC * 128, 8 * OWN // 2)
        sizes["AT"] = max(self.H * OWN, 2 * IC * 128 + OWN, 2 * OWN)
        sizes["ST"] = max(DC * OWN, 28 * self.PW)
        sizes["SC"] = 8 * QH
        total = 0
        offs = {}
        for k in ("A", "B", "C", "E", "F", "G", "H", "AT", "ST", "SC"):
            offs[k] = total
            total += sizes[k]
        self.pool_el = total
        self.pool = self.ctx.enter_context(
            nc.sbuf_tensor("pool", [128, total], BF16))
        self.R = {k: Region(offs[k], sizes[k]) for k in offs}

        self.cb = self.ctx.enter_context(
            nc.sbuf_tensor("cb", [128, 3 * 128], BF16))
        self.cb_bufs = [Buf() for _ in range(3)]
        self.cf = self.ctx.enter_context(
            nc.sbuf_tensor("cf", [128, 8 + IC * 2 + DC + self.H * 2], F32))
        self.cf_bufs = Buf()
        self.row = self.ctx.enter_context(
            nc.sbuf_tensor("rowscr", [1, 8 * max(64, self.DH)], F32))
        self.row_buf = Buf()
        self.ones_row = self.ctx.enter_context(
            nc.sbuf_tensor("ones_row", [1, 128], F32))
        self.ones_row_buf = Buf()
        self.lam128 = self.ctx.enter_context(
            nc.sbuf_tensor("lam128", [128, 1], F32))
        self.lam128_buf = Buf()
        self.cm8 = self.ctx.enter_context(
            nc.sbuf_tensor("cm8", [128, 1], F32))
        self.cm8_buf = Buf()
        self.ceps = self.ctx.enter_context(
            nc.sbuf_tensor("ceps", [128, 1], F32))
        self.ceps_buf = Buf()
        self.lnlam = self.ctx.enter_context(
            nc.sbuf_tensor("lnlam", [128, 1], F32))
        self.lnlam_buf = Buf()
        self.psum = self.ctx.enter_context(
            nc.psum_tensor("ps", [128, 4096], F32))
        self.pb = [Buf() for _ in range(8)]

    def ap(self, reg, off, n):
        r = self.R[reg]
        assert off + n <= r.size, (reg, off, n, r.size)
        return self.pool[:, r.off + off: r.off + off + n]

    def ap32(self, reg, off, n32):
        r = self.R[reg]
        assert off + 2 * n32 <= r.size, (reg, off, n32, r.size)
        return self.pool[:, r.off + off: r.off + off + 2 * n32].bitcast(F32)

    def bufs(self, reg, off, n):
        return self.R[reg].b(off, n)

    def pbank(self, b, width=None, off=0):
        w = 512 if width is None else width
        return self.psum[:, b * 512 + off: b * 512 + off + w]


    def _build(self):
        stop = self.cfg.get("stop_after", "")
        phases = [("consts", self._consts), ("ph0", self._ph0_layernorm),
                  ("ph1", self._ph1_lam), ("ph1b", self._ph1b_kv),
                  ("ph3", self._ph3_gateup), ("ph4", self._ph4_down),
                  ("ph2", self._ph2_attention), ("ph2b", self._ph2b_wo)]
        for name, fn in phases:
            fn()
            if name == stop:
                break
        sems = self.sem
        for ch in ("Q", "Q2"):
            v = self.cnt[ch]
            if v:
                self.prog["Q"].append(lambda e, c=ch, vv=v: e.wait_ge(sems[c], vv))
        for ch in ("G", "G2"):
            v = self.cnt[ch]
            if v:
                self.prog["G"].append(lambda e, c=ch, vv=v: e.wait_ge(sems[c], vv))

    def _consts(self):
        nc = self.nc
        H, DC, IC = self.H, self.DC, self.IC
        from concourse.hw_specs import get_activation_tables
        tabs = list(get_activation_tables(self.nc.m.arch).keys())
        self.tab_idx = {name: i for i, name in enumerate(tabs)}

        def load_tab(e, idx):
            return e.add_instruction(mybir.InstLoadActFuncSet(
                name=self.nc.get_next_instruction_name(),
                act_func_set_id=idx, ins=[], outs=[]))
        idx_nle = self.tab_idx["natural_log_exp_and_others"]
        self.op("S", lambda e: load_tab(e, idx_nle), serialize=True)

        cb, cf = self.cb, self.cf
        b = self.cb_bufs
        self.op("V", lambda e: e.memset(cb[:, 0:128], 1.0), writes=[b[0]])
        self.op("V", lambda e: e.memset(cb[:, 128:256], 1.0 / (2 * self.DH)),
                writes=[b[1]])
        self.op("V", lambda e: e.memset(cb[:, 256:384], 1.0 / self.D),
                writes=[b[2]])
        self.op("V", lambda e: e.memset(self.ones_row[:], 1.0),
                writes=[self.ones_row_buf])
        self.op("V", lambda e: e.memset(self.cm8[:], -8.0),
                writes=[self.cm8_buf])
        self.op("V", lambda e: e.memset(self.ceps[:], 1e-5),
                writes=[self.ceps_buf])
        self.o_gs = 0
        self.o_us = IC
        self.o_ds = 2 * IC
        self.o_gw = 2 * IC + DC
        self.o_gb = 2 * IC + DC + H
        cfb = self.cf_bufs
        self.dma("Q", cf[:, self.o_gs:self.o_gs + IC],
                 self.d_gs.rearrange("(c p) -> p c", p=128), writes=[cfb], slow=True)
        self.dma("Q", cf[:, self.o_us:self.o_us + IC],
                 self.d_us.rearrange("(c p) -> p c", p=128), writes=[cfb], slow=True)
        self.dma("Q", cf[:, self.o_ds:self.o_ds + DC],
                 self.d_ds.rearrange("(c p) -> p c", p=128), writes=[cfb], slow=True)
        self.dma("Q", cf[:, self.o_gw:self.o_gw + H],
                 self.d_gnw.rearrange("(c p) -> p c", p=128), writes=[cfb], slow=True)
        self.dma("Q", cf[:, self.o_gb:self.o_gb + H],
                 self.d_gnb.rearrange("(c p) -> p c", p=128), writes=[cfb], slow=True)
        self.op("V", lambda e: e.tensor_scalar_mul(
            cf[:, self.o_gw:self.o_gw + 2 * H],
            cf[:, self.o_gw:self.o_gw + 2 * H], 0.2),
            reads=[cfb], writes=[cfb])

    def xn_ap(self, c, t0, n):
        OWN = self.OWN
        if t0 >= OWN:
            return self.ap("B", c * (self.TOK - OWN) + (t0 - OWN), n)
        assert t0 + n <= OWN
        return self.ap("A", c * OWN + t0, n)

    def xn_bufs(self, c, t0, n):
        OWN = self.OWN
        if t0 >= OWN:
            return self.bufs("B", c * (self.TOK - OWN) + (t0 - OWN), n)
        return self.bufs("A", c * OWN + t0, n)

    def _ph0_layernorm(self):
        nc = self.nc
        D, TOK, OWN, DC, PW, NP = self.D, self.TOK, self.OWN, self.DC, self.PW, self.NP
        NP = self.OWN // PW
        xT3 = self.d_xT.rearrange("(c p) t -> p c t", p=128)
        for p in range(NP):
            t0 = p * PW
            base = self.R["A"].off
            out3 = self.pool[:, base: base + DC * OWN].rearrange(
                "p (c f) -> p c f", c=DC)[:, :, t0:t0 + PW]
            wb = []
            for c in range(DC):
                wb += self.xn_bufs(c, t0, PW)
            self.dma("G", out3, xT3[:, :, t0:t0 + PW], writes=wb)
        onesLN = self.cb[:, 256:384]
        ST = "ST"
        def scr(p):
            base = (p % 2) * 12 * PW
            muS = self.ap32(ST, base, PW)
            var = self.ap32(ST, base + 2 * PW, PW)
            rstd = self.ap32(ST, base + 4 * PW, PW)
            mumix = self.ap32(ST, base + 6 * PW, PW)
            return (muS, var, rstd, mumix,
                    self.bufs(ST, base, 2 * PW),
                    self.bufs(ST, base + 2 * PW, 2 * PW),
                    self.bufs(ST, base + 4 * PW, 2 * PW),
                    self.bufs(ST, base + 6 * PW, 2 * PW))
        for p in range(NP):
            muS, var, rstd, mumix, bmu, bvar, brstd, bmix = scr(p)
            t0 = p * PW
            bmu_k = (p % 2) * 2
            bs2_k = bmu_k + 1
            rec0, rec1 = [], []
            for c in range(DC):
                t = self.mm(self.pbank(bmu_k, PW), onesLN, self.xn_ap(c, t0, PW),
                            reads=[self.cb_bufs[2]] + self.xn_bufs(c, t0, PW),
                            writes=[self.pb[bmu_k]], start=(c == 0),
                            stop=(c == DC - 1), inc=(c == DC - 1), record=rec0)
                if t is not None:
                    self.attach(t, rec0)
            for c in range(DC):
                sq = self.ap(ST, 24 * PW + ((p * DC + c) % 2) * PW, PW)
                sqb = self.bufs(ST, 24 * PW + ((p * DC + c) % 2) * PW, PW)
                self.op("S", lambda e, i=self.xn_ap(c, t0, PW), o=sq:
                        e.activation(o, i, AF.Square),
                        reads=self.xn_bufs(c, t0, PW), writes=sqb)
                self.mm(self.pbank(bs2_k, PW), onesLN, sq,
                        reads=[self.cb_bufs[2]] + sqb,
                        writes=[self.pb[bs2_k]], start=(c == 0),
                        stop=(c == DC - 1))
            self.op("V", lambda e, o=muS, b=bmu_k: e.tensor_copy(o, self.pbank(b, PW)),
                    reads=[self.pb[bmu_k]], writes=bmu)
            self.op("V", lambda e, o=var, a=muS: e.tensor_tensor(o, a, a, AL.mult),
                    reads=bmu, writes=bvar)
            self.op("V", lambda e, o=var, a=var, b=bs2_k: e.tensor_tensor(
                o, self.pbank(b, PW), a, AL.subtract),
                reads=[self.pb[bs2_k]] + bvar, writes=bvar)
            self.op("S", lambda e, o=rstd, a=var: e.activation(
                o, a, AF.Ln, bias=self.ceps[:, 0:1]),
                reads=bvar + [self.ceps_buf], writes=brstd)
            self.op("S", lambda e, o=rstd, a=rstd: e.activation(
                o, a, AF.Exp, scale=-0.5),
                reads=brstd, writes=brstd)
            self.op("V", lambda e, o=mumix, a=muS, b2=rstd: e.scalar_tensor_tensor(
                o, a, -1.0, b2, AL.mult, AL.mult),
                reads=bmu + brstd, writes=bmix)
            for c in range(DC):
                eng = "V" if c % 2 == 0 else "P"
                t = self.ap(ST, 26 * PW + (c % 2) * PW, PW)
                tb = self.bufs(ST, 26 * PW + (c % 2) * PW, PW)
                self.op(eng, lambda e, o=t, a=self.xn_ap(c, t0, PW), b2=rstd:
                        e.tensor_tensor(o, a, b2, AL.mult),
                        reads=self.xn_bufs(c, t0, PW) + brstd, writes=tb)
                self.op(eng, lambda e, o=self.xn_ap(c, t0, PW), a=t, b2=mumix:
                        e.tensor_tensor(o, a, b2, AL.add),
                        reads=tb + bmix, writes=self.xn_bufs(c, t0, PW))
        if "xn" in self.d_dbg:
            for c in range(DC):
                self.dma("Q", self.d_dbg["xn"][c * 128:(c + 1) * 128, :],
                         self.xn_ap(c, 0, OWN), reads=self.xn_bufs(c, 0, OWN))

    def _ph1_lam(self):
        row, rb = self.row, self.row_buf
        DH = self.DH
        for i, d in enumerate((self.d_lq1, self.d_lk1, self.d_lq2, self.d_lk2)):
            self.dma("Q", row[0:1, i * DH:(i + 1) * DH],
                     d.rearrange("(o f) -> o f", o=1), writes=[rb])
        p1 = row[0:1, 4 * DH:5 * DH]
        self.op("V", lambda e: e.tensor_tensor(
            p1, row[0:1, 0:DH], row[0:1, DH:2 * DH], AL.mult),
            reads=[rb], writes=[rb])
        s1 = row[0:1, 5 * DH:5 * DH + 1]
        self.op("V", lambda e: e.reduce_sum(s1, p1, axis=AXX),
                reads=[rb], writes=[rb])
        self.op("V", lambda e: e.tensor_tensor(
            p1, row[0:1, 2 * DH:3 * DH], row[0:1, 3 * DH:4 * DH], AL.mult),
            reads=[rb], writes=[rb])
        s2 = row[0:1, 5 * DH + 1:5 * DH + 2]
        self.op("V", lambda e: e.reduce_sum(s2, p1, axis=AXX),
                reads=[rb], writes=[rb])
        e1 = row[0:1, 5 * DH + 2:5 * DH + 3]
        e2 = row[0:1, 5 * DH + 3:5 * DH + 4]
        self.op("S", lambda e: e.activation(e1, s1, AF.Exp), reads=[rb], writes=[rb])
        self.op("S", lambda e: e.activation(e2, s2, AF.Exp), reads=[rb], writes=[rb])
        lam = row[0:1, 5 * DH + 4:5 * DH + 5]
        self.op("V", lambda e: e.tensor_tensor(lam, e1, e2, AL.subtract),
                reads=[rb], writes=[rb])
        self.op("V", lambda e: e.tensor_scalar_add(lam, lam, 0.8),
                reads=[rb], writes=[rb])
        self.mm(self.pbank(2, 1), self.ones_row[:], lam,
                reads=[self.ones_row_buf, rb], writes=[self.pb[2]],
                start=True, stop=True)
        self.op("V", lambda e: e.tensor_copy(self.lam128[:], self.pbank(2, 1)),
                reads=[self.pb[2]], writes=[self.lam128_buf])




    def _wslot(self, idx):
        base = (idx % 2) * self.DC * 128
        return base

    def _ph1b_kv(self):
        DC, OWN, OC = self.DC, self.OWN, self.OC
        NIC = self.H
        ST = "ST"
        kvK = self.d_k_snd.ap()
        units = []

        def k_unit(icc):
            def emit():
                at_off = 2 * OWN + (icc % 2) * DC * 128
                wap = self.ap("AT", at_off, DC * 128)
                wb = self.bufs("AT", at_off, DC * 128)
                self.dma("G", wap, self.d_wk[icc], writes=wb)
                sslot = (icc % 2) * OWN
                for pc in range(self.NOP):
                    t0 = pc * self.OPW
                    bank = 4 + (icc % 2) * 2 + pc
                    rec = []
                    for d in range(DC):
                        t = self.mm(self.pbank(bank, self.OPW),
                                    wap[:, d * 128:(d + 1) * 128],
                                    self.xn_ap(d, t0, self.OPW),
                                    reads=wb + self.xn_bufs(d, t0, self.OPW),
                                    writes=[self.pb[bank]],
                                    start=(d == 0), stop=(d == DC - 1),
                                    inc=(d == DC - 1), record=rec)
                    self.attach(t, rec)
                    sg = self.ap(ST, sslot + t0, self.OPW)
                    sgb = self.bufs(ST, sslot + t0, self.OPW)
                    self.op("V", lambda e, o=sg, b=bank: e.tensor_copy(
                        o, self.pbank(b, self.OPW)),
                        reads=[self.pb[bank]], writes=sgb)
                self.dma("Q", kvK[icc * 128:(icc + 1) * 128, :],
                         self.ap(ST, sslot, OWN),
                         reads=self.bufs(ST, sslot, OWN),
                         writes=[self.k_snd_buf])
            return emit

        VW = self.VW
        NVG = self.W0 // VW

        def v_unit(vg, t, uidx):
            def emit():
                wvb = self.bufs("C", 0, DC * VW)
                if t == 0:
                    self.dma("G", self.ap("C", 0, DC * VW),
                             self.d_wv[vg], writes=wvb)
                bank = 4 + uidx % 4
                rec = []
                for d in range(DC):
                    tk = self.mm(self.pbank(bank, VW),
                                 self.xn_ap(d, t * 128, 128),
                                 self.ap("C", d * VW, VW),
                                 reads=self.xn_bufs(d, t * 128, 128) + wvb,
                                 writes=[self.pb[bank]],
                                 start=(d == 0), stop=(d == DC - 1),
                                 inc=(d == DC - 1), record=rec)
                self.attach(tk, rec)
                sg = self.ap(ST, 2 * OWN + (uidx % 2) * VW, VW)
                sgb = self.bufs(ST, 2 * OWN + (uidx % 2) * VW, VW)
                self.op("V", lambda e, o=sg, b=bank, w=VW: e.tensor_copy(
                    o, self.pbank(b, w)),
                    reads=[self.pb[bank]], writes=sgb)
                self.dma("Q", self.d_v_snd.ap()[
                    t * 128:(t + 1) * 128, vg * VW:(vg + 1) * VW],
                    sg, reads=sgb, writes=[self.v_snd_buf])
            return emit

        for icc in range(NIC):
            units.append(k_unit(icc))
        ui = 0
        for vg in range(NVG):
            for t in range(OC):
                units.append(v_unit(vg, t, ui))
                ui += 1
        rg = self.cfg.get("replica_groups")
        if rg is None:
            rg = [[0, 1], [2, 3], [4, 5], [6, 7]]
        semC = self.sem["C"]
        ks, kr = self.d_k_snd.ap().opt(), self.d_k_rcv.ap().opt()
        vs, vr = self.d_v_snd.ap().opt(), self.d_v_rcv.ap().opt()

        def k_trig():
            self.op("G", lambda e: e.collective_compute(
                "AllGather", AL.bypass, replica_groups=rg,
                ins=[ks], outs=[kr]).then_inc(semC, 1),
                reads=[self.k_snd_buf], inc=False)

        def v_trig():
            self.op("G", lambda e: e.collective_compute(
                "AllGather", AL.bypass, replica_groups=rg,
                ins=[vs], outs=[vr]).then_inc(semC, 1),
                reads=[self.v_snd_buf], inc=False)
            self.kv_rcv_buf.w = ("C", 2)
        self._kv_units = units
        self._kv_ktrig = (NIC, k_trig)
        self._kv_vtrig = (len(units), v_trig)
        self._kv_consumed = 0

    def _kv_step(self, n=2):
        for _ in range(n):
            if self._kv_consumed < len(self._kv_units):
                self._kv_units[self._kv_consumed]()
                self._kv_consumed += 1
                if self._kv_consumed == self._kv_ktrig[0]:
                    self._kv_ktrig[1]()
                if self._kv_consumed == self._kv_vtrig[0]:
                    self._kv_vtrig[1]()

    def _load_wcol(self, dram, chunk, slot_idx):
        si = self._wslot(slot_idx)
        wap = self.ap("H", si, self.DC * 128)
        wb = self.bufs("H", si, self.DC * 128)
        self.dma("G", wap, dram[chunk], writes=wb)
        return (wap, wb)

    def _ph2_attention(self):
        DC, HG, NG, OWN, TOK, QH, KT = (self.DC, self.HG, self.NG, self.OWN,
                                        self.TOK, self.QH, self.KT)
        DH = self.DH
        slot_i = 0
        self._pend_tail = None
        idx_nle = self.tab_idx["natural_log_exp_and_others"]

        def load_tab(e, idx=idx_nle):
            return e.add_instruction(mybir.InstLoadActFuncSet(
                name=self.nc.get_next_instruction_name(),
                act_func_set_id=idx, ins=[], outs=[]))
        self.op("S", load_tab, serialize=True)
        for z in range(2):
            apz = self.ap("G", HG * OWN + z * QH, QH)
            self.op("V", lambda e, a=apz: e.memset(a, 0.0),
                    writes=self.bufs("G", HG * OWN + z * QH, QH))
        for i in range(self.H):
            reg, off = self._wo_slot(i)
            if reg == "B":
                self.dma("G", self.ap(reg, off, self.D),
                         self.d_wo[i * 128:(i + 1) * 128, :],
                         writes=self.bufs(reg, off, self.D))
        NGrun = self.cfg.get("ph2_groups", NG)
        noheads = self.cfg.get("ph2_noheads", False)
        for g in range(NGrun):
            col0 = g * HG * 128
            for cc in range(HG):
                wap, wb = self._load_wcol(self.d_wq, g * HG + cc, slot_i)
                slot_i += 1
                for pc in range(self.NOP):
                    t0 = pc * self.OPW
                    bank = pc % 2
                    rec = []
                    for d in range(DC):
                        t = self.mm(self.pbank(bank, self.OPW),
                                    wap[:, d * 128:(d + 1) * 128],
                                    self.xn_ap(d, t0, self.OPW),
                                    reads=wb + self.xn_bufs(d, t0, self.OPW),
                                    writes=[self.pb[bank]],
                                    start=(d == 0), stop=(d == DC - 1),
                                    inc=(d == DC - 1), record=rec)
                    self.attach(t, rec)
                    q1 = self.ap("G", cc * OWN + t0, self.OPW)
                    qb1 = self.bufs("G", cc * OWN + t0, self.OPW)
                    self.op("V", lambda e, o=q1, b=bank: e.tensor_copy(
                        o, self.pbank(b, self.OPW)),
                        reads=[self.pb[bank]], writes=qb1)
            gw = HG * 128
            W0 = self.W0
            kvb = [self.kv_rcv_buf]
            for half in range(2):
                kview = self.d_k_rcv.ap()[half * W0:(half + 1) * W0, :]
                eb = []
                for cc in range(HG):
                    eb += self.bufs("E", cc * TOK + half * OWN, OWN)
                eout = self.pool[:, self.R["E"].off:
                                 self.R["E"].off + HG * TOK].rearrange(
                    "p (c t) -> p c t", c=HG)[:, :, half * OWN:(half + 1) * OWN]
                self.dma("Q", eout,
                         kview[g * gw:(g + 1) * gw, :].rearrange(
                             "(c p) t -> p c t", p=128),
                         reads=kvb, writes=eb)
                vview = self.d_v_rcv.ap()[
                    half * OWN:(half + 1) * OWN, g * gw:(g + 1) * gw]
                fb = []
                for t in range(KT // 2):
                    fb += self.bufs("F", (half * KT // 2 + t) * gw, gw)
                fout = self.pool[:, self.R["F"].off:
                                 self.R["F"].off + KT * gw].rearrange(
                    "p (t f) -> p t f", t=KT)[:, half * KT // 2:(half + 1) * KT // 2, :]
                self.dma("Q", fout,
                         vview.rearrange("(t p) f -> p t f", p=128),
                         reads=kvb, writes=fb)
            for l in range(HG):
                if noheads:
                    break
                head = g * HG + l
                for qh in range(self.NQH):
                    nj = None
                    if qh + 1 < self.NQH:
                        nj = (l, (qh + 1) * self.QH)
                    elif l + 1 < HG:
                        nj = (l + 1, 0)
                    self._pend_tail = self._attend(head, l, qh,
                                                   flush=self._pend_tail,
                                                   next_job=nj)
        if self._pend_tail is not None:
            self._pend_tail()
            self._pend_tail = None
        if "aT" in self.d_dbg:
            for hh in range(self.H):
                self.dma("Q", self.d_dbg["aT"][hh * 128:(hh + 1) * 128, :],
                         self.ap("AT", hh * OWN, OWN),
                         reads=self.bufs("AT", hh * OWN, OWN))

    def _emit_qz(self, l, q0):
        QH, DH, OWN = self.QH, self.DH, self.OWN
        aps, bs = [], []
        for comp in range(2):
            apz = self.ap("G", self.HG * OWN + comp * QH, QH)
            bz = self.bufs("G", self.HG * OWN + comp * QH, QH)
            rows = slice(0, DH) if comp == 0 else slice(DH, 128)
            qp = self.ap("G", l * OWN + q0, QH)
            qpb = self.bufs("G", l * OWN + q0, QH)
            self.op("V", lambda e, o=apz, a=qp, r=rows: e.tensor_copy(
                o[r, :], a[r, :]),
                reads=qpb, writes=bz)
            aps.append(apz)
            bs.append(bz)
        self._qz_state = (l, q0, aps, bs)
        return aps, bs

    def _attend(self, head, l, qh, flush=None, next_job=None):
        QH, KT, OWN, TOK = self.QH, self.KT, self.OWN, self.TOK
        DH = self.DH
        q0 = qh * QH
        ones128 = self.cb[:, 0:128]
        onesGN = self.cb[:, 128:256]
        SA, SB, D_, O1, O2 = 0, 2, 4, 5, 6
        kap_all = lambda kt: self.ap("E", l * TOK + kt * 128, 128)
        kb_all = lambda kt: self.bufs("E", l * TOK + kt * 128, 128)
        r_scr = [self.ap32("SC", 0, QH), self.ap32("SC", 2 * QH, QH)]
        r_bufs = [self.bufs("SC", 0, 2 * QH), self.bufs("SC", 2 * QH, 2 * QH)]

        crec = [[], []]
        prec = [[], []]

        st = getattr(self, "_qz_state", None)
        if st is not None and st[0] == l and st[1] == q0:
            qz_ap, qz_b = st[2], st[3]
        else:
            qz_ap, qz_b = self._emit_qz(l, q0)

        def cspv(j, comp):
            dbank = D_ if comp == 0 else 7
            obank = O1 if comp == 0 else O2
            for z in range(2):
                kt = 2 * j + z
                last = (kt == KT - 1)
                et = self.ap("C", kt * QH, QH)
                etb = self.bufs("C", kt * QH, QH)
                t = self.mm(self.pbank(dbank, QH), ones128, et,
                            reads=[self.cb_bufs[0]] + etb,
                            writes=[self.pb[dbank]],
                            start=(kt == 0), stop=last,
                            inc=last, record=crec[comp])
                if t is not None:
                    self.attach(t, crec[comp])
                vt = self.ap("F", kt * self.HG * 128 + l * 128, 128)
                vtb = self.bufs("F", kt * self.HG * 128 + l * 128, 128)
                t = self.mm(self.pbank(obank, QH), vt, et,
                            reads=vtb + etb,
                            writes=[self.pb[obank]],
                            start=(kt == 0), stop=last,
                            inc=last, record=prec[comp])
                if t is not None:
                    self.attach(t, prec[comp])

        npair = KT // 2

        def unit_flush(u):
            comp, j = u
            cspv(j, comp)
            if j == npair - 1:
                db = D_ if comp == 0 else 7
                self.op("S", lambda e, o=r_scr[comp], b=db: e.activation(
                    o, self.pbank(b, QH), AF.Ln),
                    reads=[self.pb[db]], writes=r_bufs[comp])
                self.op("S", lambda e, o=r_scr[comp]: e.activation(
                    o, o, AF.Exp, scale=-1.0),
                    reads=r_bufs[comp], writes=r_bufs[comp])

        pend = []
        unit_no = 0
        for comp in range(2):
            qz = qz_ap[comp]
            qzb = qz_b[comp]
            for j in range(npair):
                pair = SA if (j % 2 == 0) else SB
                for z in range(2):
                    kt = 2 * j + z
                    off = pair * 512 + z * QH
                    sap = self.psum[:, off: off + QH]
                    self.mm(sap, kap_all(kt), qz,
                            reads=kb_all(kt) + qzb,
                            writes=[self.pb[off // 512]],
                            start=True, stop=True)
                eap = self.ap("C", 2 * j * QH, 2 * QH)
                eb = self.bufs("C", 2 * j * QH, 2 * QH)
                self.op("S", lambda e, o=eap, p=pair: e.activation(
                    o, self.psum[:, p * 512: p * 512 + 2 * QH],
                    AF.Exp, scale=1.0 / math.sqrt(DH), bias=self.cm8[:, 0:1]),
                    reads=[self.pb[pair], self.pb[pair + 1],
                           self.pb[(pair * 512 + QH) // 512], self.cm8_buf],
                    writes=eb)
                lag = min(2, npair - 1)
                if unit_no == min(5, npair - 1) and flush is not None:
                    flush()
                unit_no += 1
                pend.append((comp, j))
                while len(pend) > lag:
                    unit_flush(pend.pop(0))
        if next_job is not None:
            self._emit_qz(*next_job)
        for u in pend:
            unit_flush(u)
        gnA = self.ap32("SC", 4 * QH, QH)
        gnB = self.ap32("SC", 6 * QH, QH)
        m2 = self.ap32("SC", 0, QH)
        bA = self.bufs("SC", 4 * QH, 2 * QH)
        bB = self.bufs("SC", 6 * QH, 2 * QH)
        bM = self.bufs("SC", 0, 2 * QH)
        self.op("V", lambda e, o=r_scr[1]: e.tensor_scalar(
            o, o, self.lam128[:, 0:1], None, AL.mult),
            reads=r_bufs[1] + [self.lam128_buf], writes=r_bufs[1])
        self.op("V", lambda e, o=gnA, r=r_scr[0]: e.tensor_tensor(
            o, self.pbank(5, QH), r, AL.mult),
            reads=[self.pb[5]] + r_bufs[0], writes=bA)
        self.op("V", lambda e, o=gnB, r=r_scr[1]: e.tensor_tensor(
            o, self.pbank(6, QH), r, AL.mult),
            reads=[self.pb[6]] + r_bufs[1], writes=bB)
        oB = self.ap("AT", head * OWN + q0, QH)
        oBb = self.bufs("AT", head * OWN + q0, QH)
        self.op("V", lambda e, o=oB, a=gnA, b=gnB: e.tensor_tensor(
            o, a, b, AL.subtract),
            reads=bA + bB, writes=oBb)
        sq = self.ap("SC", 0, QH)
        sqb = self.bufs("SC", 0, QH)
        self.op("V", lambda e, o=sq, a=oB: e.tensor_tensor(o, a, a, AL.mult),
                reads=oBb, writes=sqb)

        def tail():
            self.mm(self.pbank(7, QH), onesGN, sq,
                    reads=[self.cb_bufs[1]] + sqb, writes=[self.pb[7]],
                    start=True, stop=True)
            self.op("V", lambda e, o=gnB: e.tensor_copy(o, self.pbank(7, QH)),
                    reads=[self.pb[7]], writes=bB)
            self.mm(self.pbank(7, QH), onesGN, oB,
                    reads=[self.cb_bufs[1]] + oBb, writes=[self.pb[7]],
                    start=True, stop=True)
            self.op("V", lambda e, o=gnA: e.tensor_copy(o, self.pbank(7, QH)),
                    reads=[self.pb[7]], writes=bA)
            self.op("V", lambda e, o=m2, a=gnA: e.tensor_tensor(o, a, a, AL.mult),
                    reads=bA, writes=bM)
            self.op("V", lambda e, o=gnB, a=gnB, b=m2: e.tensor_tensor(
                o, a, b, AL.subtract),
                reads=bB + bM, writes=bB)
            self.op("S", lambda e, o=m2, a=gnB: e.activation(
                o, a, AF.Ln, bias=self.ceps[:, 0:1]),
                reads=bB + [self.ceps_buf], writes=bM)
            self.op("S", lambda e, o=gnB, a=m2: e.activation(
                o, a, AF.Exp, scale=-0.5),
                reads=bM, writes=bB)
            self.op("V", lambda e, o=gnA, a=oB, b=gnA: e.tensor_tensor(
                o, a, b, AL.subtract),
                reads=oBb + bA, writes=bA)
            self.op("V", lambda e, o=gnA, a=gnA, b=gnB: e.tensor_tensor(
                o, a, b, AL.mult),
                reads=bA + bB, writes=bA)
            gw = self.cf[:, self.o_gw + head:self.o_gw + head + 1]
            gb = self.cf[:, self.o_gb + head:self.o_gb + head + 1]
            self.op("V", lambda e, o=oB, a=gnA: e.tensor_scalar(
                o, a, gw, gb, AL.mult, AL.add),
                reads=bA + [self.cf_bufs], writes=oBb)
        return tail

    def _wo_slot(self, i):
        D = self.D
        per_b = self.R["B"].size // D
        per_c = self.R["C"].size // D
        if i < per_b:
            return ("B", i * D)
        i -= per_b
        if i < per_c:
            return ("C", i * D)
        i -= per_c
        return ("E", i * D)

    def _ph2b_wo(self):
        DC, OWN, D = self.DC, self.OWN, self.D
        NIC = self.H
        for i in range(NIC):
            reg, off = self._wo_slot(i)
            if reg != "B":
                self.dma("G", self.ap(reg, off, D),
                         self.d_wo[i * 128:(i + 1) * 128, :],
                         writes=self.bufs(reg, off, D))
        for dc in range(DC):
            for pc in range(self.NOP):
                t0 = pc * self.OPW
                bank = (dc % 2) * 2 + pc % 2
                rec = []
                for i in range(NIC):
                    reg, off = self._wo_slot(i)
                    t = self.mm(self.pbank(bank, self.OPW),
                                self.ap(reg, off + dc * 128, 128),
                                self.ap("AT", i * OWN + t0, self.OPW),
                                reads=self.bufs(reg, off + dc * 128, 128) +
                                      self.bufs("AT", i * OWN + t0, self.OPW),
                                writes=[self.pb[bank]],
                                start=(i == 0), stop=(i == NIC - 1),
                                inc=(i == NIC - 1), record=rec)
                self.attach(t, rec)
                fap = self.ap("ST", dc * OWN + t0, self.OPW)
                fb2 = self.bufs("ST", dc * OWN + t0, self.OPW)
                oap = self.ap("A", dc * OWN + t0, self.OPW)
                ob = self.bufs("A", dc * OWN + t0, self.OPW)
                self.op("V", lambda e, o=oap, f=fap, b=bank: e.tensor_tensor(
                    o, self.pbank(b, self.OPW), f, AL.mult),
                    reads=[self.pb[bank]] + fb2, writes=ob)
            xap = self.ap32("H", 0, OWN)
            xb = self.bufs("H", 0, 2 * OWN)
            self.dma("Q", xap, self.d_xT[dc * 128:(dc + 1) * 128, 0:OWN],
                     writes=xb)
            oap2 = self.ap32("H", 2 * OWN, OWN)
            ob2 = self.bufs("H", 2 * OWN, 2 * OWN)
            sgap = self.ap("A", dc * OWN, OWN)
            sgb = self.bufs("A", dc * OWN, OWN)
            self.op("V", lambda e, o=oap2, a=sgap, b=xap: e.tensor_tensor(
                o, a, b, AL.add),
                reads=sgb + xb, writes=ob2)
            self.dma("Q", self.d_out[dc * 128:(dc + 1) * 128, :], oap2,
                     reads=ob2)
            if "sgT" in self.d_dbg:
                self.dma("Q", self.d_dbg["sgT"][dc * 128:(dc + 1) * 128, :],
                         self.ap("A", dc * OWN, OWN),
                         reads=self.bufs("A", dc * OWN, OWN))

    def _hdn_slot(self, i):
        OWN = self.OWN
        for reg in ("B", "E", "F", "G", "C"):
            cap = self.R[reg].size // OWN
            if i < cap:
                return (reg, i * OWN)
            i -= cap
        raise AssertionError("hdn overflow")

    def _ph3_gateup(self):
        DC, IC, OWN = self.DC, self.IC, self.OWN
        idx_silu = self.tab_idx["silu_and_others"]

        def load_tab(e, idx=idx_silu):
            return e.add_instruction(mybir.InstLoadActFuncSet(
                name=self.nc.get_next_instruction_name(),
                act_func_set_id=idx, ins=[], outs=[]))
        self.op("S", load_tab, serialize=True)
        sg = self.ap("AT", 0, OWN)
        sgb = self.bufs("AT", 0, OWN)
        uu = self.ap("AT", OWN, OWN)
        uub = self.bufs("AT", OWN, OWN)
        slot_i = 0
        for i in range(IC):
            self._kv_step(5)
            si = self._wslot(slot_i); slot_i += 1
            wap = self.ap("H", si, DC * 128)
            wb = self.bufs("H", si, DC * 128)
            self.dma("G", wap, self.d_gwT[i], writes=wb)
            for pc in range(self.NOP):
                bank = pc
                rec = []
                for d in range(DC):
                    t = self.mm(self.pbank(bank, self.OPW),
                                wap[:, d * 128:(d + 1) * 128],
                                self.xn_ap(d, pc * self.OPW, self.OPW),
                                reads=wb + self.xn_bufs(d, pc * self.OPW, self.OPW),
                                writes=[self.pb[bank]],
                                start=(d == 0), stop=(d == DC - 1),
                                inc=(d == DC - 1), record=rec)
                self.attach(t, rec)
            gsl = self.cf[:, self.o_gs + i:self.o_gs + i + 1]
            self.op("S", lambda e, o=sg, s=gsl: e.activation(
                o, self.psum[:, 0:OWN], AF.Silu, scale=s),
                reads=[self.pb[0], self.pb[1], self.cf_bufs], writes=sgb)
            si = self._wslot(slot_i); slot_i += 1
            wap2 = self.ap("H", si, DC * 128)
            wb2 = self.bufs("H", si, DC * 128)
            self.dma("G", wap2, self.d_uwT[i], writes=wb2)
            for pc in range(self.NOP):
                bank = 2 + pc
                rec = []
                for d in range(DC):
                    t = self.mm(self.pbank(bank, self.OPW),
                                wap2[:, d * 128:(d + 1) * 128],
                                self.xn_ap(d, pc * self.OPW, self.OPW),
                                reads=wb2 + self.xn_bufs(d, pc * self.OPW, self.OPW),
                                writes=[self.pb[bank]],
                                start=(d == 0), stop=(d == DC - 1),
                                inc=(d == DC - 1), record=rec)
                self.attach(t, rec)
            usl = self.cf[:, self.o_us + i:self.o_us + i + 1]
            self.op("V", lambda e, o=uu, s=usl: e.tensor_scalar(
                o, self.psum[:, 1024:1024 + OWN], s, None, AL.mult),
                reads=[self.pb[2], self.pb[3], self.cf_bufs], writes=uub)
            reg, off = self._hdn_slot(i)
            hap = self.ap(reg, off, OWN)
            hb = self.bufs(reg, off, OWN)
            self.op("V", lambda e, o=hap, a=sg, b=uu: e.tensor_tensor(
                o, a, b, AL.mult),
                reads=sgb + uub, writes=hb)
        self._kv_step(len(self._kv_units))
        if "hdn" in self.d_dbg:
            for i in range(IC):
                reg, off = self._hdn_slot(i)
                self.dma("Q", self.d_dbg["hdn"][i * 128:(i + 1) * 128, :],
                         self.ap(reg, off, OWN), reads=self.bufs(reg, off, OWN))

    def _ph4_down(self):
        DC, IC, OWN = self.DC, self.IC, self.OWN
        for hd in range(DC):
            si = (hd % 2) * IC * 128
            wap = self.ap("AT", si, IC * 128)
            wb = self.bufs("AT", si, IC * 128)
            self.dma("G", wap, self.d_dwT[hd], writes=wb)
            for pc in range(self.NOP):
                bank = (hd % 2) * 2 + pc
                rec = []
                for i in range(IC):
                    reg, off = self._hdn_slot(i)
                    t = self.mm(self.pbank(bank, self.OPW),
                                wap[:, i * 128:(i + 1) * 128],
                                self.ap(reg, off + pc * self.OPW, self.OPW),
                                reads=wb + self.bufs(reg, off + pc * self.OPW, self.OPW),
                                writes=[self.pb[bank]],
                                start=(i == 0), stop=(i == IC - 1),
                                inc=(i == IC - 1), record=rec)
                self.attach(t, rec)
            dsl = self.cf[:, self.o_ds + hd:self.o_ds + hd + 1]
            pso = (hd % 2) * 2 * 512
            fap = self.ap("ST", hd * OWN, OWN)
            fb2 = self.bufs("ST", hd * OWN, OWN)
            self.op("S", lambda e, o=fap, s=dsl, po=pso: e.activation(
                o, self.psum[:, po:po + OWN], AF.Silu, scale=s),
                reads=[self.pb[(hd % 2) * 2], self.pb[(hd % 2) * 2 + 1],
                       self.cf_bufs], writes=fb2)




    def _finish(self):
        blk = self.ctx.enter_context(self.nc.Block())
        progs = self.prog

        @blk.tensor
        def _(e):
            for f in progs["T"]:
                f(e)

        @blk.vector
        def _(e):
            for f in progs["V"]:
                f(e)

        @blk.scalar
        def _(e):
            for f in progs["S"]:
                f(e)

        @blk.sync
        def _(e):
            for f in progs["Q"]:
                f(e)

        @blk.gpsimd
        def _(e):
            for f in progs["G"]:
                f(e)
        self.ctx.close()


_FULL = dict(D=2048, TOK=2048, H=16, DH=64, INTER=5632, HG=4)
_NC_CACHE = {}


def build_nc(cfg):
    key = tuple(sorted((k, tuple(sorted(v)) if isinstance(v, (list, set, tuple)) else v)
                       for k, v in cfg.items()))
    if key not in _NC_CACHE:
        _NC_CACHE[key] = MK(cfg).nc
    return _NC_CACHE[key]


def kernel(**inputs):
    x = np.ascontiguousarray(np.asarray(inputs["L0_attn_x"], dtype=np.float32))
    B, S, D = x.shape
    wq = np.ascontiguousarray(np.asarray(inputs["L0_attn_Wq"], dtype=np.float32))
    wk = np.ascontiguousarray(np.asarray(inputs["L0_attn_Wk"], dtype=np.float32))
    wv = np.ascontiguousarray(np.asarray(inputs["L0_attn_Wv"], dtype=np.float32))
    wo = np.ascontiguousarray(np.asarray(inputs["L0_attn_Wo"], dtype=np.float32))
    gwT = np.ascontiguousarray(np.asarray(inputs["L0_ffn_gate_w"], dtype=np.float32).T)
    uwT = np.ascontiguousarray(np.asarray(inputs["L0_ffn_up_w"], dtype=np.float32).T)
    dwT = np.ascontiguousarray(np.asarray(inputs["L0_ffn_down_w"], dtype=np.float32).T)
    VW = min(512, wq.shape[1])
    com = dict(
        wq=tile_cols(wq, 128), wk=tile_cols(wk, 128), wv=tile_cols(wv, VW),
        wo=wo, gwT=tile_cols(gwT, 128), uwT=tile_cols(uwT, 128),
        dwT=tile_cols(dwT, 128),
        gs=np.ascontiguousarray(np.asarray(inputs["L0_ffn_gate_s"], dtype=np.float32)),
        us=np.ascontiguousarray(np.asarray(inputs["L0_ffn_up_s"], dtype=np.float32)),
        ds=np.ascontiguousarray(np.asarray(inputs["L0_ffn_down_s"], dtype=np.float32)),
        gnw=np.ascontiguousarray(np.asarray(inputs["L0_attn_gn_weight"], dtype=np.float32)),
        gnb=np.ascontiguousarray(np.asarray(inputs["L0_attn_gn_bias"], dtype=np.float32)),
        lq1=np.ascontiguousarray(np.asarray(inputs["L0_attn_lambda_q1"], dtype=np.float32)),
        lk1=np.ascontiguousarray(np.asarray(inputs["L0_attn_lambda_k1"], dtype=np.float32)),
        lq2=np.ascontiguousarray(np.asarray(inputs["L0_attn_lambda_q2"], dtype=np.float32)),
        lk2=np.ascontiguousarray(np.asarray(inputs["L0_attn_lambda_k2"], dtype=np.float32)),
    )
    nc = build_nc(_FULL)
    OWN = S // 2
    in_maps = []
    for c in range(8):
        bb, ph = c // 2, c % 2
        if ph == 0:
            xs = x[bb]
        else:
            xs = np.concatenate([x[bb, OWN:], x[bb, :OWN]], axis=0)
        m = dict(com)
        m["xT"] = np.ascontiguousarray(xs.T)
        in_maps.append(m)
    res = run_bass_kernel_spmd(nc, in_maps, core_ids=list(range(8)))
    out = np.empty((B, S, D), dtype=np.float32)
    for c in range(8):
        bb, ph = c // 2, c % 2
        out[bb, ph * OWN:(ph + 1) * OWN] = \
            np.ascontiguousarray(res.results[c]["outT"]).T
    return out


# version 28
# speedup vs baseline: 1.3066x; 1.0409x over previous
import math
import numpy as np
import ml_dtypes
from contextlib import ExitStack

import concourse.bass as bass
import concourse.mybir as mybir
from concourse.bass_utils import run_bass_kernel_spmd

F32 = mybir.dt.float32
BF16 = mybir.dt.bfloat16
AF = mybir.ActivationFunctionType
AL = mybir.AluOpType
AXX = mybir.AxisListType.X

GR = 512


def tile_cols(a, cw):
    R, W = a.shape
    return np.ascontiguousarray(
        a.reshape(R // 128, 128, W // cw, cw).transpose(2, 1, 0, 3)
    ).reshape(W // cw, 128, (R // 128) * cw)


class Buf:
    __slots__ = ("w", "r")

    def __init__(self):
        self.w = None
        self.r = []


class Region:
    def __init__(self, off, size):
        self.off = off
        self.size = size
        n = (size + GR - 1) // GR
        self.bufs = [Buf() for _ in range(n)]

    def b(self, off, n):
        lo = off // GR
        hi = (off + n + GR - 1) // GR
        return self.bufs[lo:hi]


class MK:
    ENGS = ("T", "V", "S", "Q", "G", "P", "Q2", "G2")
    STEP = {"T": 1, "V": 1, "S": 1, "Q": 16, "G": 16, "P": 1, "Q2": 16, "G2": 16}
    QCHAN = {"Q": ("Q", "Q2"), "G": ("G", "G2")}

    def __init__(self, cfg):
        self.cfg = cfg
        c = cfg
        self.D = c["D"]; self.TOK = c["TOK"]; self.H = c["H"]
        self.DH = c["DH"]; self.INTER = c["INTER"]; self.HG = c["HG"]
        self.DC = self.D // 128
        self.KT = self.TOK // 128
        assert self.KT % 2 == 0
        self.OWN = self.TOK // 2
        self.OC = self.OWN // 128
        self.QH = min(512, self.OWN)
        self.NQH = self.OWN // self.QH
        self.IC = self.INTER // 128
        self.NG = self.H // self.HG
        self.PW = min(512, self.OWN)
        self.NP = self.TOK // self.PW
        self.OPW = min(512, self.OWN)
        self.NOP = self.OWN // self.OPW
        self.dbg = set(c.get("dbg", ()))

        self.nc = bass.Bass("TRN2", target_bir_lowering=False)
        self.ctx = ExitStack()
        self.prog = {e: [] for e in ("T", "V", "S", "Q", "G")}
        self._flip = {"Q": 0, "G": 0}
        self.cnt = {e: 0 for e in self.ENGS}
        self.maxw = {e: {} for e in self.ENGS}
        self._wpref = {}
        self._mk_dram()
        self._mk_sbuf()
        self._mk_sems()
        self._build()
        self._finish()


    def _mk_sems(self):
        self.sem = {}
        for e in self.ENGS:
            self.sem[e] = self.ctx.enter_context(self.nc.semaphore("sem" + e))
        self.sem["C"] = self.ctx.enter_context(self.nc.semaphore("semC"))
        self.k_snd_buf = Buf()
        self.v_snd_buf = Buf()
        self.kv_rcv_buf = Buf()

    def op(self, eng, fn, reads=(), writes=(), inc=True, serialize=False,
           record=None):
        toks = []
        if serialize and self.cnt[eng] > 0:
            toks.append((eng, self.cnt[eng]))
        for b in reads:
            if b.w is not None:
                toks.append(b.w)
        for b in writes:
            if b.w is not None:
                toks.append(b.w)
            toks.extend(b.r)
        need = {}
        for (e2, v) in toks:
            if eng == "T" and e2 == "T":
                continue
            if v > need.get(e2, -1):
                need[e2] = v
        if eng in ("Q", "G", "Q2", "G2") and self.cnt[eng] > need.get(eng, -1):
            need[eng] = self.cnt[eng]
        waits = []
        mw = self.maxw[eng]
        for e2, v in need.items():
            if v > mw.get(e2, 0):
                waits.append((e2, v))
                mw[e2] = v
        tok = None
        if inc:
            self.cnt[eng] += self.STEP[eng]
            tok = (eng, self.cnt[eng])
        sems = self.sem
        step = self.STEP[eng]

        def thunk(e, fn=fn, waits=waits, tok=tok, eng=eng):
            for (e2, v) in waits:
                e.wait_ge(sems[e2], v)
            ins = fn(e)
            if tok is not None:
                ins.then_inc(sems[eng], step)

        stream = {"P": "G", "Q2": "Q", "G2": "G"}.get(eng, eng)
        self.prog[stream].append(thunk)
        if tok is not None:
            for b in reads:
                b.r.append(tok)
            for b in writes:
                b.w = tok
                b.r = []
        elif record is not None:
            record.extend(reads)
            record.extend(writes)
        return tok

    def attach(self, tok, bufs):
        for b in bufs:
            b.r.append(tok)

    def dma(self, eng, out_ap, in_ap, reads=(), writes=(), slow=False):
        if eng in self._flip:
            ch = self.QCHAN[eng][self._flip[eng]]
            self._flip[eng] ^= 1
            eng = ch
        if slow:
            return self.op(eng, lambda e: e.dma_start(
                out=out_ap, in_=in_ap, allow_slow_non_contiguous=True),
                reads=reads, writes=writes)
        return self.op(eng, lambda e: e.dma_start(out=out_ap, in_=in_ap),
                       reads=reads, writes=writes)

    def mm(self, out_ap, lhsT_ap, rhs_ap, reads, writes, start, stop,
           inc=True, record=None):
        return self.op("T", lambda e: e.matmul(out_ap, lhsT_ap, rhs_ap,
                                               start=start, stop=stop),
                       reads=reads, writes=writes, inc=inc, record=record)


    def _mk_dram(self):
        nc = self.nc
        D, TOK, OWN, INTER = self.D, self.TOK, self.OWN, self.INTER
        inner = self.H * 2 * self.DH
        dt = nc.dram_tensor
        self.d_xT = dt("xT", [D, TOK], F32, kind="ExternalInput").ap()
        DC, IC = self.DC, self.IC
        NIC = self.H
        VW = min(512, inner)
        self.VW = VW
        self.d_wq = dt("wq", [NIC, 128, DC * 128], F32, kind="ExternalInput").ap()
        self.d_wk = dt("wk", [NIC, 128, DC * 128], F32, kind="ExternalInput").ap()
        self.d_wv = dt("wv", [inner // VW, 128, DC * VW], F32,
                       kind="ExternalInput").ap()
        self.d_wo = dt("wo", [inner, D], F32, kind="ExternalInput").ap()
        self.d_gwT = dt("gwT", [IC, 128, DC * 128], F32, kind="ExternalInput").ap()
        self.d_uwT = dt("uwT", [IC, 128, DC * 128], F32, kind="ExternalInput").ap()
        self.d_dwT = dt("dwT", [DC, 128, IC * 128], F32, kind="ExternalInput").ap()
        self.d_gs = dt("gs", [INTER], F32, kind="ExternalInput").ap()
        self.d_us = dt("us", [INTER], F32, kind="ExternalInput").ap()
        self.d_ds = dt("ds", [D], F32, kind="ExternalInput").ap()
        self.d_gnw = dt("gnw", [inner], F32, kind="ExternalInput").ap()
        self.d_gnb = dt("gnb", [inner], F32, kind="ExternalInput").ap()
        self.d_lq1 = dt("lq1", [self.DH], F32, kind="ExternalInput").ap()
        self.d_lk1 = dt("lk1", [self.DH], F32, kind="ExternalInput").ap()
        self.d_lq2 = dt("lq2", [self.DH], F32, kind="ExternalInput").ap()
        self.d_lk2 = dt("lk2", [self.DH], F32, kind="ExternalInput").ap()
        self.d_out = dt("outT", [D, OWN], F32, kind="ExternalOutput").ap()
        W0 = self.H * 2 * self.DH
        self.W0 = W0
        self.d_k_snd = dt("k_snd", [W0, OWN], BF16)
        self.d_k_rcv = dt("k_rcv", [2 * W0, OWN], BF16)
        self.d_v_snd = dt("v_snd", [OWN, W0], BF16)
        self.d_v_rcv = dt("v_rcv", [2 * OWN, W0], BF16)
        self.d_dbg = {}
        for name, shp in (("xn", [D, OWN]), ("aT", [inner, OWN]),
                          ("sT", [D, OWN]), ("hdn", [INTER, OWN]),
                          ("sgT", [D, OWN])):
            if name in self.dbg:
                self.d_dbg[name] = dt("dbg_" + name, shp, BF16,
                                      kind="ExternalOutput").ap()

    def _mk_sbuf(self):
        nc = self.nc
        D, TOK, OWN = self.D, self.TOK, self.OWN
        DC, KT, QH, IC, HG = self.DC, self.KT, self.QH, self.IC, self.HG
        sizes = {}
        sizes["A"] = DC * OWN
        sizes["B"] = max(16 * OWN, 8 * D)
        sizes["C"] = max(KT * QH, DC * HG * 128)
        sizes["E"] = max(HG * TOK, 4 * D)
        sizes["F"] = KT * HG * 128
        sizes["G"] = HG * OWN + 2 * QH
        sizes["H"] = max(2 * DC * 128, 8 * OWN // 2)
        sizes["AT"] = max(self.H * OWN, 2 * IC * 128 + OWN, 2 * OWN)
        sizes["ST"] = max(DC * OWN, 28 * self.PW)
        sizes["SC"] = 8 * QH
        total = 0
        offs = {}
        for k in ("A", "B", "C", "E", "F", "G", "H", "AT", "ST", "SC"):
            offs[k] = total
            total += sizes[k]
        self.pool_el = total
        self.pool = self.ctx.enter_context(
            nc.sbuf_tensor("pool", [128, total], BF16))
        self.R = {k: Region(offs[k], sizes[k]) for k in offs}

        self.cb = self.ctx.enter_context(
            nc.sbuf_tensor("cb", [128, 3 * 128], BF16))
        self.cb_bufs = [Buf() for _ in range(3)]
        self.cf = self.ctx.enter_context(
            nc.sbuf_tensor("cf", [128, 8 + IC * 2 + DC + self.H * 2], F32))
        self.cf_bufs = Buf()
        self.row = self.ctx.enter_context(
            nc.sbuf_tensor("rowscr", [1, 8 * max(64, self.DH)], F32))
        self.row_buf = Buf()
        self.ones_row = self.ctx.enter_context(
            nc.sbuf_tensor("ones_row", [1, 128], F32))
        self.ones_row_buf = Buf()
        self.lam128 = self.ctx.enter_context(
            nc.sbuf_tensor("lam128", [128, 1], F32))
        self.lam128_buf = Buf()
        self.cm8 = self.ctx.enter_context(
            nc.sbuf_tensor("cm8", [128, 1], F32))
        self.cm8_buf = Buf()
        self.ceps = self.ctx.enter_context(
            nc.sbuf_tensor("ceps", [128, 1], F32))
        self.ceps_buf = Buf()
        self.lnlam = self.ctx.enter_context(
            nc.sbuf_tensor("lnlam", [128, 1], F32))
        self.lnlam_buf = Buf()
        self.psum = self.ctx.enter_context(
            nc.psum_tensor("ps", [128, 4096], F32))
        self.pb = [Buf() for _ in range(8)]

    def ap(self, reg, off, n):
        r = self.R[reg]
        assert off + n <= r.size, (reg, off, n, r.size)
        return self.pool[:, r.off + off: r.off + off + n]

    def ap32(self, reg, off, n32):
        r = self.R[reg]
        assert off + 2 * n32 <= r.size, (reg, off, n32, r.size)
        return self.pool[:, r.off + off: r.off + off + 2 * n32].bitcast(F32)

    def bufs(self, reg, off, n):
        return self.R[reg].b(off, n)

    def pbank(self, b, width=None, off=0):
        w = 512 if width is None else width
        return self.psum[:, b * 512 + off: b * 512 + off + w]


    def _build(self):
        stop = self.cfg.get("stop_after", "")
        phases = [("consts", self._consts), ("ph0", self._ph0_layernorm),
                  ("ph1", self._ph1_lam), ("ph1b", self._ph1b_kv),
                  ("ph3", self._ph3_gateup), ("ph4", self._ph4_down),
                  ("ph2", self._ph2_attention), ("ph2b", self._ph2b_wo)]
        for name, fn in phases:
            fn()
            if name == stop:
                break
        sems = self.sem
        for ch in ("Q", "Q2"):
            v = self.cnt[ch]
            if v:
                self.prog["Q"].append(lambda e, c=ch, vv=v: e.wait_ge(sems[c], vv))
        for ch in ("G", "G2"):
            v = self.cnt[ch]
            if v:
                self.prog["G"].append(lambda e, c=ch, vv=v: e.wait_ge(sems[c], vv))

    def _consts(self):
        nc = self.nc
        H, DC, IC = self.H, self.DC, self.IC
        from concourse.hw_specs import get_activation_tables
        tabs = list(get_activation_tables(self.nc.m.arch).keys())
        self.tab_idx = {name: i for i, name in enumerate(tabs)}

        def load_tab(e, idx):
            return e.add_instruction(mybir.InstLoadActFuncSet(
                name=self.nc.get_next_instruction_name(),
                act_func_set_id=idx, ins=[], outs=[]))
        idx_nle = self.tab_idx["natural_log_exp_and_others"]
        self.op("S", lambda e: load_tab(e, idx_nle), serialize=True)

        cb, cf = self.cb, self.cf
        b = self.cb_bufs
        self.op("V", lambda e: e.memset(cb[:, 0:128], 1.0), writes=[b[0]])
        self.op("V", lambda e: e.memset(cb[:, 128:256], 1.0 / (2 * self.DH)),
                writes=[b[1]])
        self.op("V", lambda e: e.memset(cb[:, 256:384], 1.0 / self.D),
                writes=[b[2]])
        self.op("V", lambda e: e.memset(self.ones_row[:], 1.0),
                writes=[self.ones_row_buf])
        self.op("V", lambda e: e.memset(self.cm8[:], -8.0),
                writes=[self.cm8_buf])
        self.op("V", lambda e: e.memset(self.ceps[:], 1e-5),
                writes=[self.ceps_buf])
        self.o_gs = 0
        self.o_us = IC
        self.o_ds = 2 * IC
        self.o_gw = 2 * IC + DC
        self.o_gb = 2 * IC + DC + H
        cfb = self.cf_bufs
        self.dma("Q", cf[:, self.o_gs:self.o_gs + IC],
                 self.d_gs.rearrange("(c p) -> p c", p=128), writes=[cfb], slow=True)
        self.dma("Q", cf[:, self.o_us:self.o_us + IC],
                 self.d_us.rearrange("(c p) -> p c", p=128), writes=[cfb], slow=True)
        self.dma("Q", cf[:, self.o_ds:self.o_ds + DC],
                 self.d_ds.rearrange("(c p) -> p c", p=128), writes=[cfb], slow=True)
        self.dma("Q", cf[:, self.o_gw:self.o_gw + H],
                 self.d_gnw.rearrange("(c p) -> p c", p=128), writes=[cfb], slow=True)
        self.dma("Q", cf[:, self.o_gb:self.o_gb + H],
                 self.d_gnb.rearrange("(c p) -> p c", p=128), writes=[cfb], slow=True)
        self.op("V", lambda e: e.tensor_scalar_mul(
            cf[:, self.o_gw:self.o_gw + 2 * H],
            cf[:, self.o_gw:self.o_gw + 2 * H], 0.2),
            reads=[cfb], writes=[cfb])

    def xn_ap(self, c, t0, n):
        OWN = self.OWN
        if t0 >= OWN:
            return self.ap("B", c * (self.TOK - OWN) + (t0 - OWN), n)
        assert t0 + n <= OWN
        return self.ap("A", c * OWN + t0, n)

    def xn_bufs(self, c, t0, n):
        OWN = self.OWN
        if t0 >= OWN:
            return self.bufs("B", c * (self.TOK - OWN) + (t0 - OWN), n)
        return self.bufs("A", c * OWN + t0, n)

    def _ph0_layernorm(self):
        nc = self.nc
        D, TOK, OWN, DC, PW, NP = self.D, self.TOK, self.OWN, self.DC, self.PW, self.NP
        NP = self.OWN // PW
        xT3 = self.d_xT.rearrange("(c p) t -> p c t", p=128)
        for p in range(NP):
            t0 = p * PW
            base = self.R["A"].off
            out3 = self.pool[:, base: base + DC * OWN].rearrange(
                "p (c f) -> p c f", c=DC)[:, :, t0:t0 + PW]
            wb = []
            for c in range(DC):
                wb += self.xn_bufs(c, t0, PW)
            self.dma("G", out3, xT3[:, :, t0:t0 + PW], writes=wb)
        onesLN = self.cb[:, 256:384]
        ST = "ST"
        def scr(p):
            base = (p % 2) * 12 * PW
            muS = self.ap32(ST, base, PW)
            var = self.ap32(ST, base + 2 * PW, PW)
            rstd = self.ap32(ST, base + 4 * PW, PW)
            mumix = self.ap32(ST, base + 6 * PW, PW)
            return (muS, var, rstd, mumix,
                    self.bufs(ST, base, 2 * PW),
                    self.bufs(ST, base + 2 * PW, 2 * PW),
                    self.bufs(ST, base + 4 * PW, 2 * PW),
                    self.bufs(ST, base + 6 * PW, 2 * PW))
        for p in range(NP):
            muS, var, rstd, mumix, bmu, bvar, brstd, bmix = scr(p)
            t0 = p * PW
            bmu_k = (p % 2) * 2
            bs2_k = bmu_k + 1
            rec0, rec1 = [], []
            for c in range(DC):
                t = self.mm(self.pbank(bmu_k, PW), onesLN, self.xn_ap(c, t0, PW),
                            reads=[self.cb_bufs[2]] + self.xn_bufs(c, t0, PW),
                            writes=[self.pb[bmu_k]], start=(c == 0),
                            stop=(c == DC - 1), inc=(c == DC - 1), record=rec0)
                if t is not None:
                    self.attach(t, rec0)
            for c in range(DC):
                sq = self.ap(ST, 24 * PW + ((p * DC + c) % 2) * PW, PW)
                sqb = self.bufs(ST, 24 * PW + ((p * DC + c) % 2) * PW, PW)
                self.op("S", lambda e, i=self.xn_ap(c, t0, PW), o=sq:
                        e.activation(o, i, AF.Square),
                        reads=self.xn_bufs(c, t0, PW), writes=sqb)
                self.mm(self.pbank(bs2_k, PW), onesLN, sq,
                        reads=[self.cb_bufs[2]] + sqb,
                        writes=[self.pb[bs2_k]], start=(c == 0),
                        stop=(c == DC - 1))
            self.op("V", lambda e, o=muS, b=bmu_k: e.tensor_copy(o, self.pbank(b, PW)),
                    reads=[self.pb[bmu_k]], writes=bmu)
            self.op("V", lambda e, o=var, a=muS: e.tensor_tensor(o, a, a, AL.mult),
                    reads=bmu, writes=bvar)
            self.op("V", lambda e, o=var, a=var, b=bs2_k: e.tensor_tensor(
                o, self.pbank(b, PW), a, AL.subtract),
                reads=[self.pb[bs2_k]] + bvar, writes=bvar)
            self.op("S", lambda e, o=rstd, a=var: e.activation(
                o, a, AF.Ln, bias=self.ceps[:, 0:1]),
                reads=bvar + [self.ceps_buf], writes=brstd)
            self.op("S", lambda e, o=rstd, a=rstd: e.activation(
                o, a, AF.Exp, scale=-0.5),
                reads=brstd, writes=brstd)
            self.op("V", lambda e, o=mumix, a=muS, b2=rstd: e.scalar_tensor_tensor(
                o, a, -1.0, b2, AL.mult, AL.mult),
                reads=bmu + brstd, writes=bmix)
            for c in range(DC):
                eng = "V" if c % 2 == 0 else "P"
                t = self.ap(ST, 26 * PW + (c % 2) * PW, PW)
                tb = self.bufs(ST, 26 * PW + (c % 2) * PW, PW)
                self.op(eng, lambda e, o=t, a=self.xn_ap(c, t0, PW), b2=rstd:
                        e.tensor_tensor(o, a, b2, AL.mult),
                        reads=self.xn_bufs(c, t0, PW) + brstd, writes=tb)
                self.op(eng, lambda e, o=self.xn_ap(c, t0, PW), a=t, b2=mumix:
                        e.tensor_tensor(o, a, b2, AL.add),
                        reads=tb + bmix, writes=self.xn_bufs(c, t0, PW))
        if "xn" in self.d_dbg:
            for c in range(DC):
                self.dma("Q", self.d_dbg["xn"][c * 128:(c + 1) * 128, :],
                         self.xn_ap(c, 0, OWN), reads=self.xn_bufs(c, 0, OWN))

    def _ph1_lam(self):
        row, rb = self.row, self.row_buf
        DH = self.DH
        for i, d in enumerate((self.d_lq1, self.d_lk1, self.d_lq2, self.d_lk2)):
            self.dma("Q", row[0:1, i * DH:(i + 1) * DH],
                     d.rearrange("(o f) -> o f", o=1), writes=[rb])
        p1 = row[0:1, 4 * DH:5 * DH]
        self.op("V", lambda e: e.tensor_tensor(
            p1, row[0:1, 0:DH], row[0:1, DH:2 * DH], AL.mult),
            reads=[rb], writes=[rb])
        s1 = row[0:1, 5 * DH:5 * DH + 1]
        self.op("V", lambda e: e.reduce_sum(s1, p1, axis=AXX),
                reads=[rb], writes=[rb])
        self.op("V", lambda e: e.tensor_tensor(
            p1, row[0:1, 2 * DH:3 * DH], row[0:1, 3 * DH:4 * DH], AL.mult),
            reads=[rb], writes=[rb])
        s2 = row[0:1, 5 * DH + 1:5 * DH + 2]
        self.op("V", lambda e: e.reduce_sum(s2, p1, axis=AXX),
                reads=[rb], writes=[rb])
        e1 = row[0:1, 5 * DH + 2:5 * DH + 3]
        e2 = row[0:1, 5 * DH + 3:5 * DH + 4]
        self.op("S", lambda e: e.activation(e1, s1, AF.Exp), reads=[rb], writes=[rb])
        self.op("S", lambda e: e.activation(e2, s2, AF.Exp), reads=[rb], writes=[rb])
        lam = row[0:1, 5 * DH + 4:5 * DH + 5]
        self.op("V", lambda e: e.tensor_tensor(lam, e1, e2, AL.subtract),
                reads=[rb], writes=[rb])
        self.op("V", lambda e: e.tensor_scalar_add(lam, lam, 0.8),
                reads=[rb], writes=[rb])
        self.mm(self.pbank(2, 1), self.ones_row[:], lam,
                reads=[self.ones_row_buf, rb], writes=[self.pb[2]],
                start=True, stop=True)
        self.op("V", lambda e: e.tensor_copy(self.lam128[:], self.pbank(2, 1)),
                reads=[self.pb[2]], writes=[self.lam128_buf])




    def _wslot(self, idx):
        base = (idx % 2) * self.DC * 128
        return base

    def _ph1b_kv(self):
        DC, OWN, OC = self.DC, self.OWN, self.OC
        NIC = self.H
        ST = "ST"
        kvK = self.d_k_snd.ap()
        units = []

        def k_unit(icc):
            def emit():
                at_off = 2 * OWN + (icc % 2) * DC * 128
                wap = self.ap("AT", at_off, DC * 128)
                wb = self.bufs("AT", at_off, DC * 128)
                self.dma("G", wap, self.d_wk[icc], writes=wb)
                sslot = (icc % 2) * OWN
                for pc in range(self.NOP):
                    t0 = pc * self.OPW
                    bank = 4 + (icc % 2) * 2 + pc
                    rec = []
                    for d in range(DC):
                        t = self.mm(self.pbank(bank, self.OPW),
                                    wap[:, d * 128:(d + 1) * 128],
                                    self.xn_ap(d, t0, self.OPW),
                                    reads=wb + self.xn_bufs(d, t0, self.OPW),
                                    writes=[self.pb[bank]],
                                    start=(d == 0), stop=(d == DC - 1),
                                    inc=(d == DC - 1), record=rec)
                    self.attach(t, rec)
                    sg = self.ap(ST, sslot + t0, self.OPW)
                    sgb = self.bufs(ST, sslot + t0, self.OPW)
                    self.op("V", lambda e, o=sg, b=bank: e.tensor_copy(
                        o, self.pbank(b, self.OPW)),
                        reads=[self.pb[bank]], writes=sgb)
                self.dma("Q", kvK[icc * 128:(icc + 1) * 128, :],
                         self.ap(ST, sslot, OWN),
                         reads=self.bufs(ST, sslot, OWN),
                         writes=[self.k_snd_buf])
            return emit

        VW = self.VW
        NVG = self.W0 // VW

        def v_unit(vg, t, uidx):
            def emit():
                wvb = self.bufs("C", 0, DC * VW)
                if t == 0:
                    self.dma("G", self.ap("C", 0, DC * VW),
                             self.d_wv[vg], writes=wvb)
                bank = 4 + uidx % 4
                rec = []
                for d in range(DC):
                    tk = self.mm(self.pbank(bank, VW),
                                 self.xn_ap(d, t * 128, 128),
                                 self.ap("C", d * VW, VW),
                                 reads=self.xn_bufs(d, t * 128, 128) + wvb,
                                 writes=[self.pb[bank]],
                                 start=(d == 0), stop=(d == DC - 1),
                                 inc=(d == DC - 1), record=rec)
                self.attach(tk, rec)
                sg = self.ap(ST, 2 * OWN + (uidx % 2) * VW, VW)
                sgb = self.bufs(ST, 2 * OWN + (uidx % 2) * VW, VW)
                self.op("V", lambda e, o=sg, b=bank, w=VW: e.tensor_copy(
                    o, self.pbank(b, w)),
                    reads=[self.pb[bank]], writes=sgb)
                self.dma("Q", self.d_v_snd.ap()[
                    t * 128:(t + 1) * 128, vg * VW:(vg + 1) * VW],
                    sg, reads=sgb, writes=[self.v_snd_buf])
            return emit

        for icc in range(NIC):
            units.append(k_unit(icc))
        ui = 0
        for vg in range(NVG):
            for t in range(OC):
                units.append(v_unit(vg, t, ui))
                ui += 1
        rg = self.cfg.get("replica_groups")
        if rg is None:
            rg = [[0, 1], [2, 3], [4, 5], [6, 7]]
        semC = self.sem["C"]
        ks, kr = self.d_k_snd.ap().opt(), self.d_k_rcv.ap().opt()
        vs, vr = self.d_v_snd.ap().opt(), self.d_v_rcv.ap().opt()

        def k_trig():
            self.op("G", lambda e: e.collective_compute(
                "AllGather", AL.bypass, replica_groups=rg,
                ins=[ks], outs=[kr]).then_inc(semC, 1),
                reads=[self.k_snd_buf], inc=False)

        def v_trig():
            self.op("G", lambda e: e.collective_compute(
                "AllGather", AL.bypass, replica_groups=rg,
                ins=[vs], outs=[vr]).then_inc(semC, 1),
                reads=[self.v_snd_buf], inc=False)
            self.kv_rcv_buf.w = ("C", 2)
        self._kv_units = units
        self._kv_ktrig = (NIC, k_trig)
        self._kv_vtrig = (len(units), v_trig)
        self._kv_consumed = 0

    def _kv_step(self, n=2):
        for _ in range(n):
            if self._kv_consumed < len(self._kv_units):
                self._kv_units[self._kv_consumed]()
                self._kv_consumed += 1
                if self._kv_consumed == self._kv_ktrig[0]:
                    self._kv_ktrig[1]()
                if self._kv_consumed == self._kv_vtrig[0]:
                    self._kv_vtrig[1]()

    def _load_wcol(self, dram, chunk, slot_idx):
        si = self._wslot(slot_idx)
        wap = self.ap("H", si, self.DC * 128)
        wb = self.bufs("H", si, self.DC * 128)
        self.dma("G", wap, dram[chunk], writes=wb)
        return (wap, wb)

    def _ph2_attention(self):
        DC, HG, NG, OWN, TOK, QH, KT = (self.DC, self.HG, self.NG, self.OWN,
                                        self.TOK, self.QH, self.KT)
        DH = self.DH
        slot_i = 0
        self._pend_tail = None
        idx_nle = self.tab_idx["natural_log_exp_and_others"]

        def load_tab(e, idx=idx_nle):
            return e.add_instruction(mybir.InstLoadActFuncSet(
                name=self.nc.get_next_instruction_name(),
                act_func_set_id=idx, ins=[], outs=[]))
        self.op("S", load_tab, serialize=True)
        for z in range(2):
            apz = self.ap("G", HG * OWN + z * QH, QH)
            self.op("V", lambda e, a=apz: e.memset(a, 0.0),
                    writes=self.bufs("G", HG * OWN + z * QH, QH))
        for i in range(self.H):
            reg, off = self._wo_slot(i)
            if reg == "B":
                self.dma("G", self.ap(reg, off, self.D),
                         self.d_wo[i * 128:(i + 1) * 128, :],
                         writes=self.bufs(reg, off, self.D))
        NGrun = self.cfg.get("ph2_groups", NG)
        noheads = self.cfg.get("ph2_noheads", False)
        for g in range(NGrun):
            col0 = g * HG * 128
            for cc in range(HG):
                wap, wb = self._load_wcol(self.d_wq, g * HG + cc, slot_i)
                slot_i += 1
                for pc in range(self.NOP):
                    t0 = pc * self.OPW
                    bank = pc % 2
                    rec = []
                    for d in range(DC):
                        t = self.mm(self.pbank(bank, self.OPW),
                                    wap[:, d * 128:(d + 1) * 128],
                                    self.xn_ap(d, t0, self.OPW),
                                    reads=wb + self.xn_bufs(d, t0, self.OPW),
                                    writes=[self.pb[bank]],
                                    start=(d == 0), stop=(d == DC - 1),
                                    inc=(d == DC - 1), record=rec)
                    self.attach(t, rec)
                    q1 = self.ap("G", cc * OWN + t0, self.OPW)
                    qb1 = self.bufs("G", cc * OWN + t0, self.OPW)
                    self.op("V", lambda e, o=q1, b=bank: e.tensor_copy(
                        o, self.pbank(b, self.OPW)),
                        reads=[self.pb[bank]], writes=qb1)
            gw = HG * 128
            W0 = self.W0
            kvb = [self.kv_rcv_buf]
            for half in range(2):
                kview = self.d_k_rcv.ap()[half * W0:(half + 1) * W0, :]
                eb = []
                for cc in range(HG):
                    eb += self.bufs("E", cc * TOK + half * OWN, OWN)
                eout = self.pool[:, self.R["E"].off:
                                 self.R["E"].off + HG * TOK].rearrange(
                    "p (c t) -> p c t", c=HG)[:, :, half * OWN:(half + 1) * OWN]
                self.dma("Q", eout,
                         kview[g * gw:(g + 1) * gw, :].rearrange(
                             "(c p) t -> p c t", p=128),
                         reads=kvb, writes=eb)
                vview = self.d_v_rcv.ap()[
                    half * OWN:(half + 1) * OWN, g * gw:(g + 1) * gw]
                fb = []
                for t in range(KT // 2):
                    fb += self.bufs("F", (half * KT // 2 + t) * gw, gw)
                fout = self.pool[:, self.R["F"].off:
                                 self.R["F"].off + KT * gw].rearrange(
                    "p (t f) -> p t f", t=KT)[:, half * KT // 2:(half + 1) * KT // 2, :]
                self.dma("Q", fout,
                         vview.rearrange("(t p) f -> p t f", p=128),
                         reads=kvb, writes=fb)
            for l in range(HG):
                if noheads:
                    break
                head = g * HG + l
                for qh in range(self.NQH):
                    nj = None
                    if qh + 1 < self.NQH:
                        nj = (l, (qh + 1) * self.QH)
                    elif l + 1 < HG:
                        nj = (l + 1, 0)
                    self._pend_tail = self._attend(head, l, qh,
                                                   flush=self._pend_tail,
                                                   next_job=nj)
        if self._pend_tail is not None:
            self._pend_tail()
            self._pend_tail = None
        if "aT" in self.d_dbg:
            for hh in range(self.H):
                self.dma("Q", self.d_dbg["aT"][hh * 128:(hh + 1) * 128, :],
                         self.ap("AT", hh * OWN, OWN),
                         reads=self.bufs("AT", hh * OWN, OWN))

    def _emit_qz(self, l, q0):
        QH, DH, OWN = self.QH, self.DH, self.OWN
        aps, bs = [], []
        for comp in range(2):
            apz = self.ap("G", self.HG * OWN + comp * QH, QH)
            bz = self.bufs("G", self.HG * OWN + comp * QH, QH)
            rows = slice(0, DH) if comp == 0 else slice(DH, 128)
            qp = self.ap("G", l * OWN + q0, QH)
            qpb = self.bufs("G", l * OWN + q0, QH)
            self.op("V", lambda e, o=apz, a=qp, r=rows: e.tensor_copy(
                o[r, :], a[r, :]),
                reads=qpb, writes=bz)
            aps.append(apz)
            bs.append(bz)
        self._qz_state = (l, q0, aps, bs)
        return aps, bs

    def _attend(self, head, l, qh, flush=None, next_job=None):
        QH, KT, OWN, TOK = self.QH, self.KT, self.OWN, self.TOK
        DH = self.DH
        q0 = qh * QH
        ones128 = self.cb[:, 0:128]
        onesGN = self.cb[:, 128:256]
        SA, SB, D_, O1, O2 = 0, 2, 4, 5, 6
        kap_all = lambda kt: self.ap("E", l * TOK + kt * 128, 128)
        kb_all = lambda kt: self.bufs("E", l * TOK + kt * 128, 128)
        r_scr = [self.ap32("SC", 0, QH), self.ap32("SC", 2 * QH, QH)]
        r_bufs = [self.bufs("SC", 0, 2 * QH), self.bufs("SC", 2 * QH, 2 * QH)]

        crec = [[], []]
        prec = [[], []]

        st = getattr(self, "_qz_state", None)
        if st is not None and st[0] == l and st[1] == q0:
            qz_ap, qz_b = st[2], st[3]
        else:
            qz_ap, qz_b = self._emit_qz(l, q0)

        def cspv(j, comp):
            dbank = D_ if comp == 0 else 7
            obank = O1 if comp == 0 else O2
            for z in range(2):
                kt = 2 * j + z
                last = (kt == KT - 1)
                et = self.ap("C", kt * QH, QH)
                etb = self.bufs("C", kt * QH, QH)
                t = self.mm(self.pbank(dbank, QH), ones128, et,
                            reads=[self.cb_bufs[0]] + etb,
                            writes=[self.pb[dbank]],
                            start=(kt == 0), stop=last,
                            inc=last, record=crec[comp])
                if t is not None:
                    self.attach(t, crec[comp])
                vt = self.ap("F", kt * self.HG * 128 + l * 128, 128)
                vtb = self.bufs("F", kt * self.HG * 128 + l * 128, 128)
                t = self.mm(self.pbank(obank, QH), vt, et,
                            reads=vtb + etb,
                            writes=[self.pb[obank]],
                            start=(kt == 0), stop=last,
                            inc=last, record=prec[comp])
                if t is not None:
                    self.attach(t, prec[comp])

        npair = KT // 2

        def unit_flush(u):
            comp, j = u
            cspv(j, comp)
            if j == npair - 1:
                db = D_ if comp == 0 else 7
                self.op("S", lambda e, o=r_scr[comp], b=db: e.activation(
                    o, self.pbank(b, QH), AF.Ln),
                    reads=[self.pb[db]], writes=r_bufs[comp])
                self.op("S", lambda e, o=r_scr[comp]: e.activation(
                    o, o, AF.Exp, scale=-1.0),
                    reads=r_bufs[comp], writes=r_bufs[comp])

        pend = []
        unit_no = 0
        for comp in range(2):
            qz = qz_ap[comp]
            qzb = qz_b[comp]
            for j in range(npair):
                pair = SA if (j % 2 == 0) else SB
                for z in range(2):
                    kt = 2 * j + z
                    off = pair * 512 + z * QH
                    sap = self.psum[:, off: off + QH]
                    self.mm(sap, kap_all(kt), qz,
                            reads=kb_all(kt) + qzb,
                            writes=[self.pb[off // 512]],
                            start=True, stop=True)
                eap = self.ap("C", 2 * j * QH, 2 * QH)
                eb = self.bufs("C", 2 * j * QH, 2 * QH)
                self.op("S", lambda e, o=eap, p=pair: e.activation(
                    o, self.psum[:, p * 512: p * 512 + 2 * QH],
                    AF.Exp, scale=1.0 / math.sqrt(DH), bias=self.cm8[:, 0:1]),
                    reads=[self.pb[pair], self.pb[pair + 1],
                           self.pb[(pair * 512 + QH) // 512], self.cm8_buf],
                    writes=eb)
                lag = min(2, npair - 1)
                if unit_no == min(7, npair - 1) and flush is not None:
                    flush()
                unit_no += 1
                pend.append((comp, j))
                while len(pend) > lag:
                    unit_flush(pend.pop(0))
        if next_job is not None:
            self._emit_qz(*next_job)
        for u in pend:
            unit_flush(u)
        gnA = self.ap32("SC", 4 * QH, QH)
        gnB = self.ap32("SC", 6 * QH, QH)
        m2 = self.ap32("SC", 0, QH)
        bA = self.bufs("SC", 4 * QH, 2 * QH)
        bB = self.bufs("SC", 6 * QH, 2 * QH)
        bM = self.bufs("SC", 0, 2 * QH)
        self.op("V", lambda e, o=r_scr[1]: e.tensor_scalar(
            o, o, self.lam128[:, 0:1], None, AL.mult),
            reads=r_bufs[1] + [self.lam128_buf], writes=r_bufs[1])
        self.op("V", lambda e, o=gnA, r=r_scr[0]: e.tensor_tensor(
            o, self.pbank(5, QH), r, AL.mult),
            reads=[self.pb[5]] + r_bufs[0], writes=bA)
        self.op("V", lambda e, o=gnB, r=r_scr[1]: e.tensor_tensor(
            o, self.pbank(6, QH), r, AL.mult),
            reads=[self.pb[6]] + r_bufs[1], writes=bB)
        oB = self.ap("AT", head * OWN + q0, QH)
        oBb = self.bufs("AT", head * OWN + q0, QH)
        self.op("V", lambda e, o=oB, a=gnA, b=gnB: e.tensor_tensor(
            o, a, b, AL.subtract),
            reads=bA + bB, writes=oBb)
        sq = self.ap("SC", 0, QH)
        sqb = self.bufs("SC", 0, QH)
        self.op("V", lambda e, o=sq, a=oB: e.tensor_tensor(o, a, a, AL.mult),
                reads=oBb, writes=sqb)

        def tail():
            self.mm(self.pbank(7, QH), onesGN, sq,
                    reads=[self.cb_bufs[1]] + sqb, writes=[self.pb[7]],
                    start=True, stop=True)
            self.op("V", lambda e, o=gnB: e.tensor_copy(o, self.pbank(7, QH)),
                    reads=[self.pb[7]], writes=bB)
            self.mm(self.pbank(7, QH), onesGN, oB,
                    reads=[self.cb_bufs[1]] + oBb, writes=[self.pb[7]],
                    start=True, stop=True)
            self.op("V", lambda e, o=gnA: e.tensor_copy(o, self.pbank(7, QH)),
                    reads=[self.pb[7]], writes=bA)
            self.op("V", lambda e, o=m2, a=gnA: e.tensor_tensor(o, a, a, AL.mult),
                    reads=bA, writes=bM)
            self.op("V", lambda e, o=gnB, a=gnB, b=m2: e.tensor_tensor(
                o, a, b, AL.subtract),
                reads=bB + bM, writes=bB)
            self.op("S", lambda e, o=m2, a=gnB: e.activation(
                o, a, AF.Ln, bias=self.ceps[:, 0:1]),
                reads=bB + [self.ceps_buf], writes=bM)
            self.op("S", lambda e, o=gnB, a=m2: e.activation(
                o, a, AF.Exp, scale=-0.5),
                reads=bM, writes=bB)
            self.op("V", lambda e, o=gnA, a=oB, b=gnA: e.tensor_tensor(
                o, a, b, AL.subtract),
                reads=oBb + bA, writes=bA)
            self.op("V", lambda e, o=gnA, a=gnA, b=gnB: e.tensor_tensor(
                o, a, b, AL.mult),
                reads=bA + bB, writes=bA)
            gw = self.cf[:, self.o_gw + head:self.o_gw + head + 1]
            gb = self.cf[:, self.o_gb + head:self.o_gb + head + 1]
            self.op("V", lambda e, o=oB, a=gnA: e.tensor_scalar(
                o, a, gw, gb, AL.mult, AL.add),
                reads=bA + [self.cf_bufs], writes=oBb)
        return tail

    def _wo_slot(self, i):
        D = self.D
        per_b = self.R["B"].size // D
        per_c = self.R["C"].size // D
        if i < per_b:
            return ("B", i * D)
        i -= per_b
        if i < per_c:
            return ("C", i * D)
        i -= per_c
        return ("E", i * D)

    def _ph2b_wo(self):
        DC, OWN, D = self.DC, self.OWN, self.D
        NIC = self.H
        for i in range(NIC):
            reg, off = self._wo_slot(i)
            if reg != "B":
                self.dma("G", self.ap(reg, off, D),
                         self.d_wo[i * 128:(i + 1) * 128, :],
                         writes=self.bufs(reg, off, D))
        for dc in range(DC):
            for pc in range(self.NOP):
                t0 = pc * self.OPW
                bank = (dc % 2) * 2 + pc % 2
                rec = []
                for i in range(NIC):
                    reg, off = self._wo_slot(i)
                    t = self.mm(self.pbank(bank, self.OPW),
                                self.ap(reg, off + dc * 128, 128),
                                self.ap("AT", i * OWN + t0, self.OPW),
                                reads=self.bufs(reg, off + dc * 128, 128) +
                                      self.bufs("AT", i * OWN + t0, self.OPW),
                                writes=[self.pb[bank]],
                                start=(i == 0), stop=(i == NIC - 1),
                                inc=(i == NIC - 1), record=rec)
                self.attach(t, rec)
                fap = self.ap("ST", dc * OWN + t0, self.OPW)
                fb2 = self.bufs("ST", dc * OWN + t0, self.OPW)
                oap = self.ap("A", dc * OWN + t0, self.OPW)
                ob = self.bufs("A", dc * OWN + t0, self.OPW)
                self.op("V", lambda e, o=oap, f=fap, b=bank: e.tensor_tensor(
                    o, self.pbank(b, self.OPW), f, AL.mult),
                    reads=[self.pb[bank]] + fb2, writes=ob)
            xap = self.ap32("H", 0, OWN)
            xb = self.bufs("H", 0, 2 * OWN)
            self.dma("Q", xap, self.d_xT[dc * 128:(dc + 1) * 128, 0:OWN],
                     writes=xb)
            oap2 = self.ap32("H", 2 * OWN, OWN)
            ob2 = self.bufs("H", 2 * OWN, 2 * OWN)
            sgap = self.ap("A", dc * OWN, OWN)
            sgb = self.bufs("A", dc * OWN, OWN)
            self.op("V", lambda e, o=oap2, a=sgap, b=xap: e.tensor_tensor(
                o, a, b, AL.add),
                reads=sgb + xb, writes=ob2)
            self.dma("Q", self.d_out[dc * 128:(dc + 1) * 128, :], oap2,
                     reads=ob2)
            if "sgT" in self.d_dbg:
                self.dma("Q", self.d_dbg["sgT"][dc * 128:(dc + 1) * 128, :],
                         self.ap("A", dc * OWN, OWN),
                         reads=self.bufs("A", dc * OWN, OWN))

    def _hdn_slot(self, i):
        OWN = self.OWN
        for reg in ("B", "E", "F", "G", "C"):
            cap = self.R[reg].size // OWN
            if i < cap:
                return (reg, i * OWN)
            i -= cap
        raise AssertionError("hdn overflow")

    def _ph3_gateup(self):
        DC, IC, OWN = self.DC, self.IC, self.OWN
        idx_silu = self.tab_idx["silu_and_others"]

        def load_tab(e, idx=idx_silu):
            return e.add_instruction(mybir.InstLoadActFuncSet(
                name=self.nc.get_next_instruction_name(),
                act_func_set_id=idx, ins=[], outs=[]))
        self.op("S", load_tab, serialize=True)
        sg = self.ap("AT", 0, OWN)
        sgb = self.bufs("AT", 0, OWN)
        uu = self.ap("AT", OWN, OWN)
        uub = self.bufs("AT", OWN, OWN)
        slot_i = 0
        for i in range(IC):
            self._kv_step(4)
            si = self._wslot(slot_i); slot_i += 1
            wap = self.ap("H", si, DC * 128)
            wb = self.bufs("H", si, DC * 128)
            self.dma("G", wap, self.d_gwT[i], writes=wb)
            for pc in range(self.NOP):
                bank = pc
                rec = []
                for d in range(DC):
                    t = self.mm(self.pbank(bank, self.OPW),
                                wap[:, d * 128:(d + 1) * 128],
                                self.xn_ap(d, pc * self.OPW, self.OPW),
                                reads=wb + self.xn_bufs(d, pc * self.OPW, self.OPW),
                                writes=[self.pb[bank]],
                                start=(d == 0), stop=(d == DC - 1),
                                inc=(d == DC - 1), record=rec)
                self.attach(t, rec)
            gsl = self.cf[:, self.o_gs + i:self.o_gs + i + 1]
            self.op("S", lambda e, o=sg, s=gsl: e.activation(
                o, self.psum[:, 0:OWN], AF.Silu, scale=s),
                reads=[self.pb[0], self.pb[1], self.cf_bufs], writes=sgb)
            si = self._wslot(slot_i); slot_i += 1
            wap2 = self.ap("H", si, DC * 128)
            wb2 = self.bufs("H", si, DC * 128)
            self.dma("G", wap2, self.d_uwT[i], writes=wb2)
            for pc in range(self.NOP):
                bank = 2 + pc
                rec = []
                for d in range(DC):
                    t = self.mm(self.pbank(bank, self.OPW),
                                wap2[:, d * 128:(d + 1) * 128],
                                self.xn_ap(d, pc * self.OPW, self.OPW),
                                reads=wb2 + self.xn_bufs(d, pc * self.OPW, self.OPW),
                                writes=[self.pb[bank]],
                                start=(d == 0), stop=(d == DC - 1),
                                inc=(d == DC - 1), record=rec)
                self.attach(t, rec)
            usl = self.cf[:, self.o_us + i:self.o_us + i + 1]
            self.op("V", lambda e, o=uu, s=usl: e.tensor_scalar(
                o, self.psum[:, 1024:1024 + OWN], s, None, AL.mult),
                reads=[self.pb[2], self.pb[3], self.cf_bufs], writes=uub)
            reg, off = self._hdn_slot(i)
            hap = self.ap(reg, off, OWN)
            hb = self.bufs(reg, off, OWN)
            self.op("V", lambda e, o=hap, a=sg, b=uu: e.tensor_tensor(
                o, a, b, AL.mult),
                reads=sgb + uub, writes=hb)
        self._kv_step(len(self._kv_units))
        if "hdn" in self.d_dbg:
            for i in range(IC):
                reg, off = self._hdn_slot(i)
                self.dma("Q", self.d_dbg["hdn"][i * 128:(i + 1) * 128, :],
                         self.ap(reg, off, OWN), reads=self.bufs(reg, off, OWN))

    def _ph4_down(self):
        DC, IC, OWN = self.DC, self.IC, self.OWN
        for hd in range(DC):
            si = (hd % 2) * IC * 128
            wap = self.ap("AT", si, IC * 128)
            wb = self.bufs("AT", si, IC * 128)
            self.dma("G", wap, self.d_dwT[hd], writes=wb)
            for pc in range(self.NOP):
                bank = (hd % 2) * 2 + pc
                rec = []
                for i in range(IC):
                    reg, off = self._hdn_slot(i)
                    t = self.mm(self.pbank(bank, self.OPW),
                                wap[:, i * 128:(i + 1) * 128],
                                self.ap(reg, off + pc * self.OPW, self.OPW),
                                reads=wb + self.bufs(reg, off + pc * self.OPW, self.OPW),
                                writes=[self.pb[bank]],
                                start=(i == 0), stop=(i == IC - 1),
                                inc=(i == IC - 1), record=rec)
                self.attach(t, rec)
            dsl = self.cf[:, self.o_ds + hd:self.o_ds + hd + 1]
            pso = (hd % 2) * 2 * 512
            fap = self.ap("ST", hd * OWN, OWN)
            fb2 = self.bufs("ST", hd * OWN, OWN)
            self.op("S", lambda e, o=fap, s=dsl, po=pso: e.activation(
                o, self.psum[:, po:po + OWN], AF.Silu, scale=s),
                reads=[self.pb[(hd % 2) * 2], self.pb[(hd % 2) * 2 + 1],
                       self.cf_bufs], writes=fb2)




    def _finish(self):
        blk = self.ctx.enter_context(self.nc.Block())
        progs = self.prog

        @blk.tensor
        def _(e):
            for f in progs["T"]:
                f(e)

        @blk.vector
        def _(e):
            for f in progs["V"]:
                f(e)

        @blk.scalar
        def _(e):
            for f in progs["S"]:
                f(e)

        @blk.sync
        def _(e):
            for f in progs["Q"]:
                f(e)

        @blk.gpsimd
        def _(e):
            for f in progs["G"]:
                f(e)
        self.ctx.close()


_FULL = dict(D=2048, TOK=2048, H=16, DH=64, INTER=5632, HG=4)
_NC_CACHE = {}


def build_nc(cfg):
    key = tuple(sorted((k, tuple(sorted(v)) if isinstance(v, (list, set, tuple)) else v)
                       for k, v in cfg.items()))
    if key not in _NC_CACHE:
        _NC_CACHE[key] = MK(cfg).nc
    return _NC_CACHE[key]


def kernel(**inputs):
    x = np.ascontiguousarray(np.asarray(inputs["L0_attn_x"], dtype=np.float32))
    B, S, D = x.shape
    wq = np.ascontiguousarray(np.asarray(inputs["L0_attn_Wq"], dtype=np.float32))
    wk = np.ascontiguousarray(np.asarray(inputs["L0_attn_Wk"], dtype=np.float32))
    wv = np.ascontiguousarray(np.asarray(inputs["L0_attn_Wv"], dtype=np.float32))
    wo = np.ascontiguousarray(np.asarray(inputs["L0_attn_Wo"], dtype=np.float32))
    gwT = np.ascontiguousarray(np.asarray(inputs["L0_ffn_gate_w"], dtype=np.float32).T)
    uwT = np.ascontiguousarray(np.asarray(inputs["L0_ffn_up_w"], dtype=np.float32).T)
    dwT = np.ascontiguousarray(np.asarray(inputs["L0_ffn_down_w"], dtype=np.float32).T)
    VW = min(512, wq.shape[1])
    com = dict(
        wq=tile_cols(wq, 128), wk=tile_cols(wk, 128), wv=tile_cols(wv, VW),
        wo=wo, gwT=tile_cols(gwT, 128), uwT=tile_cols(uwT, 128),
        dwT=tile_cols(dwT, 128),
        gs=np.ascontiguousarray(np.asarray(inputs["L0_ffn_gate_s"], dtype=np.float32)),
        us=np.ascontiguousarray(np.asarray(inputs["L0_ffn_up_s"], dtype=np.float32)),
        ds=np.ascontiguousarray(np.asarray(inputs["L0_ffn_down_s"], dtype=np.float32)),
        gnw=np.ascontiguousarray(np.asarray(inputs["L0_attn_gn_weight"], dtype=np.float32)),
        gnb=np.ascontiguousarray(np.asarray(inputs["L0_attn_gn_bias"], dtype=np.float32)),
        lq1=np.ascontiguousarray(np.asarray(inputs["L0_attn_lambda_q1"], dtype=np.float32)),
        lk1=np.ascontiguousarray(np.asarray(inputs["L0_attn_lambda_k1"], dtype=np.float32)),
        lq2=np.ascontiguousarray(np.asarray(inputs["L0_attn_lambda_q2"], dtype=np.float32)),
        lk2=np.ascontiguousarray(np.asarray(inputs["L0_attn_lambda_k2"], dtype=np.float32)),
    )
    nc = build_nc(_FULL)
    OWN = S // 2
    in_maps = []
    for c in range(8):
        bb, ph = c // 2, c % 2
        if ph == 0:
            xs = x[bb]
        else:
            xs = np.concatenate([x[bb, OWN:], x[bb, :OWN]], axis=0)
        m = dict(com)
        m["xT"] = np.ascontiguousarray(xs.T)
        in_maps.append(m)
    res = run_bass_kernel_spmd(nc, in_maps, core_ids=list(range(8)))
    out = np.empty((B, S, D), dtype=np.float32)
    for c in range(8):
        bb, ph = c // 2, c % 2
        out[bb, ph * OWN:(ph + 1) * OWN] = \
            np.ascontiguousarray(res.results[c]["outT"]).T
    return out


# version 29 (final)
# speedup vs baseline: 1.3165x; 1.0076x over previous
import math
import numpy as np
import ml_dtypes
from contextlib import ExitStack

import concourse.bass as bass
import concourse.mybir as mybir
from concourse.bass_utils import run_bass_kernel_spmd

F32 = mybir.dt.float32
BF16 = mybir.dt.bfloat16
AF = mybir.ActivationFunctionType
AL = mybir.AluOpType
AXX = mybir.AxisListType.X

GR = 512


def tile_cols(a, cw):
    R, W = a.shape
    return np.ascontiguousarray(
        a.reshape(R // 128, 128, W // cw, cw).transpose(2, 1, 0, 3)
    ).reshape(W // cw, 128, (R // 128) * cw)


class Buf:
    __slots__ = ("w", "r")

    def __init__(self):
        self.w = None
        self.r = []


class Region:
    def __init__(self, off, size):
        self.off = off
        self.size = size
        n = (size + GR - 1) // GR
        self.bufs = [Buf() for _ in range(n)]

    def b(self, off, n):
        lo = off // GR
        hi = (off + n + GR - 1) // GR
        return self.bufs[lo:hi]


class MK:
    ENGS = ("T", "V", "S", "Q", "G", "P", "Q2", "G2", "Q3", "G3")
    STEP = {"T": 1, "V": 1, "S": 1, "Q": 16, "G": 16, "P": 1,
            "Q2": 16, "G2": 16, "Q3": 16, "G3": 16}
    QCHAN = {"Q": ("Q", "Q2", "Q3"), "G": ("G", "G2", "G3")}

    def __init__(self, cfg):
        self.cfg = cfg
        c = cfg
        self.D = c["D"]; self.TOK = c["TOK"]; self.H = c["H"]
        self.DH = c["DH"]; self.INTER = c["INTER"]; self.HG = c["HG"]
        self.DC = self.D // 128
        self.KT = self.TOK // 128
        assert self.KT % 2 == 0
        self.OWN = self.TOK // 2
        self.OC = self.OWN // 128
        self.QH = min(512, self.OWN)
        self.NQH = self.OWN // self.QH
        self.IC = self.INTER // 128
        self.NG = self.H // self.HG
        self.PW = min(512, self.OWN)
        self.NP = self.TOK // self.PW
        self.OPW = min(512, self.OWN)
        self.NOP = self.OWN // self.OPW
        self.dbg = set(c.get("dbg", ()))

        self.nc = bass.Bass("TRN2", target_bir_lowering=False)
        self.ctx = ExitStack()
        self.prog = {e: [] for e in ("T", "V", "S", "Q", "G")}
        self._flip = {"Q": 0, "G": 0}
        self.cnt = {e: 0 for e in self.ENGS}
        self.maxw = {e: {} for e in self.ENGS}
        self._wpref = {}
        self._mk_dram()
        self._mk_sbuf()
        self._mk_sems()
        self._build()
        self._finish()


    def _mk_sems(self):
        self.sem = {}
        for e in self.ENGS:
            self.sem[e] = self.ctx.enter_context(self.nc.semaphore("sem" + e))
        self.sem["C"] = self.ctx.enter_context(self.nc.semaphore("semC"))
        self.k_snd_buf = Buf()
        self.v_snd_buf = Buf()
        self.kv_rcv_buf = Buf()

    def op(self, eng, fn, reads=(), writes=(), inc=True, serialize=False,
           record=None):
        toks = []
        if serialize and self.cnt[eng] > 0:
            toks.append((eng, self.cnt[eng]))
        for b in reads:
            if b.w is not None:
                toks.append(b.w)
        for b in writes:
            if b.w is not None:
                toks.append(b.w)
            toks.extend(b.r)
        need = {}
        for (e2, v) in toks:
            if eng == "T" and e2 == "T":
                continue
            if v > need.get(e2, -1):
                need[e2] = v
        if eng in ("Q", "G", "Q2", "G2", "Q3", "G3") and self.cnt[eng] > need.get(eng, -1):
            need[eng] = self.cnt[eng]
        waits = []
        mw = self.maxw[eng]
        for e2, v in need.items():
            if v > mw.get(e2, 0):
                waits.append((e2, v))
                mw[e2] = v
        tok = None
        if inc:
            self.cnt[eng] += self.STEP[eng]
            tok = (eng, self.cnt[eng])
        sems = self.sem
        step = self.STEP[eng]

        def thunk(e, fn=fn, waits=waits, tok=tok, eng=eng):
            for (e2, v) in waits:
                e.wait_ge(sems[e2], v)
            ins = fn(e)
            if tok is not None:
                ins.then_inc(sems[eng], step)

        stream = {"P": "G", "Q2": "Q", "G2": "G", "Q3": "Q", "G3": "G"}.get(eng, eng)
        self.prog[stream].append(thunk)
        if tok is not None:
            for b in reads:
                b.r.append(tok)
            for b in writes:
                b.w = tok
                b.r = []
        elif record is not None:
            record.extend(reads)
            record.extend(writes)
        return tok

    def attach(self, tok, bufs):
        for b in bufs:
            b.r.append(tok)

    def dma(self, eng, out_ap, in_ap, reads=(), writes=(), slow=False):
        if eng in self._flip:
            ch = self.QCHAN[eng][self._flip[eng]]
            self._flip[eng] = (self._flip[eng] + 1) % len(self.QCHAN[eng])
            eng = ch
        if slow:
            return self.op(eng, lambda e: e.dma_start(
                out=out_ap, in_=in_ap, allow_slow_non_contiguous=True),
                reads=reads, writes=writes)
        return self.op(eng, lambda e: e.dma_start(out=out_ap, in_=in_ap),
                       reads=reads, writes=writes)

    def mm(self, out_ap, lhsT_ap, rhs_ap, reads, writes, start, stop,
           inc=True, record=None):
        return self.op("T", lambda e: e.matmul(out_ap, lhsT_ap, rhs_ap,
                                               start=start, stop=stop),
                       reads=reads, writes=writes, inc=inc, record=record)


    def _mk_dram(self):
        nc = self.nc
        D, TOK, OWN, INTER = self.D, self.TOK, self.OWN, self.INTER
        inner = self.H * 2 * self.DH
        dt = nc.dram_tensor
        self.d_xT = dt("xT", [D, TOK], F32, kind="ExternalInput").ap()
        DC, IC = self.DC, self.IC
        NIC = self.H
        VW = min(512, inner)
        self.VW = VW
        self.d_wq = dt("wq", [NIC, 128, DC * 128], F32, kind="ExternalInput").ap()
        self.d_wk = dt("wk", [NIC, 128, DC * 128], F32, kind="ExternalInput").ap()
        self.d_wv = dt("wv", [inner // VW, 128, DC * VW], F32,
                       kind="ExternalInput").ap()
        self.d_wo = dt("wo", [inner, D], F32, kind="ExternalInput").ap()
        self.d_gwT = dt("gwT", [IC, 128, DC * 128], F32, kind="ExternalInput").ap()
        self.d_uwT = dt("uwT", [IC, 128, DC * 128], F32, kind="ExternalInput").ap()
        self.d_dwT = dt("dwT", [DC, 128, IC * 128], F32, kind="ExternalInput").ap()
        self.d_gs = dt("gs", [INTER], F32, kind="ExternalInput").ap()
        self.d_us = dt("us", [INTER], F32, kind="ExternalInput").ap()
        self.d_ds = dt("ds", [D], F32, kind="ExternalInput").ap()
        self.d_gnw = dt("gnw", [inner], F32, kind="ExternalInput").ap()
        self.d_gnb = dt("gnb", [inner], F32, kind="ExternalInput").ap()
        self.d_lq1 = dt("lq1", [self.DH], F32, kind="ExternalInput").ap()
        self.d_lk1 = dt("lk1", [self.DH], F32, kind="ExternalInput").ap()
        self.d_lq2 = dt("lq2", [self.DH], F32, kind="ExternalInput").ap()
        self.d_lk2 = dt("lk2", [self.DH], F32, kind="ExternalInput").ap()
        self.d_out = dt("outT", [D, OWN], F32, kind="ExternalOutput").ap()
        W0 = self.H * 2 * self.DH
        self.W0 = W0
        self.d_k_snd = dt("k_snd", [W0, OWN], BF16)
        self.d_k_rcv = dt("k_rcv", [2 * W0, OWN], BF16)
        self.d_v_snd = dt("v_snd", [OWN, W0], BF16)
        self.d_v_rcv = dt("v_rcv", [2 * OWN, W0], BF16)
        self.d_dbg = {}
        for name, shp in (("xn", [D, OWN]), ("aT", [inner, OWN]),
                          ("sT", [D, OWN]), ("hdn", [INTER, OWN]),
                          ("sgT", [D, OWN])):
            if name in self.dbg:
                self.d_dbg[name] = dt("dbg_" + name, shp, BF16,
                                      kind="ExternalOutput").ap()

    def _mk_sbuf(self):
        nc = self.nc
        D, TOK, OWN = self.D, self.TOK, self.OWN
        DC, KT, QH, IC, HG = self.DC, self.KT, self.QH, self.IC, self.HG
        sizes = {}
        sizes["A"] = DC * OWN
        sizes["B"] = max(16 * OWN, 8 * D)
        sizes["C"] = max(KT * QH, DC * HG * 128)
        sizes["E"] = max(HG * TOK, 4 * D)
        sizes["F"] = KT * HG * 128
        sizes["G"] = HG * OWN + 2 * QH
        sizes["H"] = max(2 * DC * 128, 8 * OWN // 2)
        sizes["AT"] = max(self.H * OWN, 2 * IC * 128 + OWN, 2 * OWN)
        sizes["ST"] = max(DC * OWN, 28 * self.PW)
        sizes["SC"] = 8 * QH
        total = 0
        offs = {}
        for k in ("A", "B", "C", "E", "F", "G", "H", "AT", "ST", "SC"):
            offs[k] = total
            total += sizes[k]
        self.pool_el = total
        self.pool = self.ctx.enter_context(
            nc.sbuf_tensor("pool", [128, total], BF16))
        self.R = {k: Region(offs[k], sizes[k]) for k in offs}

        self.cb = self.ctx.enter_context(
            nc.sbuf_tensor("cb", [128, 3 * 128], BF16))
        self.cb_bufs = [Buf() for _ in range(3)]
        self.cf = self.ctx.enter_context(
            nc.sbuf_tensor("cf", [128, 8 + IC * 2 + DC + self.H * 2], F32))
        self.cf_bufs = Buf()
        self.row = self.ctx.enter_context(
            nc.sbuf_tensor("rowscr", [1, 8 * max(64, self.DH)], F32))
        self.row_buf = Buf()
        self.ones_row = self.ctx.enter_context(
            nc.sbuf_tensor("ones_row", [1, 128], F32))
        self.ones_row_buf = Buf()
        self.lam128 = self.ctx.enter_context(
            nc.sbuf_tensor("lam128", [128, 1], F32))
        self.lam128_buf = Buf()
        self.cm8 = self.ctx.enter_context(
            nc.sbuf_tensor("cm8", [128, 1], F32))
        self.cm8_buf = Buf()
        self.ceps = self.ctx.enter_context(
            nc.sbuf_tensor("ceps", [128, 1], F32))
        self.ceps_buf = Buf()
        self.lnlam = self.ctx.enter_context(
            nc.sbuf_tensor("lnlam", [128, 1], F32))
        self.lnlam_buf = Buf()
        self.psum = self.ctx.enter_context(
            nc.psum_tensor("ps", [128, 4096], F32))
        self.pb = [Buf() for _ in range(8)]

    def ap(self, reg, off, n):
        r = self.R[reg]
        assert off + n <= r.size, (reg, off, n, r.size)
        return self.pool[:, r.off + off: r.off + off + n]

    def ap32(self, reg, off, n32):
        r = self.R[reg]
        assert off + 2 * n32 <= r.size, (reg, off, n32, r.size)
        return self.pool[:, r.off + off: r.off + off + 2 * n32].bitcast(F32)

    def bufs(self, reg, off, n):
        return self.R[reg].b(off, n)

    def pbank(self, b, width=None, off=0):
        w = 512 if width is None else width
        return self.psum[:, b * 512 + off: b * 512 + off + w]


    def _build(self):
        stop = self.cfg.get("stop_after", "")
        phases = [("consts", self._consts), ("ph0", self._ph0_layernorm),
                  ("ph1", self._ph1_lam), ("ph1b", self._ph1b_kv),
                  ("ph3", self._ph3_gateup), ("ph4", self._ph4_down),
                  ("ph2", self._ph2_attention), ("ph2b", self._ph2b_wo)]
        for name, fn in phases:
            fn()
            if name == stop:
                break
        sems = self.sem
        for ch in ("Q", "Q2", "Q3"):
            v = self.cnt[ch]
            if v:
                self.prog["Q"].append(lambda e, c=ch, vv=v: e.wait_ge(sems[c], vv))
        for ch in ("G", "G2", "G3"):
            v = self.cnt[ch]
            if v:
                self.prog["G"].append(lambda e, c=ch, vv=v: e.wait_ge(sems[c], vv))

    def _consts(self):
        nc = self.nc
        H, DC, IC = self.H, self.DC, self.IC
        from concourse.hw_specs import get_activation_tables
        tabs = list(get_activation_tables(self.nc.m.arch).keys())
        self.tab_idx = {name: i for i, name in enumerate(tabs)}

        def load_tab(e, idx):
            return e.add_instruction(mybir.InstLoadActFuncSet(
                name=self.nc.get_next_instruction_name(),
                act_func_set_id=idx, ins=[], outs=[]))
        idx_nle = self.tab_idx["natural_log_exp_and_others"]
        self.op("S", lambda e: load_tab(e, idx_nle), serialize=True)

        cb, cf = self.cb, self.cf
        b = self.cb_bufs
        self.op("V", lambda e: e.memset(cb[:, 0:128], 1.0), writes=[b[0]])
        self.op("V", lambda e: e.memset(cb[:, 128:256], 1.0 / (2 * self.DH)),
                writes=[b[1]])
        self.op("V", lambda e: e.memset(cb[:, 256:384], 1.0 / self.D),
                writes=[b[2]])
        self.op("V", lambda e: e.memset(self.ones_row[:], 1.0),
                writes=[self.ones_row_buf])
        self.op("V", lambda e: e.memset(self.cm8[:], -8.0),
                writes=[self.cm8_buf])
        self.op("V", lambda e: e.memset(self.ceps[:], 1e-5),
                writes=[self.ceps_buf])
        self.o_gs = 0
        self.o_us = IC
        self.o_ds = 2 * IC
        self.o_gw = 2 * IC + DC
        self.o_gb = 2 * IC + DC + H
        cfb = self.cf_bufs
        self.dma("Q", cf[:, self.o_gs:self.o_gs + IC],
                 self.d_gs.rearrange("(c p) -> p c", p=128), writes=[cfb], slow=True)
        self.dma("Q", cf[:, self.o_us:self.o_us + IC],
                 self.d_us.rearrange("(c p) -> p c", p=128), writes=[cfb], slow=True)
        self.dma("Q", cf[:, self.o_ds:self.o_ds + DC],
                 self.d_ds.rearrange("(c p) -> p c", p=128), writes=[cfb], slow=True)
        self.dma("Q", cf[:, self.o_gw:self.o_gw + H],
                 self.d_gnw.rearrange("(c p) -> p c", p=128), writes=[cfb], slow=True)
        self.dma("Q", cf[:, self.o_gb:self.o_gb + H],
                 self.d_gnb.rearrange("(c p) -> p c", p=128), writes=[cfb], slow=True)
        self.op("V", lambda e: e.tensor_scalar_mul(
            cf[:, self.o_gw:self.o_gw + 2 * H],
            cf[:, self.o_gw:self.o_gw + 2 * H], 0.2),
            reads=[cfb], writes=[cfb])

    def xn_ap(self, c, t0, n):
        OWN = self.OWN
        if t0 >= OWN:
            return self.ap("B", c * (self.TOK - OWN) + (t0 - OWN), n)
        assert t0 + n <= OWN
        return self.ap("A", c * OWN + t0, n)

    def xn_bufs(self, c, t0, n):
        OWN = self.OWN
        if t0 >= OWN:
            return self.bufs("B", c * (self.TOK - OWN) + (t0 - OWN), n)
        return self.bufs("A", c * OWN + t0, n)

    def _ph0_layernorm(self):
        nc = self.nc
        D, TOK, OWN, DC, PW, NP = self.D, self.TOK, self.OWN, self.DC, self.PW, self.NP
        NP = self.OWN // PW
        xT3 = self.d_xT.rearrange("(c p) t -> p c t", p=128)
        for p in range(NP):
            t0 = p * PW
            base = self.R["A"].off
            out3 = self.pool[:, base: base + DC * OWN].rearrange(
                "p (c f) -> p c f", c=DC)[:, :, t0:t0 + PW]
            wb = []
            for c in range(DC):
                wb += self.xn_bufs(c, t0, PW)
            self.dma("G", out3, xT3[:, :, t0:t0 + PW], writes=wb)
        onesLN = self.cb[:, 256:384]
        ST = "ST"
        def scr(p):
            base = (p % 2) * 12 * PW
            muS = self.ap32(ST, base, PW)
            var = self.ap32(ST, base + 2 * PW, PW)
            rstd = self.ap32(ST, base + 4 * PW, PW)
            mumix = self.ap32(ST, base + 6 * PW, PW)
            return (muS, var, rstd, mumix,
                    self.bufs(ST, base, 2 * PW),
                    self.bufs(ST, base + 2 * PW, 2 * PW),
                    self.bufs(ST, base + 4 * PW, 2 * PW),
                    self.bufs(ST, base + 6 * PW, 2 * PW))
        for p in range(NP):
            muS, var, rstd, mumix, bmu, bvar, brstd, bmix = scr(p)
            t0 = p * PW
            bmu_k = (p % 2) * 2
            bs2_k = bmu_k + 1
            rec0, rec1 = [], []
            for c in range(DC):
                t = self.mm(self.pbank(bmu_k, PW), onesLN, self.xn_ap(c, t0, PW),
                            reads=[self.cb_bufs[2]] + self.xn_bufs(c, t0, PW),
                            writes=[self.pb[bmu_k]], start=(c == 0),
                            stop=(c == DC - 1), inc=(c == DC - 1), record=rec0)
                if t is not None:
                    self.attach(t, rec0)
            for c in range(DC):
                sq = self.ap(ST, 24 * PW + ((p * DC + c) % 2) * PW, PW)
                sqb = self.bufs(ST, 24 * PW + ((p * DC + c) % 2) * PW, PW)
                self.op("S", lambda e, i=self.xn_ap(c, t0, PW), o=sq:
                        e.activation(o, i, AF.Square),
                        reads=self.xn_bufs(c, t0, PW), writes=sqb)
                self.mm(self.pbank(bs2_k, PW), onesLN, sq,
                        reads=[self.cb_bufs[2]] + sqb,
                        writes=[self.pb[bs2_k]], start=(c == 0),
                        stop=(c == DC - 1))
            self.op("V", lambda e, o=muS, b=bmu_k: e.tensor_copy(o, self.pbank(b, PW)),
                    reads=[self.pb[bmu_k]], writes=bmu)
            self.op("V", lambda e, o=var, a=muS: e.tensor_tensor(o, a, a, AL.mult),
                    reads=bmu, writes=bvar)
            self.op("V", lambda e, o=var, a=var, b=bs2_k: e.tensor_tensor(
                o, self.pbank(b, PW), a, AL.subtract),
                reads=[self.pb[bs2_k]] + bvar, writes=bvar)
            self.op("S", lambda e, o=rstd, a=var: e.activation(
                o, a, AF.Ln, bias=self.ceps[:, 0:1]),
                reads=bvar + [self.ceps_buf], writes=brstd)
            self.op("S", lambda e, o=rstd, a=rstd: e.activation(
                o, a, AF.Exp, scale=-0.5),
                reads=brstd, writes=brstd)
            self.op("V", lambda e, o=mumix, a=muS, b2=rstd: e.scalar_tensor_tensor(
                o, a, -1.0, b2, AL.mult, AL.mult),
                reads=bmu + brstd, writes=bmix)
            for c in range(DC):
                eng = "V" if c % 2 == 0 else "P"
                t = self.ap(ST, 26 * PW + (c % 2) * PW, PW)
                tb = self.bufs(ST, 26 * PW + (c % 2) * PW, PW)
                self.op(eng, lambda e, o=t, a=self.xn_ap(c, t0, PW), b2=rstd:
                        e.tensor_tensor(o, a, b2, AL.mult),
                        reads=self.xn_bufs(c, t0, PW) + brstd, writes=tb)
                self.op(eng, lambda e, o=self.xn_ap(c, t0, PW), a=t, b2=mumix:
                        e.tensor_tensor(o, a, b2, AL.add),
                        reads=tb + bmix, writes=self.xn_bufs(c, t0, PW))
        if "xn" in self.d_dbg:
            for c in range(DC):
                self.dma("Q", self.d_dbg["xn"][c * 128:(c + 1) * 128, :],
                         self.xn_ap(c, 0, OWN), reads=self.xn_bufs(c, 0, OWN))

    def _ph1_lam(self):
        row, rb = self.row, self.row_buf
        DH = self.DH
        for i, d in enumerate((self.d_lq1, self.d_lk1, self.d_lq2, self.d_lk2)):
            self.dma("Q", row[0:1, i * DH:(i + 1) * DH],
                     d.rearrange("(o f) -> o f", o=1), writes=[rb])
        p1 = row[0:1, 4 * DH:5 * DH]
        self.op("V", lambda e: e.tensor_tensor(
            p1, row[0:1, 0:DH], row[0:1, DH:2 * DH], AL.mult),
            reads=[rb], writes=[rb])
        s1 = row[0:1, 5 * DH:5 * DH + 1]
        self.op("V", lambda e: e.reduce_sum(s1, p1, axis=AXX),
                reads=[rb], writes=[rb])
        self.op("V", lambda e: e.tensor_tensor(
            p1, row[0:1, 2 * DH:3 * DH], row[0:1, 3 * DH:4 * DH], AL.mult),
            reads=[rb], writes=[rb])
        s2 = row[0:1, 5 * DH + 1:5 * DH + 2]
        self.op("V", lambda e: e.reduce_sum(s2, p1, axis=AXX),
                reads=[rb], writes=[rb])
        e1 = row[0:1, 5 * DH + 2:5 * DH + 3]
        e2 = row[0:1, 5 * DH + 3:5 * DH + 4]
        self.op("S", lambda e: e.activation(e1, s1, AF.Exp), reads=[rb], writes=[rb])
        self.op("S", lambda e: e.activation(e2, s2, AF.Exp), reads=[rb], writes=[rb])
        lam = row[0:1, 5 * DH + 4:5 * DH + 5]
        self.op("V", lambda e: e.tensor_tensor(lam, e1, e2, AL.subtract),
                reads=[rb], writes=[rb])
        self.op("V", lambda e: e.tensor_scalar_add(lam, lam, 0.8),
                reads=[rb], writes=[rb])
        self.mm(self.pbank(2, 1), self.ones_row[:], lam,
                reads=[self.ones_row_buf, rb], writes=[self.pb[2]],
                start=True, stop=True)
        self.op("V", lambda e: e.tensor_copy(self.lam128[:], self.pbank(2, 1)),
                reads=[self.pb[2]], writes=[self.lam128_buf])




    def _wslot(self, idx):
        base = (idx % 2) * self.DC * 128
        return base

    def _ph1b_kv(self):
        DC, OWN, OC = self.DC, self.OWN, self.OC
        NIC = self.H
        ST = "ST"
        kvK = self.d_k_snd.ap()
        units = []

        def k_unit(icc):
            def emit():
                at_off = 2 * OWN + (icc % 2) * DC * 128
                wap = self.ap("AT", at_off, DC * 128)
                wb = self.bufs("AT", at_off, DC * 128)
                self.dma("G", wap, self.d_wk[icc], writes=wb)
                sslot = (icc % 2) * OWN
                for pc in range(self.NOP):
                    t0 = pc * self.OPW
                    bank = 4 + (icc % 2) * 2 + pc
                    rec = []
                    for d in range(DC):
                        t = self.mm(self.pbank(bank, self.OPW),
                                    wap[:, d * 128:(d + 1) * 128],
                                    self.xn_ap(d, t0, self.OPW),
                                    reads=wb + self.xn_bufs(d, t0, self.OPW),
                                    writes=[self.pb[bank]],
                                    start=(d == 0), stop=(d == DC - 1),
                                    inc=(d == DC - 1), record=rec)
                    self.attach(t, rec)
                    sg = self.ap(ST, sslot + t0, self.OPW)
                    sgb = self.bufs(ST, sslot + t0, self.OPW)
                    self.op("V", lambda e, o=sg, b=bank: e.tensor_copy(
                        o, self.pbank(b, self.OPW)),
                        reads=[self.pb[bank]], writes=sgb)
                self.dma("Q", kvK[icc * 128:(icc + 1) * 128, :],
                         self.ap(ST, sslot, OWN),
                         reads=self.bufs(ST, sslot, OWN),
                         writes=[self.k_snd_buf])
            return emit

        VW = self.VW
        NVG = self.W0 // VW

        def v_unit(vg, t, uidx):
            def emit():
                wvb = self.bufs("C", 0, DC * VW)
                if t == 0:
                    self.dma("G", self.ap("C", 0, DC * VW),
                             self.d_wv[vg], writes=wvb)
                bank = 4 + uidx % 4
                rec = []
                for d in range(DC):
                    tk = self.mm(self.pbank(bank, VW),
                                 self.xn_ap(d, t * 128, 128),
                                 self.ap("C", d * VW, VW),
                                 reads=self.xn_bufs(d, t * 128, 128) + wvb,
                                 writes=[self.pb[bank]],
                                 start=(d == 0), stop=(d == DC - 1),
                                 inc=(d == DC - 1), record=rec)
                self.attach(tk, rec)
                sg = self.ap(ST, 2 * OWN + (uidx % 2) * VW, VW)
                sgb = self.bufs(ST, 2 * OWN + (uidx % 2) * VW, VW)
                self.op("V", lambda e, o=sg, b=bank, w=VW: e.tensor_copy(
                    o, self.pbank(b, w)),
                    reads=[self.pb[bank]], writes=sgb)
                self.dma("Q", self.d_v_snd.ap()[
                    t * 128:(t + 1) * 128, vg * VW:(vg + 1) * VW],
                    sg, reads=sgb, writes=[self.v_snd_buf])
            return emit

        for icc in range(NIC):
            units.append(k_unit(icc))
        ui = 0
        for vg in range(NVG):
            for t in range(OC):
                units.append(v_unit(vg, t, ui))
                ui += 1
        rg = self.cfg.get("replica_groups")
        if rg is None:
            rg = [[0, 1], [2, 3], [4, 5], [6, 7]]
        semC = self.sem["C"]
        ks, kr = self.d_k_snd.ap().opt(), self.d_k_rcv.ap().opt()
        vs, vr = self.d_v_snd.ap().opt(), self.d_v_rcv.ap().opt()

        def k_trig():
            self.op("G", lambda e: e.collective_compute(
                "AllGather", AL.bypass, replica_groups=rg,
                ins=[ks], outs=[kr]).then_inc(semC, 1),
                reads=[self.k_snd_buf], inc=False)

        def v_trig():
            self.op("G", lambda e: e.collective_compute(
                "AllGather", AL.bypass, replica_groups=rg,
                ins=[vs], outs=[vr]).then_inc(semC, 1),
                reads=[self.v_snd_buf], inc=False)
            self.kv_rcv_buf.w = ("C", 2)
        self._kv_units = units
        self._kv_ktrig = (NIC, k_trig)
        self._kv_vtrig = (len(units), v_trig)
        self._kv_consumed = 0

    def _kv_step(self, n=2):
        for _ in range(n):
            if self._kv_consumed < len(self._kv_units):
                self._kv_units[self._kv_consumed]()
                self._kv_consumed += 1
                if self._kv_consumed == self._kv_ktrig[0]:
                    self._kv_ktrig[1]()
                if self._kv_consumed == self._kv_vtrig[0]:
                    self._kv_vtrig[1]()

    def _load_wcol(self, dram, chunk, slot_idx):
        si = self._wslot(slot_idx)
        wap = self.ap("H", si, self.DC * 128)
        wb = self.bufs("H", si, self.DC * 128)
        self.dma("G", wap, dram[chunk], writes=wb)
        return (wap, wb)

    def _ph2_attention(self):
        DC, HG, NG, OWN, TOK, QH, KT = (self.DC, self.HG, self.NG, self.OWN,
                                        self.TOK, self.QH, self.KT)
        DH = self.DH
        slot_i = 0
        self._pend_tail = None
        idx_nle = self.tab_idx["natural_log_exp_and_others"]

        def load_tab(e, idx=idx_nle):
            return e.add_instruction(mybir.InstLoadActFuncSet(
                name=self.nc.get_next_instruction_name(),
                act_func_set_id=idx, ins=[], outs=[]))
        self.op("S", load_tab, serialize=True)
        for z in range(2):
            apz = self.ap("G", HG * OWN + z * QH, QH)
            self.op("V", lambda e, a=apz: e.memset(a, 0.0),
                    writes=self.bufs("G", HG * OWN + z * QH, QH))
        for i in range(self.H):
            reg, off = self._wo_slot(i)
            if reg == "B":
                self.dma("G", self.ap(reg, off, self.D),
                         self.d_wo[i * 128:(i + 1) * 128, :],
                         writes=self.bufs(reg, off, self.D))
        NGrun = self.cfg.get("ph2_groups", NG)
        noheads = self.cfg.get("ph2_noheads", False)
        for g in range(NGrun):
            col0 = g * HG * 128
            for cc in range(HG):
                wap, wb = self._load_wcol(self.d_wq, g * HG + cc, slot_i)
                slot_i += 1
                for pc in range(self.NOP):
                    t0 = pc * self.OPW
                    bank = pc % 2
                    rec = []
                    for d in range(DC):
                        t = self.mm(self.pbank(bank, self.OPW),
                                    wap[:, d * 128:(d + 1) * 128],
                                    self.xn_ap(d, t0, self.OPW),
                                    reads=wb + self.xn_bufs(d, t0, self.OPW),
                                    writes=[self.pb[bank]],
                                    start=(d == 0), stop=(d == DC - 1),
                                    inc=(d == DC - 1), record=rec)
                    self.attach(t, rec)
                    q1 = self.ap("G", cc * OWN + t0, self.OPW)
                    qb1 = self.bufs("G", cc * OWN + t0, self.OPW)
                    self.op("V", lambda e, o=q1, b=bank: e.tensor_copy(
                        o, self.pbank(b, self.OPW)),
                        reads=[self.pb[bank]], writes=qb1)
            gw = HG * 128
            W0 = self.W0
            kvb = [self.kv_rcv_buf]
            for half in range(2):
                kview = self.d_k_rcv.ap()[half * W0:(half + 1) * W0, :]
                eb = []
                for cc in range(HG):
                    eb += self.bufs("E", cc * TOK + half * OWN, OWN)
                eout = self.pool[:, self.R["E"].off:
                                 self.R["E"].off + HG * TOK].rearrange(
                    "p (c t) -> p c t", c=HG)[:, :, half * OWN:(half + 1) * OWN]
                self.dma("Q", eout,
                         kview[g * gw:(g + 1) * gw, :].rearrange(
                             "(c p) t -> p c t", p=128),
                         reads=kvb, writes=eb)
                vview = self.d_v_rcv.ap()[
                    half * OWN:(half + 1) * OWN, g * gw:(g + 1) * gw]
                fb = []
                for t in range(KT // 2):
                    fb += self.bufs("F", (half * KT // 2 + t) * gw, gw)
                fout = self.pool[:, self.R["F"].off:
                                 self.R["F"].off + KT * gw].rearrange(
                    "p (t f) -> p t f", t=KT)[:, half * KT // 2:(half + 1) * KT // 2, :]
                self.dma("Q", fout,
                         vview.rearrange("(t p) f -> p t f", p=128),
                         reads=kvb, writes=fb)
            for l in range(HG):
                if noheads:
                    break
                head = g * HG + l
                for qh in range(self.NQH):
                    nj = None
                    if qh + 1 < self.NQH:
                        nj = (l, (qh + 1) * self.QH)
                    elif l + 1 < HG:
                        nj = (l + 1, 0)
                    self._pend_tail = self._attend(head, l, qh,
                                                   flush=self._pend_tail,
                                                   next_job=nj)
        if self._pend_tail is not None:
            self._pend_tail()
            self._pend_tail = None
        if "aT" in self.d_dbg:
            for hh in range(self.H):
                self.dma("Q", self.d_dbg["aT"][hh * 128:(hh + 1) * 128, :],
                         self.ap("AT", hh * OWN, OWN),
                         reads=self.bufs("AT", hh * OWN, OWN))

    def _emit_qz(self, l, q0):
        QH, DH, OWN = self.QH, self.DH, self.OWN
        aps, bs = [], []
        for comp in range(2):
            apz = self.ap("G", self.HG * OWN + comp * QH, QH)
            bz = self.bufs("G", self.HG * OWN + comp * QH, QH)
            rows = slice(0, DH) if comp == 0 else slice(DH, 128)
            qp = self.ap("G", l * OWN + q0, QH)
            qpb = self.bufs("G", l * OWN + q0, QH)
            self.op("V", lambda e, o=apz, a=qp, r=rows: e.tensor_copy(
                o[r, :], a[r, :]),
                reads=qpb, writes=bz)
            aps.append(apz)
            bs.append(bz)
        self._qz_state = (l, q0, aps, bs)
        return aps, bs

    def _attend(self, head, l, qh, flush=None, next_job=None):
        QH, KT, OWN, TOK = self.QH, self.KT, self.OWN, self.TOK
        DH = self.DH
        q0 = qh * QH
        ones128 = self.cb[:, 0:128]
        onesGN = self.cb[:, 128:256]
        SA, SB, D_, O1, O2 = 0, 2, 4, 5, 6
        kap_all = lambda kt: self.ap("E", l * TOK + kt * 128, 128)
        kb_all = lambda kt: self.bufs("E", l * TOK + kt * 128, 128)
        r_scr = [self.ap32("SC", 0, QH), self.ap32("SC", 2 * QH, QH)]
        r_bufs = [self.bufs("SC", 0, 2 * QH), self.bufs("SC", 2 * QH, 2 * QH)]

        crec = [[], []]
        prec = [[], []]

        st = getattr(self, "_qz_state", None)
        if st is not None and st[0] == l and st[1] == q0:
            qz_ap, qz_b = st[2], st[3]
        else:
            qz_ap, qz_b = self._emit_qz(l, q0)

        def cspv(j, comp):
            dbank = D_ if comp == 0 else 7
            obank = O1 if comp == 0 else O2
            for z in range(2):
                kt = 2 * j + z
                last = (kt == KT - 1)
                et = self.ap("C", kt * QH, QH)
                etb = self.bufs("C", kt * QH, QH)
                t = self.mm(self.pbank(dbank, QH), ones128, et,
                            reads=[self.cb_bufs[0]] + etb,
                            writes=[self.pb[dbank]],
                            start=(kt == 0), stop=last,
                            inc=last, record=crec[comp])
                if t is not None:
                    self.attach(t, crec[comp])
                vt = self.ap("F", kt * self.HG * 128 + l * 128, 128)
                vtb = self.bufs("F", kt * self.HG * 128 + l * 128, 128)
                t = self.mm(self.pbank(obank, QH), vt, et,
                            reads=vtb + etb,
                            writes=[self.pb[obank]],
                            start=(kt == 0), stop=last,
                            inc=last, record=prec[comp])
                if t is not None:
                    self.attach(t, prec[comp])

        npair = KT // 2

        def unit_flush(u):
            comp, j = u
            cspv(j, comp)
            if j == npair - 1:
                db = D_ if comp == 0 else 7
                self.op("S", lambda e, o=r_scr[comp], b=db: e.activation(
                    o, self.pbank(b, QH), AF.Ln),
                    reads=[self.pb[db]], writes=r_bufs[comp])
                self.op("S", lambda e, o=r_scr[comp]: e.activation(
                    o, o, AF.Exp, scale=-1.0),
                    reads=r_bufs[comp], writes=r_bufs[comp])

        pend = []
        unit_no = 0
        for comp in range(2):
            qz = qz_ap[comp]
            qzb = qz_b[comp]
            for j in range(npair):
                pair = SA if (j % 2 == 0) else SB
                for z in range(2):
                    kt = 2 * j + z
                    off = pair * 512 + z * QH
                    sap = self.psum[:, off: off + QH]
                    self.mm(sap, kap_all(kt), qz,
                            reads=kb_all(kt) + qzb,
                            writes=[self.pb[off // 512]],
                            start=True, stop=True)
                eap = self.ap("C", 2 * j * QH, 2 * QH)
                eb = self.bufs("C", 2 * j * QH, 2 * QH)
                self.op("S", lambda e, o=eap, p=pair: e.activation(
                    o, self.psum[:, p * 512: p * 512 + 2 * QH],
                    AF.Exp, scale=1.0 / math.sqrt(DH), bias=self.cm8[:, 0:1]),
                    reads=[self.pb[pair], self.pb[pair + 1],
                           self.pb[(pair * 512 + QH) // 512], self.cm8_buf],
                    writes=eb)
                lag = min(2, npair - 1)
                if unit_no == min(5, npair - 1) and flush is not None:
                    flush()
                unit_no += 1
                pend.append((comp, j))
                while len(pend) > lag:
                    unit_flush(pend.pop(0))
        if next_job is not None:
            self._emit_qz(*next_job)
        for u in pend:
            unit_flush(u)
        gnA = self.ap32("SC", 4 * QH, QH)
        gnB = self.ap32("SC", 6 * QH, QH)
        m2 = self.ap32("SC", 0, QH)
        bA = self.bufs("SC", 4 * QH, 2 * QH)
        bB = self.bufs("SC", 6 * QH, 2 * QH)
        bM = self.bufs("SC", 0, 2 * QH)
        self.op("V", lambda e, o=r_scr[1]: e.tensor_scalar(
            o, o, self.lam128[:, 0:1], None, AL.mult),
            reads=r_bufs[1] + [self.lam128_buf], writes=r_bufs[1])
        self.op("V", lambda e, o=gnA, r=r_scr[0]: e.tensor_tensor(
            o, self.pbank(5, QH), r, AL.mult),
            reads=[self.pb[5]] + r_bufs[0], writes=bA)
        self.op("V", lambda e, o=gnB, r=r_scr[1]: e.tensor_tensor(
            o, self.pbank(6, QH), r, AL.mult),
            reads=[self.pb[6]] + r_bufs[1], writes=bB)
        oB = self.ap("AT", head * OWN + q0, QH)
        oBb = self.bufs("AT", head * OWN + q0, QH)
        self.op("V", lambda e, o=oB, a=gnA, b=gnB: e.tensor_tensor(
            o, a, b, AL.subtract),
            reads=bA + bB, writes=oBb)
        sq = self.ap("SC", 0, QH)
        sqb = self.bufs("SC", 0, QH)
        self.op("V", lambda e, o=sq, a=oB: e.tensor_tensor(o, a, a, AL.mult),
                reads=oBb, writes=sqb)

        def tail():
            self.mm(self.pbank(7, QH), onesGN, sq,
                    reads=[self.cb_bufs[1]] + sqb, writes=[self.pb[7]],
                    start=True, stop=True)
            self.op("V", lambda e, o=gnB: e.tensor_copy(o, self.pbank(7, QH)),
                    reads=[self.pb[7]], writes=bB)
            self.mm(self.pbank(7, QH), onesGN, oB,
                    reads=[self.cb_bufs[1]] + oBb, writes=[self.pb[7]],
                    start=True, stop=True)
            self.op("V", lambda e, o=gnA: e.tensor_copy(o, self.pbank(7, QH)),
                    reads=[self.pb[7]], writes=bA)
            self.op("V", lambda e, o=m2, a=gnA: e.tensor_tensor(o, a, a, AL.mult),
                    reads=bA, writes=bM)
            self.op("V", lambda e, o=gnB, a=gnB, b=m2: e.tensor_tensor(
                o, a, b, AL.subtract),
                reads=bB + bM, writes=bB)
            self.op("S", lambda e, o=m2, a=gnB: e.activation(
                o, a, AF.Ln, bias=self.ceps[:, 0:1]),
                reads=bB + [self.ceps_buf], writes=bM)
            self.op("S", lambda e, o=gnB, a=m2: e.activation(
                o, a, AF.Exp, scale=-0.5),
                reads=bM, writes=bB)
            self.op("V", lambda e, o=gnA, a=oB, b=gnA: e.tensor_tensor(
                o, a, b, AL.subtract),
                reads=oBb + bA, writes=bA)
            self.op("V", lambda e, o=gnA, a=gnA, b=gnB: e.tensor_tensor(
                o, a, b, AL.mult),
                reads=bA + bB, writes=bA)
            gw = self.cf[:, self.o_gw + head:self.o_gw + head + 1]
            gb = self.cf[:, self.o_gb + head:self.o_gb + head + 1]
            self.op("V", lambda e, o=oB, a=gnA: e.tensor_scalar(
                o, a, gw, gb, AL.mult, AL.add),
                reads=bA + [self.cf_bufs], writes=oBb)
        return tail

    def _wo_slot(self, i):
        D = self.D
        per_b = self.R["B"].size // D
        per_c = self.R["C"].size // D
        if i < per_b:
            return ("B", i * D)
        i -= per_b
        if i < per_c:
            return ("C", i * D)
        i -= per_c
        return ("E", i * D)

    def _ph2b_wo(self):
        DC, OWN, D = self.DC, self.OWN, self.D
        NIC = self.H
        for i in range(NIC):
            reg, off = self._wo_slot(i)
            if reg != "B":
                self.dma("G", self.ap(reg, off, D),
                         self.d_wo[i * 128:(i + 1) * 128, :],
                         writes=self.bufs(reg, off, D))
        for dc in range(DC):
            for pc in range(self.NOP):
                t0 = pc * self.OPW
                bank = (dc % 2) * 2 + pc % 2
                rec = []
                for i in range(NIC):
                    reg, off = self._wo_slot(i)
                    t = self.mm(self.pbank(bank, self.OPW),
                                self.ap(reg, off + dc * 128, 128),
                                self.ap("AT", i * OWN + t0, self.OPW),
                                reads=self.bufs(reg, off + dc * 128, 128) +
                                      self.bufs("AT", i * OWN + t0, self.OPW),
                                writes=[self.pb[bank]],
                                start=(i == 0), stop=(i == NIC - 1),
                                inc=(i == NIC - 1), record=rec)
                self.attach(t, rec)
                fap = self.ap("ST", dc * OWN + t0, self.OPW)
                fb2 = self.bufs("ST", dc * OWN + t0, self.OPW)
                oap = self.ap("A", dc * OWN + t0, self.OPW)
                ob = self.bufs("A", dc * OWN + t0, self.OPW)
                self.op("V", lambda e, o=oap, f=fap, b=bank: e.tensor_tensor(
                    o, self.pbank(b, self.OPW), f, AL.mult),
                    reads=[self.pb[bank]] + fb2, writes=ob)
            xap = self.ap32("H", 0, OWN)
            xb = self.bufs("H", 0, 2 * OWN)
            self.dma("Q", xap, self.d_xT[dc * 128:(dc + 1) * 128, 0:OWN],
                     writes=xb)
            oap2 = self.ap32("H", 2 * OWN, OWN)
            ob2 = self.bufs("H", 2 * OWN, 2 * OWN)
            sgap = self.ap("A", dc * OWN, OWN)
            sgb = self.bufs("A", dc * OWN, OWN)
            self.op("V", lambda e, o=oap2, a=sgap, b=xap: e.tensor_tensor(
                o, a, b, AL.add),
                reads=sgb + xb, writes=ob2)
            self.dma("Q", self.d_out[dc * 128:(dc + 1) * 128, :], oap2,
                     reads=ob2)
            if "sgT" in self.d_dbg:
                self.dma("Q", self.d_dbg["sgT"][dc * 128:(dc + 1) * 128, :],
                         self.ap("A", dc * OWN, OWN),
                         reads=self.bufs("A", dc * OWN, OWN))

    def _hdn_slot(self, i):
        OWN = self.OWN
        for reg in ("B", "E", "F", "G", "C"):
            cap = self.R[reg].size // OWN
            if i < cap:
                return (reg, i * OWN)
            i -= cap
        raise AssertionError("hdn overflow")

    def _ph3_gateup(self):
        DC, IC, OWN = self.DC, self.IC, self.OWN
        idx_silu = self.tab_idx["silu_and_others"]

        def load_tab(e, idx=idx_silu):
            return e.add_instruction(mybir.InstLoadActFuncSet(
                name=self.nc.get_next_instruction_name(),
                act_func_set_id=idx, ins=[], outs=[]))
        self.op("S", load_tab, serialize=True)
        sg = self.ap("AT", 0, OWN)
        sgb = self.bufs("AT", 0, OWN)
        uu = self.ap("AT", OWN, OWN)
        uub = self.bufs("AT", OWN, OWN)
        slot_i = 0
        for i in range(IC):
            self._kv_step(4)
            si = self._wslot(slot_i); slot_i += 1
            wap = self.ap("H", si, DC * 128)
            wb = self.bufs("H", si, DC * 128)
            self.dma("G", wap, self.d_gwT[i], writes=wb)
            for pc in range(self.NOP):
                bank = pc
                rec = []
                for d in range(DC):
                    t = self.mm(self.pbank(bank, self.OPW),
                                wap[:, d * 128:(d + 1) * 128],
                                self.xn_ap(d, pc * self.OPW, self.OPW),
                                reads=wb + self.xn_bufs(d, pc * self.OPW, self.OPW),
                                writes=[self.pb[bank]],
                                start=(d == 0), stop=(d == DC - 1),
                                inc=(d == DC - 1), record=rec)
                self.attach(t, rec)
            gsl = self.cf[:, self.o_gs + i:self.o_gs + i + 1]
            self.op("S", lambda e, o=sg, s=gsl: e.activation(
                o, self.psum[:, 0:OWN], AF.Silu, scale=s),
                reads=[self.pb[0], self.pb[1], self.cf_bufs], writes=sgb)
            si = self._wslot(slot_i); slot_i += 1
            wap2 = self.ap("H", si, DC * 128)
            wb2 = self.bufs("H", si, DC * 128)
            self.dma("G", wap2, self.d_uwT[i], writes=wb2)
            for pc in range(self.NOP):
                bank = 2 + pc
                rec = []
                for d in range(DC):
                    t = self.mm(self.pbank(bank, self.OPW),
                                wap2[:, d * 128:(d + 1) * 128],
                                self.xn_ap(d, pc * self.OPW, self.OPW),
                                reads=wb2 + self.xn_bufs(d, pc * self.OPW, self.OPW),
                                writes=[self.pb[bank]],
                                start=(d == 0), stop=(d == DC - 1),
                                inc=(d == DC - 1), record=rec)
                self.attach(t, rec)
            usl = self.cf[:, self.o_us + i:self.o_us + i + 1]
            self.op("V", lambda e, o=uu, s=usl: e.tensor_scalar(
                o, self.psum[:, 1024:1024 + OWN], s, None, AL.mult),
                reads=[self.pb[2], self.pb[3], self.cf_bufs], writes=uub)
            reg, off = self._hdn_slot(i)
            hap = self.ap(reg, off, OWN)
            hb = self.bufs(reg, off, OWN)
            self.op("V", lambda e, o=hap, a=sg, b=uu: e.tensor_tensor(
                o, a, b, AL.mult),
                reads=sgb + uub, writes=hb)
        self._kv_step(len(self._kv_units))
        if "hdn" in self.d_dbg:
            for i in range(IC):
                reg, off = self._hdn_slot(i)
                self.dma("Q", self.d_dbg["hdn"][i * 128:(i + 1) * 128, :],
                         self.ap(reg, off, OWN), reads=self.bufs(reg, off, OWN))

    def _ph4_down(self):
        DC, IC, OWN = self.DC, self.IC, self.OWN
        for hd in range(DC):
            si = (hd % 2) * IC * 128
            wap = self.ap("AT", si, IC * 128)
            wb = self.bufs("AT", si, IC * 128)
            self.dma("G", wap, self.d_dwT[hd], writes=wb)
            for pc in range(self.NOP):
                bank = (hd % 2) * 2 + pc
                rec = []
                for i in range(IC):
                    reg, off = self._hdn_slot(i)
                    t = self.mm(self.pbank(bank, self.OPW),
                                wap[:, i * 128:(i + 1) * 128],
                                self.ap(reg, off + pc * self.OPW, self.OPW),
                                reads=wb + self.bufs(reg, off + pc * self.OPW, self.OPW),
                                writes=[self.pb[bank]],
                                start=(i == 0), stop=(i == IC - 1),
                                inc=(i == IC - 1), record=rec)
                self.attach(t, rec)
            dsl = self.cf[:, self.o_ds + hd:self.o_ds + hd + 1]
            pso = (hd % 2) * 2 * 512
            fap = self.ap("ST", hd * OWN, OWN)
            fb2 = self.bufs("ST", hd * OWN, OWN)
            self.op("S", lambda e, o=fap, s=dsl, po=pso: e.activation(
                o, self.psum[:, po:po + OWN], AF.Silu, scale=s),
                reads=[self.pb[(hd % 2) * 2], self.pb[(hd % 2) * 2 + 1],
                       self.cf_bufs], writes=fb2)




    def _finish(self):
        blk = self.ctx.enter_context(self.nc.Block())
        progs = self.prog

        @blk.tensor
        def _(e):
            for f in progs["T"]:
                f(e)

        @blk.vector
        def _(e):
            for f in progs["V"]:
                f(e)

        @blk.scalar
        def _(e):
            for f in progs["S"]:
                f(e)

        @blk.sync
        def _(e):
            for f in progs["Q"]:
                f(e)

        @blk.gpsimd
        def _(e):
            for f in progs["G"]:
                f(e)
        self.ctx.close()


_FULL = dict(D=2048, TOK=2048, H=16, DH=64, INTER=5632, HG=4)
_NC_CACHE = {}


def build_nc(cfg):
    key = tuple(sorted((k, tuple(sorted(v)) if isinstance(v, (list, set, tuple)) else v)
                       for k, v in cfg.items()))
    if key not in _NC_CACHE:
        _NC_CACHE[key] = MK(cfg).nc
    return _NC_CACHE[key]


def kernel(**inputs):
    x = np.ascontiguousarray(np.asarray(inputs["L0_attn_x"], dtype=np.float32))
    B, S, D = x.shape
    wq = np.ascontiguousarray(np.asarray(inputs["L0_attn_Wq"], dtype=np.float32))
    wk = np.ascontiguousarray(np.asarray(inputs["L0_attn_Wk"], dtype=np.float32))
    wv = np.ascontiguousarray(np.asarray(inputs["L0_attn_Wv"], dtype=np.float32))
    wo = np.ascontiguousarray(np.asarray(inputs["L0_attn_Wo"], dtype=np.float32))
    gwT = np.ascontiguousarray(np.asarray(inputs["L0_ffn_gate_w"], dtype=np.float32).T)
    uwT = np.ascontiguousarray(np.asarray(inputs["L0_ffn_up_w"], dtype=np.float32).T)
    dwT = np.ascontiguousarray(np.asarray(inputs["L0_ffn_down_w"], dtype=np.float32).T)
    VW = min(512, wq.shape[1])
    com = dict(
        wq=tile_cols(wq, 128), wk=tile_cols(wk, 128), wv=tile_cols(wv, VW),
        wo=wo, gwT=tile_cols(gwT, 128), uwT=tile_cols(uwT, 128),
        dwT=tile_cols(dwT, 128),
        gs=np.ascontiguousarray(np.asarray(inputs["L0_ffn_gate_s"], dtype=np.float32)),
        us=np.ascontiguousarray(np.asarray(inputs["L0_ffn_up_s"], dtype=np.float32)),
        ds=np.ascontiguousarray(np.asarray(inputs["L0_ffn_down_s"], dtype=np.float32)),
        gnw=np.ascontiguousarray(np.asarray(inputs["L0_attn_gn_weight"], dtype=np.float32)),
        gnb=np.ascontiguousarray(np.asarray(inputs["L0_attn_gn_bias"], dtype=np.float32)),
        lq1=np.ascontiguousarray(np.asarray(inputs["L0_attn_lambda_q1"], dtype=np.float32)),
        lk1=np.ascontiguousarray(np.asarray(inputs["L0_attn_lambda_k1"], dtype=np.float32)),
        lq2=np.ascontiguousarray(np.asarray(inputs["L0_attn_lambda_q2"], dtype=np.float32)),
        lk2=np.ascontiguousarray(np.asarray(inputs["L0_attn_lambda_k2"], dtype=np.float32)),
    )
    nc = build_nc(_FULL)
    OWN = S // 2
    in_maps = []
    for c in range(8):
        bb, ph = c // 2, c % 2
        if ph == 0:
            xs = x[bb]
        else:
            xs = np.concatenate([x[bb, OWN:], x[bb, :OWN]], axis=0)
        m = dict(com)
        m["xT"] = np.ascontiguousarray(xs.T)
        in_maps.append(m)
    res = run_bass_kernel_spmd(nc, in_maps, core_ids=list(range(8)))
    out = np.empty((B, S, D), dtype=np.float32)
    for c in range(8):
        bb, ph = c // 2, c % 2
        out[bb, ph * OWN:(ph + 1) * OWN] = \
            np.ascontiguousarray(res.results[c]["outT"]).T
    return out
